# version 03
# baseline (speedup 1.0000x reference)
import math
from contextlib import ExitStack

import numpy as np
import ml_dtypes

import concourse.bass as bass
import concourse.mybir as mybir
from concourse.bass_utils import run_bass_kernel_spmd

F32 = mybir.dt.float32
BF16 = mybir.dt.bfloat16
AF = mybir.ActivationFunctionType
ALU = mybir.AluOpType
AXX = mybir.AxisListType.X

D = 2048
DK = 16
NST = 256
HID = 5461
HCN = 43
TA = 8192
TO = 4096
TT = 512
NTA = 16
NTB = 8
BPT = 4
PI = math.pi
TWO_PI_C1 = 6.28125
TWO_PI_C2 = 2.0 * math.pi - 6.28125
MAGIC = 12582912.0

C_ONE, C_ZERO, C_HALFPI, C_EPS = 0, 1, 2, 3

FLATC = (D * HID) // 128
SCH = 2048
NSC = 43
SC_TAIL = FLATC - (NSC - 1) * SCH
W13_SUB = [(1408, 1408), (1408, 1408), (1408, 1408), (1280, 1237)]
NQ_W13 = DK * len(W13_SUB)
NQ_ALL = 2 * NQ_W13 + HCN
NLOAD_S = 3 * NSC
G_TWR_TOTAL = 2 * NQ_W13 + HCN


class Sem:
    def __init__(self, h, name):
        self.h = h
        self.name = name
        self.total = 0


class Eng:
    def __init__(self, e, name, log):
        self.e = e
        self.name = name
        self.log = log

    def wait(self, sem, val):
        if val <= 0:
            return
        self.e.wait_ge(sem.h, val)
        self.log.append((self.name, "wait", sem, val))

    def put(self, inst, sem=None, n=1):
        if sem is not None:
            inst.then_inc(sem.h, n)
            sem.total += n
            self.log.append((self.name, "inc", sem, n))
        return inst


def _simulate(log):
    streams = {}
    for ev in log:
        streams.setdefault(ev[0], []).append(ev)
    pos = {k: 0 for k in streams}
    counts = {}
    progressed = True
    while progressed:
        progressed = False
        for name, evs in streams.items():
            while pos[name] < len(evs):
                _, kind, sem, val = evs[pos[name]]
                if kind == "wait" and counts.get(id(sem), 0) < val:
                    break
                if kind == "inc":
                    counts[id(sem)] = counts.get(id(sem), 0) + val
                pos[name] += 1
                progressed = True
    stuck = {k: pos[k] for k in streams if pos[k] < len(streams[k])}
    if stuck:
        det = []
        for k in stuck:
            ev = streams[k][pos[k]]
            det.append(f"{k}@{pos[k]}/{len(streams[k])} waits {ev[2].name}>={ev[3]} "
                       f"(have {counts.get(id(ev[2]), 0)}, total {ev[2].total})")
        raise RuntimeError("static deadlock: " + "; ".join(det))


def _check_totals(log):
    bad = []
    for ev in log:
        if ev[1] == "wait" and ev[3] > ev[2].total:
            bad.append(f"{ev[0]} waits {ev[2].name}>={ev[3]} total={ev[2].total}")
    if bad:
        raise RuntimeError("impossible waits: " + "; ".join(bad[:10]))


def _prologue_steps(t):
    S = []

    def s(fn):
        S.append(("s", fn))

    def v(fn):
        S.append(("v", fn))

    s(lambda e: e.activation(out=t["pt1"][:], in_=t["nusb"][:], func=AF.Abs))
    s(lambda e: e.activation(out=t["pt2"][:], in_=t["pt1"][:], func=AF.Exp, scale=-1.0))
    s(lambda e: e.activation(out=t["pt2"][:], in_=t["pt2"][:], func=AF.Ln,
                             bias=t["cbc"][:, C_ONE:C_ONE + 1]))
    v(lambda e: e.tensor_scalar_max(t["pt3"][:], t["nusb"][:], 0.0))
    v(lambda e: e.tensor_add(t["pt4"][:], t["pt3"][:], t["pt2"][:]))
    s(lambda e: e.activation(out=t["pt1"][:], in_=t["pt4"][:], func=AF.Exp))
    s(lambda e: e.activation(out=t["pr_mag"][:], in_=t["pt1"][:], func=AF.Exp, scale=-1.0))
    v(lambda e: e.tensor_mul(t["pt3"][:], t["pr_mag"][:], t["pr_mag"][:]))
    v(lambda e: e.tensor_scalar(out=t["pt3"][:], in0=t["pt3"][:], scalar1=-1.0,
                                scalar2=1.0, op0=ALU.mult, op1=ALU.add))
    s(lambda e: e.activation(out=t["pt1"][:], in_=t["pt3"][:], func=AF.Ln))
    s(lambda e: e.activation(out=t["pr_gam"][:], in_=t["pt1"][:], func=AF.Exp, scale=0.5))
    s(lambda e: e.activation(out=t["pt1"][:], in_=t["thsb"][:], func=AF.Exp))
    v(lambda e: e.tensor_scalar_mul(t["pt2"][:], t["pt1"][:], 1.0 / (2 * PI)))
    v(lambda e: e.tensor_scalar(out=t["pt2"][:], in0=t["pt2"][:], scalar1=MAGIC,
                                scalar2=MAGIC, op0=ALU.add, op1=ALU.subtract))
    v(lambda e: e.scalar_tensor_tensor(out=t["pt4"][:], in0=t["pt2"][:],
                                       scalar=-TWO_PI_C1, in1=t["pt1"][:],
                                       op0=ALU.mult, op1=ALU.add))
    v(lambda e: e.scalar_tensor_tensor(out=t["pt3"][:], in0=t["pt2"][:],
                                       scalar=-TWO_PI_C2, in1=t["pt4"][:],
                                       op0=ALU.mult, op1=ALU.add))
    s(lambda e: e.activation(out=t["pr_sin"][:], in_=t["pt3"][:], func=AF.Sin))
    s(lambda e: e.activation(out=t["pt2"][:], in_=t["pt3"][:], func=AF.Abs))
    s(lambda e: e.activation(out=t["pr_cos"][:], in_=t["pt2"][:], func=AF.Sin,
                             scale=-1.0, bias=t["cbc"][:, C_HALFPI:C_HALFPI + 1]))
    return S


def build_nc():
    nc = bass.Bass("TRN2", target_bir_lowering=False, debug=False)

    xa = nc.dram_tensor("xa", [TA, D], F32, kind="ExternalInput").ap()
    breT = nc.dram_tensor("breT", [D, NST], F32, kind="ExternalInput").ap()
    bimT = nc.dram_tensor("bimT", [D, NST], F32, kind="ExternalInput").ap()
    creT = nc.dram_tensor("creT", [NST, D], F32, kind="ExternalInput").ap()
    cimT = nc.dram_tensor("cimT", [NST, D], F32, kind="ExternalInput").ap()
    nu_in = nc.dram_tensor("nu_in", [1, NST], F32, kind="ExternalInput").ap()
    th_in = nc.dram_tensor("th_in", [1, NST], F32, kind="ExternalInput").ap()
    dvec = nc.dram_tensor("dvec", [1, D], F32, kind="ExternalInput").ap()
    w1T = nc.dram_tensor("w1T", [D, HID], F32, kind="ExternalInput").ap()
    w3T = nc.dram_tensor("w3T", [D, HID], F32, kind="ExternalInput").ap()
    w2T = nc.dram_tensor("w2T", [HID, D], F32, kind="ExternalInput").ap()
    ident_in = nc.dram_tensor("ident_in", [128, 128], BF16, kind="ExternalInput").ap()
    consts = nc.dram_tensor("consts", [1, 8], F32, kind="ExternalInput").ap()
    out = nc.dram_tensor("out", [TO, D], F32, kind="ExternalOutput").ap()
    w1qs = nc.dram_tensor("w1qs", [HCN, 128, D], BF16, kind="Internal").ap()
    w3qs = nc.dram_tensor("w3qs", [HCN, 128, D], BF16, kind="Internal").ap()
    w2qs = nc.dram_tensor("w2qs", [HCN, 128, D], BF16, kind="Internal").ap()
    tstat = nc.dram_tensor("tstat", [8, 128], F32, kind="Internal").ap()

    w1flat = w1T.rearrange("a b -> (a b)").rearrange("(p c) -> p c", p=128)
    w3flat = w3T.rearrange("a b -> (a b)").rearrange("(p c) -> p c", p=128)
    w2flat = w2T.rearrange("a b -> (a b)").rearrange("(p c) -> p c", p=128)

    log = []
    with ExitStack() as ctx:
        sb = lambda n, s, d=F32: ctx.enter_context(nc.sbuf_tensor(n, s, d))
        ps_ = lambda n: ctx.enter_context(nc.psum_tensor(n, [128, 512], F32))
        mksem = lambda n: Sem(ctx.enter_context(nc.semaphore(n)), n)

        Hre = [sb(f"hre{i}", [128, TO], BF16) for i in range(2)]
        Him = [sb(f"him{i}", [128, TO], BF16) for i in range(2)]
        lnF = sb("lnF", [128, DK * TT], BF16)
        psq = sb("psq", [128, D])
        dbc = sb("dbc", [128, D])
        cT = [sb("creb", [128, 2 * D], BF16), sb("cimb", [128, 2 * D], BF16)]
        ident = sb("ident", [128, 128], BF16)
        cbc = sb("cbc", [128, 8])
        stat_rstd = sb("st_rstd", [128, 64])
        stat_nb = sb("st_nb", [128, 64])
        stat_ssum = sb("st_ssum", [128, 64])
        stat_sqs = sb("st_sqs", [128, 64])
        stat2_rstd = sb("s2_rstd", [128, 32])
        stat2_nb = sb("s2_nb", [128, 32])
        stat2_ssum = sb("s2_ssum", [128, 32])
        stat2_sqs = sb("s2_sqs", [128, 32])
        nusb = sb("nusb", [128, 2])
        thsb = sb("thsb", [128, 2])
        pr_mag = sb("pr_mag", [128, 2])
        pr_gam = sb("pr_gam", [128, 2])
        pr_cos = sb("pr_cos", [128, 2])
        pr_sin = sb("pr_sin", [128, 2])
        pt1 = sb("pt1", [128, 2])
        pt2 = sb("pt2", [128, 2])
        pt3 = sb("pt3", [128, 2])
        pt4 = sb("pt4", [128, 2])
        w2k_c = sb("w2k_c", [128, 2])
        w2k_s = sb("w2k_s", [128, 2])
        sc_s1 = sb("sc_s1", [128, 1])
        sc_s3 = sb("sc_s3", [128, 1])
        sc_s2 = sb("sc_s2", [128, 1])
        sc_thr = [sb(f"sc_thr{i}", [128, 1]) for i in range(3)]
        redarr = sb("redarr", [128, 3 * 48])
        redtot = sb("redtot", [128, 1])
        rrow = sb("rrow", [1, 128])
        rsum = sb("rsum", [1, 1])
        hreL = sb("hreL", [128, 2])
        himL = sb("himL", [128, 2])
        s1i = sb("s1i", [128, 2])
        s2i = sb("s2i", [128, 2])
        gt1 = sb("gt1", [128, 2])
        gt2 = sb("gt2", [128, 2])
        mean_t = sb("mean_t", [128, 4])
        meann_t = sb("meann_t", [128, 4])
        msq_t = sb("msq_t", [128, 4])
        var_t = sb("var_t", [128, 4])

        a_ctx = ExitStack()
        asb = lambda n, s, d=F32: a_ctx.enter_context(nc.sbuf_tensor(n, s, d))
        bT = [asb("breb", [128, DK * NST], BF16), asb("bimb", [128, DK * NST], BF16)]
        xar = [asb(f"xar{i}", [128, D]) for i in range(4)]
        ln1tok = asb("ln1tok", [128, BPT * D], BF16)
        scr = [asb(f"scr{i}", [128, TT]) for i in range(4)]
        s1t = asb("s1t", [128, TT])
        s2t = asb("s2t", [128, TT])
        cs_t = asb("cs_t", [128, 2 * TT])
        sn_t = asb("sn_t", [128, 2 * TT])
        csp_t = asb("csp_t", [128, 2 * TT])
        snp_t = asb("snp_t", [128, 2 * TT])
        tin = [asb(f"tin{i}", [128, SCH]) for i in range(2)]
        tq = asb("tq", [128, SCH], BF16)
        tl = asb("tl", [128, SCH], BF16)
        hdum = [asb(f"hd{i}", [128, TT], BF16) for i in range(4)]
        a_ctx.close()

        x1 = [sb(f"x1_{i}", [128, D]) for i in range(BPT)]
        gbuf = sb("gbuf", [128, HCN * TT], BF16)
        ln2tok = [sb(f"ln2tok{i}", [128, D], BF16) for i in range(2)]
        w1p = [sb(f"w1p{i}", [128, D], BF16) for i in range(2)]
        w3p = [sb(f"w3p{i}", [128, D], BF16) for i in range(2)]
        w2c = [sb(f"w2c{i}", [128, TT], BF16) for i in range(2)]
        xq = [sb(f"xq{i}", [128, TT]) for i in range(2)]
        lnbq = [sb(f"lnbq{i}", [128, TT]) for i in range(2)]
        silb = [sb(f"silb{i}", [128, TT]) for i in range(2)]
        ost = [sb(f"ost{i}", [128, TT]) for i in range(2)]

        pbu = [ps_(f"pbu{i}") for i in range(4)]
        ptr = [ps_(f"ptr{i}") for i in range(2)]
        py = [ps_(f"py{i}") for i in range(2)]

        sems = {}
        for n in ["A_xa", "A_sq", "A_var", "A_rstd", "A_nb", "A_ln1tok", "A_trps",
                  "A_trevac", "A_bu", "A_bufree", "A_h", "L_pro", "L_cast", "TV",
                  "TS", "G_tld", "V_red", "V_q", "G_twr", "G_st", "B_x", "B_lnb",
                  "B_base", "B_yps", "B_yfree", "B_sq2", "B_var2", "B_rstd2",
                  "B_nb2", "B_ln2tok", "B_p1", "B_p3", "B_p2", "B_h1d", "B_h3d",
                  "B_sil", "B_g", "B_outsb", "B_w2u", "G_out", "V_zw2",
                  "V_thr0", "V_thr1", "V_thr2"]:
            sems[n] = mksem(n)
        g = lambda n: sems[n]
        V_thr = [g("V_thr0"), g("V_thr1"), g("V_thr2")]

        ptiles = {"nusb": nusb, "thsb": thsb, "cbc": cbc, "pt1": pt1, "pt2": pt2,
                  "pt3": pt3, "pt4": pt4, "pr_mag": pr_mag, "pr_gam": pr_gam,
                  "pr_cos": pr_cos, "pr_sin": pr_sin}
        PSTEPS = _prologue_steps(ptiles)
        NPS_S = sum(1 for e, _ in PSTEPS if e == "s")
        NPS_V = sum(1 for e, _ in PSTEPS if e == "v")

        block = ctx.enter_context(nc.Block())

        @block.sync
        def _(sync):
            E = Eng(sync, "sync", log)
            E.put(sync.dma_start(out=cbc[:], in_=consts.broadcast_to((128, 8))),
                  g("L_pro"), 16)
            E.put(sync.dma_start(out=nusb[:],
                                 in_=nu_in.rearrange("o (h p) -> p (o h)", p=128),
                                 allow_slow_non_contiguous=True),
                  g("L_pro"), 16)
            E.put(sync.dma_start(out=thsb[:],
                                 in_=th_in.rearrange("o (h p) -> p (o h)", p=128),
                                 allow_slow_non_contiguous=True),
                  g("L_pro"), 16)
            E.put(sync.dma_start(out=ident[:], in_=ident_in), g("L_pro"), 16)
            E.put(sync.dma_start(out=dbc[:], in_=dvec.broadcast_to((128, D))),
                  g("L_pro"), 16)
            for gg in range(NTA * BPT):
                if gg >= 4:
                    E.wait(g("A_ln1tok"), gg - 3)
                E.put(sync.dma_start(out=xar[gg % 4][:],
                                     in_=xa[gg * 128:(gg + 1) * 128, :]),
                      g("A_xa"), 16)
            E.wait(g("G_twr"), 16 * G_TWR_TOTAL)
            E.wait(g("A_h"), NTA)
            for j in range(NTB):
                for hc in range(HCN):
                    gi = HCN * j + hc
                    if gi >= 2:
                        E.wait(g("B_h1d"), gi - 1)
                    E.put(sync.dma_start(out=w1p[hc % 2][:], in_=w1qs[hc, :, :]),
                          g("B_p1"), 16)
                    if gi >= 2:
                        E.wait(g("B_h3d"), gi - 1)
                    E.put(sync.dma_start(out=w3p[hc % 2][:], in_=w3qs[hc, :, :]),
                          g("B_p3"), 16)
                for dq in range(4):
                    for hc in range(HCN):
                        widx = (4 * j + dq) * HCN + hc
                        if widx >= 2:
                            E.wait(g("B_w2u"), widx - 1)
                        E.put(sync.dma_start(out=w2c[hc % 2][:],
                                             in_=w2qs[hc, :, dq * TT:(dq + 1) * TT]),
                              g("B_p2"), 16)

        @block.scalar
        def _(scalar):
            E = Eng(scalar, "scalar", log)
            E.wait(g("L_pro"), 16 * 3)
            nv = 0
            ns = 0
            for eng, fn in PSTEPS:
                if eng == "s":
                    E.wait(g("TV"), nv)
                    E.put(fn(scalar), g("TS"), 1)
                    ns += 1
                    E.wait(g("TS"), ns)
                else:
                    nv += 1

            for k in range(NTA):
                for b in range(BPT):
                    gg = BPT * k + b
                    E.wait(g("A_xa"), 16 * (gg + 1))
                    E.put(scalar.activation(out=psq[:], in_=xar[gg % 4][:],
                                            func=AF.Square,
                                            accum_out=stat_sqs[:, gg:gg + 1]),
                          g("A_sq") if b == 3 else None)
                E.wait(g("A_var"), k + 1)
                E.put(scalar.activation(out=var_t[:], in_=var_t[:], func=AF.Ln,
                                        bias=cbc[:, C_EPS:C_EPS + 1]), g("TS"))
                E.wait(g("TS"), NPS_S + k + 1)
                E.put(scalar.activation(out=stat_rstd[:, 4 * k:4 * k + 4],
                                        in_=var_t[:], func=AF.Exp, scale=-0.5),
                      g("A_rstd"))
                E.wait(g("A_nb"), k + 1)
                for b in range(BPT):
                    gg = BPT * k + b
                    E.put(scalar.activation(out=ln1tok[:, b * D:(b + 1) * D],
                                            in_=xar[gg % 4][:], func=AF.Identity,
                                            scale=stat_rstd[:, gg:gg + 1],
                                            bias=stat_nb[:, gg:gg + 1]),
                          g("A_ln1tok"))

            def xload(j, p):
                qg = 16 * j + p
                b, dq = p % 4, p // 4
                if qg == 0:
                    E.wait(g("A_h"), NTA)
                    E.wait(g("V_q"), 2 * NQ_ALL)
                if qg >= 2:
                    E.wait(g("B_base"), qg - 1)
                r0 = TO + j * TT + b * 128
                E.put(scalar.dma_start(out=xq[qg % 2][:],
                                       in_=xa[r0:r0 + 128, dq * TT:(dq + 1) * TT]),
                      g("B_x"), 16)

            def lnb(j, p):
                qg = 16 * j + p
                b = p % 4
                gg = 32 + 4 * j + b
                E.wait(g("B_x"), 16 * (qg + 1))
                if qg >= 2:
                    E.wait(g("B_base"), qg - 1)
                E.put(scalar.activation(out=lnbq[qg % 2][:], in_=xq[qg % 2][:],
                                        func=AF.Identity,
                                        scale=stat_rstd[:, gg:gg + 1],
                                        bias=stat_nb[:, gg:gg + 1]), g("B_lnb"))

            def sq2_chain(j):
                for b in range(BPT):
                    gg = 4 * j + b
                    E.wait(g("B_yfree"), 16 * j + 4 * b + 4)
                    E.put(scalar.activation(out=psq[:], in_=x1[b][:], func=AF.Square,
                                            accum_out=stat2_sqs[:, gg:gg + 1]),
                          g("B_sq2") if b == 3 else None)
                E.wait(g("B_var2"), j + 1)
                E.put(scalar.activation(out=var_t[:], in_=var_t[:], func=AF.Ln,
                                        bias=cbc[:, C_EPS:C_EPS + 1]), g("TS"))
                E.wait(g("TS"), NPS_S + NTA + j + 1)
                E.put(scalar.activation(out=stat2_rstd[:, 4 * j:4 * j + 4],
                                        in_=var_t[:], func=AF.Exp, scale=-0.5),
                      g("B_rstd2"))
                E.wait(g("B_nb2"), j + 1)
                if j == 0:
                    E.wait(g("A_h"), NTA)
                for b in range(BPT):
                    gg = 4 * j + b
                    prev_groups = 16 * j + 4 * b - 8
                    if prev_groups >= 0:
                        E.wait(g("A_trps"), 256 + prev_groups + 4)
                    E.put(scalar.activation(out=ln2tok[b % 2][:], in_=x1[b][:],
                                            func=AF.Identity,
                                            scale=stat2_rstd[:, gg:gg + 1],
                                            bias=stat2_nb[:, gg:gg + 1]),
                          g("B_ln2tok"))

            def silu(j, hc):
                gi = HCN * j + hc
                E.wait(g("B_h1d"), gi + 1)
                if gi >= 2:
                    E.wait(g("B_g"), gi - 1)
                bank = (hc % 2) * 2
                E.put(scalar.activation(out=silb[hc % 2][:], in_=pbu[bank][:],
                                        func=AF.Silu, scale=sc_s1[:, 0:1]),
                      g("B_sil"))

            for p in range(16):
                xload(0, p)
                lnb(0, p)
            sq2_chain(0)
            for j in range(NTB):
                for hc in range(HCN):
                    silu(j, hc)
                if j + 1 < NTB:
                    for p in range(16):
                        xload(j + 1, p)
                        lnb(j + 1, p)
                    sq2_chain(j + 1)

        @block.vector
        def _(vector):
            E = Eng(vector, "vector", log)
            tvn = [0]

            def vput(inst):
                E.put(inst, g("TV"), 1)
                tvn[0] += 1

            def vbar():
                E.wait(g("TV"), tvn[0])

            E.wait(g("L_pro"), 16 * 5)
            ns = 0
            for eng, fn in PSTEPS:
                if eng == "v":
                    E.wait(g("TS"), ns)
                    vput(fn(vector))
                    vbar()
                else:
                    ns += 1
            E.wait(g("TS"), NPS_S)

            for nh in range(2):
                c0 = nh * TT
                nhs = slice(nh, nh + 1)
                vput(vector.tensor_copy(cs_t[:, c0:c0 + 1], cbc[:, C_ONE:C_ONE + 1]))
                vput(vector.tensor_copy(sn_t[:, c0:c0 + 1], cbc[:, C_ZERO:C_ZERO + 1]))
                vput(vector.tensor_copy(w2k_c[:, nhs], pr_cos[:, nhs]))
                vput(vector.tensor_copy(w2k_s[:, nhs], pr_sin[:, nhs]))
                vbar()
                w = 1
                while w < TT:
                    cc = w2k_c[:, nhs]
                    ss = w2k_s[:, nhs]
                    vput(vector.tensor_scalar(out=scr[0][:, 0:w], in0=cs_t[:, c0:c0 + w],
                                              scalar1=cc, scalar2=None, op0=ALU.mult))
                    vput(vector.tensor_scalar(out=scr[1][:, 0:w], in0=sn_t[:, c0:c0 + w],
                                              scalar1=ss, scalar2=None, op0=ALU.mult))
                    vput(vector.tensor_scalar(out=scr[2][:, 0:w], in0=sn_t[:, c0:c0 + w],
                                              scalar1=cc, scalar2=None, op0=ALU.mult))
                    vput(vector.tensor_scalar(out=scr[3][:, 0:w], in0=cs_t[:, c0:c0 + w],
                                              scalar1=ss, scalar2=None, op0=ALU.mult))
                    vbar()
                    vput(vector.tensor_sub(cs_t[:, c0 + w:c0 + 2 * w],
                                           scr[0][:, 0:w], scr[1][:, 0:w]))
                    vput(vector.tensor_add(sn_t[:, c0 + w:c0 + 2 * w],
                                           scr[2][:, 0:w], scr[3][:, 0:w]))
                    vput(vector.tensor_mul(pt1[:, nhs], cc, cc))
                    vput(vector.tensor_mul(pt2[:, nhs], ss, ss))
                    vput(vector.tensor_mul(pt4[:, nhs], cc, ss))
                    vbar()
                    vput(vector.tensor_sub(w2k_c[:, nhs], pt1[:, nhs], pt2[:, nhs]))
                    vput(vector.tensor_scalar_mul(w2k_s[:, nhs], pt4[:, nhs], 2.0))
                    vbar()
                    w *= 2
                vput(vector.tensor_scalar(out=csp_t[:, c0:c0 + TT],
                                          in0=cs_t[:, c0:c0 + TT],
                                          scalar1=pr_gam[:, nhs], scalar2=None,
                                          op0=ALU.mult))
                vput(vector.tensor_scalar(out=snp_t[:, c0:c0 + TT],
                                          in0=sn_t[:, c0:c0 + TT],
                                          scalar1=pr_gam[:, nhs], scalar2=None,
                                          op0=ALU.mult))
                vbar()

            tern_ops = []
            tld_n = [0]
            qn_n = [0]
            for widx, flat in ((0, w1flat), (1, w3flat), (2, w2flat)):
                for ci in range(NSC):
                    cw = SCH if ci < NSC - 1 else SC_TAIL

                    def f(ci=ci, cw=cw, widx=widx):
                        i = tld_n[0]
                        E.wait(g("G_tld"), 16 * (i + 1))
                        E.put(vector.reduce_sum(
                            redarr[:, widx * 48 + ci:widx * 48 + ci + 1],
                            tin[i % 2][:, 0:cw], axis=AXX,
                            apply_absolute_value=True), g("V_red"))
                        tld_n[0] += 1
                    tern_ops.append(f)

                def fin(widx=widx):
                    E.wait(g("V_red"), NSC * (widx + 1))
                    E.put(vector.reduce_sum(redtot[:],
                                            redarr[:, widx * 48:widx * 48 + NSC],
                                            axis=AXX), V_thr[widx])
                    E.wait(g("G_st"), 16 * (4 * widx + 2))
                    E.put(vector.reduce_sum(rsum[:], rrow[:], axis=AXX), V_thr[widx])
                    E.wait(V_thr[widx], 2)
                    E.put(vector.tensor_scalar_mul(rsum[:], rsum[:],
                                                   1.0 / (D * HID)), V_thr[widx])
                tern_ops.append(fin)

                def thr(widx=widx):
                    st = [sc_s1, sc_s3, sc_s2][widx]
                    E.wait(g("G_st"), 16 * (4 * widx + 4))
                    E.put(vector.tensor_scalar_mul(sc_thr[widx][:], st[:], 0.5),
                          V_thr[widx])
                tern_ops.append(thr)

            def add_q(widx, pw, valid, zero_first):
                def f():
                    i = tld_n[0]
                    if zero_first:
                        E.put(vector.tensor_copy(
                            tin[i % 2][64:128, :],
                            cbc[64:128, C_ZERO:C_ZERO + 1].broadcast_to((64, SCH))),
                            g("V_zw2"))
                    E.wait(g("G_tld"), 16 * (i + 1))
                    E.wait(V_thr[widx], 4)
                    thr_ap = sc_thr[widx][:, 0:1]
                    E.put(vector.tensor_scalar(out=tl[:, 0:valid],
                                               in0=tin[i % 2][:, 0:valid],
                                               scalar1=-1.0, scalar2=thr_ap,
                                               op0=ALU.mult, op1=ALU.is_gt))
                    if pw > valid:
                        E.put(vector.scalar_tensor_tensor(
                            out=tq[:, 0:valid], in0=tin[i % 2][:, 0:valid],
                            scalar=thr_ap, in1=tl[:, 0:valid],
                            op0=ALU.is_gt, op1=ALU.subtract), g("V_q"))
                        E.put(vector.tensor_copy(
                            tq[:, valid:pw],
                            cbc[:, C_ZERO:C_ZERO + 1].broadcast_to((128, pw - valid))),
                            g("V_q"))
                    else:
                        E.put(vector.scalar_tensor_tensor(
                            out=tq[:, 0:valid], in0=tin[i % 2][:, 0:valid],
                            scalar=thr_ap, in1=tl[:, 0:valid],
                            op0=ALU.is_gt, op1=ALU.subtract), g("V_q"), 2)
                    tld_n[0] += 1
                    qn_n[0] += 1
                tern_ops.append(f)

            for widx in (0, 1):
                for dk in range(DK):
                    for (pw, valid) in W13_SUB:
                        add_q(widx, pw, valid, 0)
            for hc in range(HCN):
                add_q(2, SCH, SCH, hc == HCN - 1)

            n_tern = len(tern_ops)
            per_tile = (n_tern + 7) // 8
            ti = [0]

            for k in range(NTA):
                for b in range(BPT):
                    gg = BPT * k + b
                    E.wait(g("A_xa"), 16 * (gg + 1))
                    E.put(vector.reduce_sum(stat_ssum[:, gg:gg + 1], xar[gg % 4][:],
                                            axis=AXX))
                E.wait(g("A_sq"), k + 1)
                if k > 0:
                    E.wait(g("A_rstd"), k)
                c4 = slice(4 * k, 4 * k + 4)
                vput(vector.tensor_scalar_mul(mean_t[:], stat_ssum[:, c4], 1.0 / D))
                vput(vector.tensor_scalar_mul(meann_t[:], stat_ssum[:, c4], -1.0 / D))
                vbar()
                vput(vector.tensor_mul(msq_t[:], mean_t[:], mean_t[:]))
                vbar()
                E.put(vector.scalar_tensor_tensor(out=var_t[:], in0=stat_sqs[:, c4],
                                                  scalar=1.0 / D, in1=msq_t[:],
                                                  op0=ALU.mult, op1=ALU.subtract),
                      g("A_var"))
                E.wait(g("A_rstd"), k + 1)
                E.put(vector.tensor_mul(stat_nb[:, c4], meann_t[:], stat_rstd[:, c4]),
                      g("A_nb"))
                for dk in range(DK):
                    c = DK * k + dk
                    E.wait(g("A_trps"), c + 1)
                    E.put(vector.tensor_copy(lnF[:, dk * TT:(dk + 1) * TT],
                                             ptr[dk % 2][:]), g("A_trevac"))
                if k < 8:
                    for _ in range(per_tile):
                        if ti[0] < n_tern:
                            tern_ops[ti[0]]()
                            ti[0] += 1
                E.wait(g("A_bu"), k + 1)
                for nh in range(2):
                    c0 = nh * TT
                    nhs = slice(nh, nh + 1)
                    psre = pbu[2 * nh + 0]
                    psim = pbu[2 * nh + 1]
                    E.put(vector.tensor_mul(scr[0][:], csp_t[:, c0:c0 + TT], psre[:]))
                    E.put(vector.tensor_mul(scr[1][:], snp_t[:, c0:c0 + TT], psim[:]))
                    E.put(vector.tensor_mul(scr[2][:], snp_t[:, c0:c0 + TT], psre[:]))
                    E.put(vector.tensor_mul(scr[3][:], csp_t[:, c0:c0 + TT], psim[:]),
                          g("A_bufree") if nh == 1 else None)
                    E.put(vector.tensor_add(scr[0][:], scr[0][:], scr[1][:]))
                    E.put(vector.tensor_sub(scr[2][:], scr[2][:], scr[3][:]))
                    if k == 0:
                        init1 = 0.0
                        init2 = 0.0
                    else:
                        vput(vector.tensor_scalar(out=gt1[:, nhs], in0=hreL[:, nhs],
                                                  scalar1=pr_cos[:, nhs], scalar2=None,
                                                  op0=ALU.mult))
                        vput(vector.tensor_scalar(out=gt2[:, nhs], in0=himL[:, nhs],
                                                  scalar1=pr_sin[:, nhs], scalar2=None,
                                                  op0=ALU.mult))
                        vbar()
                        vput(vector.tensor_add(s1i[:, nhs], gt1[:, nhs], gt2[:, nhs]))
                        vput(vector.tensor_scalar(out=gt1[:, nhs], in0=himL[:, nhs],
                                                  scalar1=pr_cos[:, nhs], scalar2=None,
                                                  op0=ALU.mult))
                        vput(vector.tensor_scalar(out=gt2[:, nhs], in0=hreL[:, nhs],
                                                  scalar1=pr_sin[:, nhs], scalar2=None,
                                                  op0=ALU.mult))
                        vbar()
                        vput(vector.tensor_sub(s2i[:, nhs], gt1[:, nhs], gt2[:, nhs]))
                        vbar()
                        init1 = s1i[:, nhs]
                        init2 = s2i[:, nhs]
                    rbc = pr_mag[:, nhs].broadcast_to((128, TT))
                    E.put(vector.tensor_tensor_scan(out=s1t[:], data0=rbc,
                                                    data1=scr[0][:], initial=init1,
                                                    op0=ALU.mult, op1=ALU.add))
                    E.put(vector.tensor_tensor_scan(out=s2t[:], data0=rbc,
                                                    data1=scr[2][:], initial=init2,
                                                    op0=ALU.mult, op1=ALU.add))
                    if k < 8:
                        dre_s = hdum[2 * nh][:, 0:TT]
                        dim_s = hdum[2 * nh + 1][:, 0:TT]
                    else:
                        t0 = (k - 8) * TT
                        dre_s = Hre[nh][:, t0:t0 + TT]
                        dim_s = Him[nh][:, t0:t0 + TT]
                    E.put(vector.tensor_mul(scr[0][:], cs_t[:, c0:c0 + TT], s1t[:]))
                    E.put(vector.tensor_mul(scr[1][:], sn_t[:, c0:c0 + TT], s2t[:]))
                    E.put(vector.tensor_add(dre_s, scr[0][:], scr[1][:]))
                    vput(vector.tensor_add(hreL[:, nhs], scr[0][:, TT - 1:TT],
                                           scr[1][:, TT - 1:TT]))
                    E.put(vector.tensor_mul(scr[2][:], cs_t[:, c0:c0 + TT], s2t[:]))
                    E.put(vector.tensor_mul(scr[3][:], sn_t[:, c0:c0 + TT], s1t[:]))
                    E.put(vector.tensor_sub(dim_s, scr[2][:], scr[3][:]),
                          g("A_h") if nh == 1 else None)
                    vput(vector.tensor_sub(himL[:, nhs], scr[2][:, TT - 1:TT],
                                           scr[3][:, TT - 1:TT]))
                    vbar()
            while ti[0] < n_tern:
                tern_ops[ti[0]]()
                ti[0] += 1

            def base_q(j, p):
                qg = 16 * j + p
                b, dq = p % 4, p // 4
                E.wait(g("B_lnb"), qg + 1)
                sl = slice(dq * TT, (dq + 1) * TT)
                E.put(vector.tensor_mul(x1[b][:, sl], lnbq[qg % 2][:], dbc[:, sl]))
                E.put(vector.tensor_add(x1[b][:, sl], x1[b][:, sl], xq[qg % 2][:]),
                      g("B_base"))

            def yevac(j, qi):
                yc = 16 * j + qi
                b, dq = qi // 4, qi % 4
                E.wait(g("B_yps"), yc + 1)
                sl = slice(dq * TT, (dq + 1) * TT)
                E.put(vector.tensor_add(x1[b][:, sl], x1[b][:, sl], py[qi % 2][:]),
                      g("B_yfree"))
                if dq == 3:
                    gg = 4 * j + b
                    E.put(vector.reduce_sum(stat2_ssum[:, gg:gg + 1], x1[b][:],
                                            axis=AXX))

            def stats2(j):
                E.wait(g("B_sq2"), j + 1)
                if j > 0:
                    E.wait(g("B_rstd2"), j)
                else:
                    E.wait(g("A_rstd"), NTA)
                c4 = slice(4 * j, 4 * j + 4)
                vput(vector.tensor_scalar_mul(mean_t[:], stat2_ssum[:, c4], 1.0 / D))
                vput(vector.tensor_scalar_mul(meann_t[:], stat2_ssum[:, c4], -1.0 / D))
                vbar()
                vput(vector.tensor_mul(msq_t[:], mean_t[:], mean_t[:]))
                vbar()
                E.put(vector.scalar_tensor_tensor(out=var_t[:], in0=stat2_sqs[:, c4],
                                                  scalar=1.0 / D, in1=msq_t[:],
                                                  op0=ALU.mult, op1=ALU.subtract),
                      g("B_var2"))
                E.wait(g("B_rstd2"), j + 1)
                E.put(vector.tensor_mul(stat2_nb[:, c4], meann_t[:],
                                        stat2_rstd[:, c4]), g("B_nb2"))

            def tr2evac(j):
                for b in range(BPT):
                    for dkg in range(4):
                        tc = 256 + 16 * j + 4 * b + dkg
                        E.wait(g("A_trps"), tc + 1)
                        outap = lnF[:].rearrange("p (dk c) -> p dk c", dk=DK)[
                            :, 4 * dkg:4 * dkg + 4, b * 128:(b + 1) * 128]
                        srcap = ptr[(4 * b + dkg) % 2][:].rearrange(
                            "p (i c) -> p i c", i=4)
                        E.put(vector.tensor_copy(outap, srcap), g("A_trevac"))

            def gop(j, hc):
                gi = HCN * j + hc
                E.wait(g("B_h3d"), gi + 1)
                E.wait(g("B_sil"), gi + 1)
                if j > 0 and hc == 0:
                    E.wait(g("B_w2u"), 172 * j)
                bank = (hc % 2) * 2 + 1
                E.put(vector.scalar_tensor_tensor(out=gbuf[:, hc * TT:(hc + 1) * TT],
                                                  in0=pbu[bank][:],
                                                  scalar=sc_s3[:, 0:1],
                                                  in1=silb[hc % 2][:], op0=ALU.mult,
                                                  op1=ALU.mult), g("B_g"))

            def outevac(j, dq):
                E.wait(g("B_w2u"), 172 * j + 43 * (dq + 1))
                for b in range(BPT):
                    oi = 16 * j + 4 * dq + b
                    if oi >= 2:
                        E.wait(g("G_out"), 16 * (oi - 1))
                    sl = slice(dq * TT, (dq + 1) * TT)
                    E.put(vector.scalar_tensor_tensor(out=ost[oi % 2][:],
                                                      in0=pbu[b][:],
                                                      scalar=sc_s2[:, 0:1],
                                                      in1=x1[b][:, sl],
                                                      op0=ALU.mult, op1=ALU.add),
                          g("B_outsb"))

            for p in range(16):
                base_q(0, p)
            for qi in range(16):
                yevac(0, qi)
            stats2(0)
            tr2evac(0)
            for j in range(NTB):
                for hc in range(HCN):
                    gop(j, hc)
                for dq in range(4):
                    outevac(j, dq)
                    if j + 1 < NTB:
                        for b in range(BPT):
                            base_q(j + 1, 4 * dq + b)
                if j + 1 < NTB:
                    for qi in range(16):
                        yevac(j + 1, qi)
                    stats2(j + 1)
                    tr2evac(j + 1)

        @block.tensor
        def _(tensor):
            E = Eng(tensor, "tensor", log)
            E.wait(g("L_pro"), 16 * 4)
            E.wait(g("L_cast"), 16 * 4)

            for k in range(NTA):
                E.wait(g("A_ln1tok"), 4 * k + 4)
                for dk in range(DK):
                    c = DK * k + dk
                    E.wait(g("A_trevac"), c - 1)
                    for b in range(BPT):
                        E.put(tensor.matmul(
                            ptr[dk % 2][:, b * 128:(b + 1) * 128],
                            ln1tok[:, b * D + dk * 128:b * D + dk * 128 + 128],
                            ident[:], start=(b == 0), stop=(b == 3),
                            skip_group_check=True),
                            g("A_trps") if b == 3 else None)
                if k > 0:
                    E.wait(g("A_bufree"), k)
                for dk in range(DK):
                    c = DK * k + dk
                    E.wait(g("A_trevac"), c + 1)
                    for nh in range(2):
                        for cm in range(2):
                            last = (dk == DK - 1 and nh == 1 and cm == 1)
                            E.put(tensor.matmul(
                                pbu[2 * nh + cm][:],
                                bT[cm][:, dk * NST + nh * 128:dk * NST + nh * 128 + 128],
                                lnF[:, dk * TT:(dk + 1) * TT],
                                start=(dk == 0), stop=(dk == DK - 1),
                                skip_group_check=True),
                                g("A_bu") if last else None)

            def Y(j):
                for qi in range(16):
                    b, dq = qi // 4, qi % 4
                    yc = 16 * j + qi
                    if qi == 0:
                        E.wait(g("A_h"), min(9 + j, NTA))
                    E.wait(g("B_yfree"), yc - 1)
                    t0 = j * TT + b * 128
                    srcs = [(Hre[0], cT[0], 0), (Him[0], cT[1], 0),
                            (Hre[1], cT[0], 1), (Him[1], cT[1], 1)]
                    for i, (hsrc, csrc, nh) in enumerate(srcs):
                        E.put(tensor.matmul(
                            py[qi % 2][:], hsrc[:, t0:t0 + 128],
                            csrc[:, nh * D + dq * TT:nh * D + (dq + 1) * TT],
                            start=(i == 0), stop=(i == 3),
                            skip_group_check=True),
                            g("B_yps") if i == 3 else None)

            def TR2(j):
                for b in range(BPT):
                    E.wait(g("B_ln2tok"), 4 * j + b + 1)
                    for dkg in range(4):
                        tc = 256 + 16 * j + 4 * b + dkg
                        E.wait(g("A_trevac"), tc - 1)
                        for i in range(4):
                            dk = 4 * dkg + i
                            E.put(tensor.matmul(
                                ptr[(4 * b + dkg) % 2][:, i * 128:(i + 1) * 128],
                                ln2tok[b % 2][:, dk * 128:(dk + 1) * 128],
                                ident[:], start=(i == 0), stop=(i == 3),
                                skip_group_check=True),
                                g("A_trps") if i == 3 else None)

            def H(j):
                if j == 0:
                    E.wait(g("A_bufree"), NTA)
                for hc in range(HCN):
                    gi = HCN * j + hc
                    if hc == 0:
                        E.wait(g("A_trevac"), 256 + 16 * j + 16)
                        E.wait(g("B_outsb"), 16 * j)
                    E.wait(g("B_p1"), 16 * (gi + 1))
                    if gi >= 2:
                        E.wait(g("B_sil"), gi - 1)
                    b1 = (hc % 2) * 2
                    for dk in range(DK):
                        E.put(tensor.matmul(
                            pbu[b1][:], w1p[hc % 2][:, dk * 128:(dk + 1) * 128],
                            lnF[:, dk * TT:(dk + 1) * TT],
                            start=(dk == 0), stop=(dk == DK - 1),
                            skip_group_check=True),
                            g("B_h1d") if dk == DK - 1 else None)
                    E.wait(g("B_p3"), 16 * (gi + 1))
                    if gi >= 2:
                        E.wait(g("B_g"), gi - 1)
                    for dk in range(DK):
                        E.put(tensor.matmul(
                            pbu[b1 + 1][:], w3p[hc % 2][:, dk * 128:(dk + 1) * 128],
                            lnF[:, dk * TT:(dk + 1) * TT],
                            start=(dk == 0), stop=(dk == DK - 1),
                            skip_group_check=True),
                            g("B_h3d") if dk == DK - 1 else None)

            def OUT(j):
                for dq in range(4):
                    for hc in range(HCN):
                        widx = (4 * j + dq) * HCN + hc
                        E.wait(g("B_p2"), 16 * (widx + 1))
                        E.wait(g("B_g"), HCN * j + hc + 1)
                        if hc == 0:
                            if dq == 0:
                                E.wait(g("B_g"), HCN * (j + 1))
                            else:
                                E.wait(g("B_outsb"), 16 * j + 4 * dq)
                        for b in range(BPT):
                            E.put(tensor.matmul(
                                pbu[b][:],
                                gbuf[:, hc * TT + b * 128:hc * TT + (b + 1) * 128],
                                w2c[hc % 2][:],
                                start=(hc == 0), stop=(hc == HCN - 1),
                                skip_group_check=True),
                                g("B_w2u") if b == BPT - 1 else None)

            Y(0)
            TR2(0)
            for j in range(NTB):
                H(j)
                OUT(j)
                if j + 1 < NTB:
                    Y(j + 1)
                    TR2(j + 1)

        @block.gpsimd
        def _(gpsimd):
            E = Eng(gpsimd, "gpsimd", log)
            E.put(gpsimd.dma_start(out=bT[0][:].rearrange("p (dk n) -> p dk n", dk=DK),
                                   in_=breT.rearrange("(dk p) n -> p dk n", p=128)),
                  g("L_cast"), 16)
            E.put(gpsimd.dma_start(out=bT[1][:].rearrange("p (dk n) -> p dk n", dk=DK),
                                   in_=bimT.rearrange("(dk p) n -> p dk n", p=128)),
                  g("L_cast"), 16)
            E.put(gpsimd.dma_start(out=cT[0][:].rearrange("p (h d) -> p h d", h=2),
                                   in_=creT.rearrange("(h p) d -> p h d", p=128)),
                  g("L_cast"), 16)
            E.put(gpsimd.dma_start(out=cT[1][:].rearrange("p (h d) -> p h d", h=2),
                                   in_=cimT.rearrange("(h p) d -> p h d", p=128)),
                  g("L_cast"), 16)

            tld = [0]
            qn = [0]
            for widx, flat in ((0, w1flat), (1, w3flat), (2, w2flat)):
                for ci in range(NSC):
                    cw = SCH if ci < NSC - 1 else SC_TAIL
                    i = tld[0]
                    if i >= 2:
                        E.wait(g("V_red"), i - 1)
                    E.put(gpsimd.dma_start(out=tin[i % 2][:, 0:cw],
                                           in_=flat[:, ci * SCH:ci * SCH + cw]),
                          g("G_tld"), 16)
                    tld[0] += 1
                E.wait(V_thr[widx], 1)
                E.put(gpsimd.dma_start(
                    out=tstat[widx:widx + 1, :].rearrange("a p -> p a"),
                    in_=redtot[:]), g("G_st"), 16)
                E.wait(g("G_st"), 16 * (4 * widx + 1))
                E.put(gpsimd.dma_start(out=rrow[:], in_=tstat[widx:widx + 1, :]),
                      g("G_st"), 16)
                E.wait(V_thr[widx], 3)
                E.put(gpsimd.dma_start(out=tstat[4 + widx:5 + widx, 0:1], in_=rsum[:]),
                      g("G_st"), 16)
                E.wait(g("G_st"), 16 * (4 * widx + 3))
                st = [sc_s1, sc_s3, sc_s2][widx]
                E.put(gpsimd.dma_start(
                    out=st[:],
                    in_=tstat[4 + widx:5 + widx, 0:1].broadcast_to((128, 1))),
                    g("G_st"), 16)

            def qload(load_ap):
                i = tld[0]
                q = qn[0]
                if q >= 2:
                    E.wait(g("V_q"), 2 * (q - 1))
                elif i >= 2:
                    E.wait(g("V_red"), NLOAD_S)
                E.put(gpsimd.dma_start(out=load_ap[0], in_=load_ap[1]), g("G_tld"), 16)
                tld[0] += 1
                qn[0] += 1
                E.wait(g("V_q"), 2 * qn[0])

            for widx, (wsrc, wdst) in enumerate(((w1T, w1qs), (w3T, w3qs))):
                for dk in range(DK):
                    h0 = 0
                    for (pw, valid) in W13_SUB:
                        qload((tin[tld[0] % 2][:, 0:valid],
                               wsrc[dk * 128:(dk + 1) * 128, h0:h0 + valid]))
                        npan = pw // 128
                        hc0 = h0 // 128
                        store = wdst[hc0:hc0 + npan, :, dk * 128:(dk + 1) * 128]
                        store = store.rearrange("hc p jj -> p hc jj")
                        srcv = tq[:, 0:pw].rearrange("p (hc jj) -> p hc jj", hc=npan)
                        E.put(gpsimd.dma_start(out=store, in_=srcv), g("G_twr"), 16)
                        h0 += pw
            for hc in range(HCN):
                rows = 128 if hc < HCN - 1 else HID - 128 * (HCN - 1)
                if rows < 128:
                    E.wait(g("V_zw2"), 1)
                qload((tin[tld[0] % 2][0:rows, :],
                       w2T[hc * 128:hc * 128 + rows, :]))
                E.put(gpsimd.dma_start(out=w2qs[hc, :, :], in_=tq[:]), g("G_twr"), 16)

            for j in range(NTB):
                for dq in range(4):
                    for b in range(BPT):
                        oi = 16 * j + 4 * dq + b
                        E.wait(g("B_outsb"), oi + 1)
                        r0 = j * TT + b * 128
                        E.put(gpsimd.dma_start(
                            out=out[r0:r0 + 128, dq * TT:(dq + 1) * TT],
                            in_=ost[oi % 2][:]), g("G_out"), 16)
            E.wait(g("G_out"), 16 * 128)

    _check_totals(log)
    _simulate(log)
    return nc


_NC_CACHE = {}


def _get_nc():
    if "nc" not in _NC_CACHE:
        _NC_CACHE["nc"] = build_nc()
    return _NC_CACHE["nc"]


def kernel(L0_attn_x, L0_attn_nu_log, L0_attn_theta_log, L0_attn_B_re, L0_attn_B_im,
           L0_attn_C_re, L0_attn_C_im, L0_attn_D, L0_ffn_w1, L0_ffn_w3, L0_ffn_w2):
    x = np.ascontiguousarray(np.asarray(L0_attn_x, dtype=np.float32))
    assert x.shape == (4, 8192, 2048)

    shared = {
        "breT": np.ascontiguousarray(np.asarray(L0_attn_B_re, np.float32).T),
        "bimT": np.ascontiguousarray(np.asarray(L0_attn_B_im, np.float32).T),
        "creT": np.ascontiguousarray(np.asarray(L0_attn_C_re, np.float32).T),
        "cimT": np.ascontiguousarray(np.asarray(L0_attn_C_im, np.float32).T),
        "nu_in": np.ascontiguousarray(
            np.asarray(L0_attn_nu_log, np.float32).reshape(1, NST)),
        "th_in": np.ascontiguousarray(
            np.asarray(L0_attn_theta_log, np.float32).reshape(1, NST)),
        "dvec": np.ascontiguousarray(np.asarray(L0_attn_D, np.float32).reshape(1, D)),
        "w1T": np.ascontiguousarray(np.asarray(L0_ffn_w1, np.float32).T),
        "w3T": np.ascontiguousarray(np.asarray(L0_ffn_w3, np.float32).T),
        "w2T": np.ascontiguousarray(np.asarray(L0_ffn_w2, np.float32).T),
        "ident_in": np.eye(128, dtype=ml_dtypes.bfloat16),
        "consts": np.array([[1.0, 0.0, math.pi / 2, 1e-5, 0, 0, 0, 0]], np.float32),
    }
    zeros = np.zeros((TO, D), np.float32)
    in_maps = []
    for core in range(8):
        b, h = core // 2, core % 2
        xa_c = np.concatenate([zeros, x[b, :TO]], axis=0) if h == 0 else x[b]
        m = dict(shared)
        m["xa"] = np.ascontiguousarray(xa_c)
        in_maps.append(m)

    nc = _get_nc()
    res = run_bass_kernel_spmd(nc, in_maps, core_ids=list(range(8)))
    outp = np.empty((4, 8192, 2048), np.float32)
    for core in range(8):
        b, h = core // 2, core % 2
        outp[b, h * TO:(h + 1) * TO] = res.results[core]["out"]
    return outp


if __name__ == "__main__":
    nc = build_nc()
    print("build OK")


# version 06
# speedup vs baseline: 1.1798x; 1.1798x over previous
import math
from contextlib import ExitStack

import numpy as np
import ml_dtypes

import concourse.bass as bass
import concourse.mybir as mybir
from concourse.bass_utils import run_bass_kernel_spmd

F32 = mybir.dt.float32
BF16 = mybir.dt.bfloat16
AF = mybir.ActivationFunctionType
ALU = mybir.AluOpType
AXX = mybir.AxisListType.X

D = 2048
DK = 16
NST = 256
HID = 5461
HCN = 43
TA = 8192
TO = 4096
TT = 512
NTA = 16
NTB = 8
BPT = 4
PI = math.pi
TWO_PI_C1 = 6.28125
TWO_PI_C2 = 2.0 * math.pi - 6.28125
MAGIC = 12582912.0

C_ONE, C_ZERO, C_HALFPI, C_EPS = 0, 1, 2, 3
DEBUG = False

FLATC = (D * HID) // 128
SCH = 2048
NSC = 43
SC_TAIL = FLATC - (NSC - 1) * SCH
SSCH = 768
NSSC = (FLATC + SSCH - 1) // SSCH
SS_TAIL = FLATC - (NSSC - 1) * SSCH
W13_SUB9 = [(640, 640)] * 8 + [(384, 341)]
W2_SUB = [(768, 768), (640, 640), (640, 640)]
W13_SUB = [(1408, 1408), (1408, 1408), (1408, 1408), (1280, 1237)]
NQ_W13 = DK * len(W13_SUB)
NQ_ALL = 2 * NQ_W13 + HCN
NLOAD_S = 3 * NSC
G_TWR_TOTAL = 2 * NQ_W13 + HCN


class Sem:
    def __init__(self, h, name):
        self.h = h
        self.name = name
        self.total = 0


class Eng:
    def __init__(self, e, name, log):
        self.e = e
        self.name = name
        self.log = log

    def wait(self, sem, val):
        if val <= 0:
            return
        self.e.wait_ge(sem.h, val)
        self.log.append((self.name, "wait", sem, val))

    def put(self, inst, sem=None, n=1):
        if sem is not None:
            inst.then_inc(sem.h, n)
            sem.total += n
            self.log.append((self.name, "inc", sem, n))
        return inst


def _simulate(log):
    streams = {}
    for ev in log:
        streams.setdefault(ev[0], []).append(ev)
    pos = {k: 0 for k in streams}
    counts = {}
    progressed = True
    while progressed:
        progressed = False
        for name, evs in streams.items():
            while pos[name] < len(evs):
                _, kind, sem, val = evs[pos[name]]
                if kind == "wait" and counts.get(id(sem), 0) < val:
                    break
                if kind == "inc":
                    counts[id(sem)] = counts.get(id(sem), 0) + val
                pos[name] += 1
                progressed = True
    stuck = {k: pos[k] for k in streams if pos[k] < len(streams[k])}
    if stuck:
        det = []
        for k in stuck:
            ev = streams[k][pos[k]]
            det.append(f"{k}@{pos[k]}/{len(streams[k])} waits {ev[2].name}>={ev[3]} "
                       f"(have {counts.get(id(ev[2]), 0)}, total {ev[2].total})")
        raise RuntimeError("static deadlock: " + "; ".join(det))


def _check_totals(log):
    bad = []
    for ev in log:
        if ev[1] == "wait" and ev[3] > ev[2].total:
            bad.append(f"{ev[0]} waits {ev[2].name}>={ev[3]} total={ev[2].total}")
    if bad:
        raise RuntimeError("impossible waits: " + "; ".join(bad[:10]))


def _prologue_steps(t):
    S = []

    def s(fn):
        S.append(("s", fn))

    def v(fn):
        S.append(("v", fn))

    s(lambda e: e.activation(out=t["pt1"][:], in_=t["nusb"][:], func=AF.Abs))
    s(lambda e: e.activation(out=t["pt2"][:], in_=t["pt1"][:], func=AF.Exp, scale=-1.0))
    s(lambda e: e.activation(out=t["pt2"][:], in_=t["pt2"][:], func=AF.Ln,
                             bias=t["cbc"][:, C_ONE:C_ONE + 1]))
    v(lambda e: e.tensor_scalar_max(t["pt3"][:], t["nusb"][:], 0.0))
    v(lambda e: e.tensor_add(t["pt4"][:], t["pt3"][:], t["pt2"][:]))
    s(lambda e: e.activation(out=t["pt1"][:], in_=t["pt4"][:], func=AF.Exp))
    s(lambda e: e.activation(out=t["pr_mag"][:], in_=t["pt1"][:], func=AF.Exp, scale=-1.0))
    v(lambda e: e.tensor_mul(t["pt3"][:], t["pr_mag"][:], t["pr_mag"][:]))
    v(lambda e: e.tensor_scalar(out=t["pt3"][:], in0=t["pt3"][:], scalar1=-1.0,
                                scalar2=1.0, op0=ALU.mult, op1=ALU.add))
    s(lambda e: e.activation(out=t["pt1"][:], in_=t["pt3"][:], func=AF.Ln))
    s(lambda e: e.activation(out=t["pr_gam"][:], in_=t["pt1"][:], func=AF.Exp, scale=0.5))
    s(lambda e: e.activation(out=t["pt1"][:], in_=t["thsb"][:], func=AF.Exp))
    v(lambda e: e.tensor_scalar_mul(t["pt2"][:], t["pt1"][:], 1.0 / (2 * PI)))
    v(lambda e: e.tensor_scalar(out=t["pt2"][:], in0=t["pt2"][:], scalar1=MAGIC,
                                scalar2=MAGIC, op0=ALU.add, op1=ALU.subtract))
    v(lambda e: e.scalar_tensor_tensor(out=t["pt4"][:], in0=t["pt2"][:],
                                       scalar=-TWO_PI_C1, in1=t["pt1"][:],
                                       op0=ALU.mult, op1=ALU.add))
    v(lambda e: e.scalar_tensor_tensor(out=t["pt3"][:], in0=t["pt2"][:],
                                       scalar=-TWO_PI_C2, in1=t["pt4"][:],
                                       op0=ALU.mult, op1=ALU.add))
    s(lambda e: e.activation(out=t["pr_sin"][:], in_=t["pt3"][:], func=AF.Sin))
    s(lambda e: e.activation(out=t["pt2"][:], in_=t["pt3"][:], func=AF.Abs))
    s(lambda e: e.activation(out=t["pr_cos"][:], in_=t["pt2"][:], func=AF.Sin,
                             scale=-1.0, bias=t["cbc"][:, C_HALFPI:C_HALFPI + 1]))
    return S


def build_nc():
    nc = bass.Bass("TRN2", target_bir_lowering=False, debug=False)

    xa = nc.dram_tensor("xa", [TA, D], F32, kind="ExternalInput").ap()
    breT = nc.dram_tensor("breT", [D, NST], F32, kind="ExternalInput").ap()
    bimT = nc.dram_tensor("bimT", [D, NST], F32, kind="ExternalInput").ap()
    creT = nc.dram_tensor("creT", [NST, D], F32, kind="ExternalInput").ap()
    cimT = nc.dram_tensor("cimT", [NST, D], F32, kind="ExternalInput").ap()
    nu_in = nc.dram_tensor("nu_in", [1, NST], F32, kind="ExternalInput").ap()
    th_in = nc.dram_tensor("th_in", [1, NST], F32, kind="ExternalInput").ap()
    dvec = nc.dram_tensor("dvec", [1, D], F32, kind="ExternalInput").ap()
    w1T = nc.dram_tensor("w1T", [D, HID], F32, kind="ExternalInput").ap()
    w3T = nc.dram_tensor("w3T", [D, HID], F32, kind="ExternalInput").ap()
    w2T = nc.dram_tensor("w2T", [HID, D], F32, kind="ExternalInput").ap()
    ident_in = nc.dram_tensor("ident_in", [128, 128], BF16, kind="ExternalInput").ap()
    consts = nc.dram_tensor("consts", [1, 8], F32, kind="ExternalInput").ap()
    out = nc.dram_tensor("out", [TO, D], F32, kind="ExternalOutput").ap()
    w1qs = nc.dram_tensor("w1qs", [HCN, 128, D], BF16, kind="Internal").ap()
    w3qs = nc.dram_tensor("w3qs", [HCN, 128, D], BF16, kind="Internal").ap()
    w2qs = nc.dram_tensor("w2qs", [HCN, 128, D], BF16, kind="Internal").ap()
    if DEBUG:
        dbg_small = nc.dram_tensor("dbg_small", [128, 32], F32, kind="ExternalOutput").ap()
        dbg_stat = nc.dram_tensor("dbg_stat", [128, 128], F32, kind="ExternalOutput").ap()
        dbg_h = nc.dram_tensor("dbg_h", [128, TO], F32, kind="ExternalOutput").ap()
    tstat = nc.dram_tensor("tstat", [8, 128], F32, kind="Internal").ap()

    w1flat = w1T.rearrange("a b -> (a b)").rearrange("(p c) -> p c", p=128)
    w3flat = w3T.rearrange("a b -> (a b)").rearrange("(p c) -> p c", p=128)
    w2flat = w2T.rearrange("a b -> (a b)").rearrange("(p c) -> p c", p=128)

    log = []
    with ExitStack() as ctx:
        sb = lambda n, s, d=F32: ctx.enter_context(nc.sbuf_tensor(n, s, d))
        ps_ = lambda n: ctx.enter_context(nc.psum_tensor(n, [128, 512], F32))
        mksem = lambda n: Sem(ctx.enter_context(nc.semaphore(n)), n)

        Hre = [sb(f"hre{i}", [128, TO], BF16) for i in range(2)]
        Him = [sb(f"him{i}", [128, TO], BF16) for i in range(2)]
        lnF = sb("lnF", [128, DK * TT], BF16)
        psq = sb("psq", [128, D])
        dbc = sb("dbc", [128, D])
        cT = [sb("creb", [128, 2 * D], BF16), sb("cimb", [128, 2 * D], BF16)]
        ident = sb("ident", [128, 128], BF16)
        cbc = sb("cbc", [128, 8])
        stat_rstd = sb("st_rstd", [128, 64])
        stat_nb = sb("st_nb", [128, 64])
        stat_ssum = sb("st_ssum", [128, 64])
        stat_sqs = sb("st_sqs", [128, 64])
        stat2_rstd = sb("s2_rstd", [128, 32])
        stat2_nb = sb("s2_nb", [128, 32])
        stat2_ssum = sb("s2_ssum", [128, 32])
        stat2_sqs = sb("s2_sqs", [128, 32])
        nusb = sb("nusb", [128, 2])
        thsb = sb("thsb", [128, 2])
        pr_mag = sb("pr_mag", [128, 2])
        pr_gam = sb("pr_gam", [128, 2])
        pr_cos = sb("pr_cos", [128, 2])
        pr_sin = sb("pr_sin", [128, 2])
        pt1 = sb("pt1", [128, 2])
        pt2 = sb("pt2", [128, 2])
        pt3 = sb("pt3", [128, 2])
        pt4 = sb("pt4", [128, 2])
        w2k_c = sb("w2k_c", [128, 2])
        w2k_s = sb("w2k_s", [128, 2])
        sc_s1 = sb("sc_s1", [128, 1])
        sc_s3 = sb("sc_s3", [128, 1])
        sc_s2 = sb("sc_s2", [128, 1])
        sc_thr = [sb(f"sc_thr{i}", [128, 1]) for i in range(3)]
        redarr = sb("redarr", [128, 3 * 128])
        redtot = sb("redtot", [128, 1])
        rrow = sb("rrow", [1, 128])
        rsum = sb("rsum", [1, 1])
        hreL = sb("hreL", [128, 2])
        himL = sb("himL", [128, 2])
        s1i = sb("s1i", [128, 2])
        s2i = sb("s2i", [128, 2])
        gt1 = sb("gt1", [128, 2])
        gt2 = sb("gt2", [128, 2])
        mean_t = sb("mean_t", [128, 4])
        meann_t = sb("meann_t", [128, 4])
        msq_t = sb("msq_t", [128, 4])
        var_t = sb("var_t", [128, 4])

        a_ctx = ExitStack()
        asb = lambda n, s, d=F32: a_ctx.enter_context(nc.sbuf_tensor(n, s, d))
        bT = [asb("breb", [128, DK * NST], BF16), asb("bimb", [128, DK * NST], BF16)]
        xar = [asb(f"xar{i}", [128, D]) for i in range(4)]
        ln1tok = asb("ln1tok", [128, BPT * D], BF16)
        scr = [asb(f"scr{i}", [128, TT]) for i in range(4)]
        s1t = asb("s1t", [128, TT])
        s2t = asb("s2t", [128, TT])
        cs_t = asb("cs_t", [128, 2 * TT])
        sn_t = asb("sn_t", [128, 2 * TT])
        csp_t = asb("csp_t", [128, 2 * TT])
        snp_t = asb("snp_t", [128, 2 * TT])
        tin = [asb(f"tin{i}", [128, 768]) for i in range(4)]
        tins = [asb(f"tins{i}", [128, SSCH]) for i in range(4)]
        tq = [asb("tq0", [128, 768], BF16), asb("tq1", [128, 768], BF16)]
        tl = asb("tl", [128, 768], BF16)
        a_ctx.close()

        x1 = [sb(f"x1_{i}", [128, D]) for i in range(BPT)]
        gbuf = sb("gbuf", [128, HCN * TT], BF16)
        ln2tok = [sb(f"ln2tok{i}", [128, D], BF16) for i in range(2)]
        w1p = [sb(f"w1p{i}", [128, D], BF16) for i in range(2)]
        w3p = [sb(f"w3p{i}", [128, D], BF16) for i in range(2)]
        w2c = [sb(f"w2c{i}", [128, 2 * TT], BF16) for i in range(4)]
        xq = [sb(f"xq{i}", [128, TT]) for i in range(2)]
        lnbq = [sb(f"lnbq{i}", [128, TT]) for i in range(2)]
        silb = [sb(f"silb{i}", [128, TT]) for i in range(2)]
        ost = [sb(f"ost{i}", [128, TT]) for i in range(2)]

        pbu = [ps_(f"pbu{i}") for i in range(4)]
        ptr = [ps_(f"ptr{i}") for i in range(2)]
        py = [ps_(f"py{i}") for i in range(2)]

        sems = {}
        for n in ["A_xa", "A_sq", "A_var", "A_rstd", "A_nb", "A_ln1tok", "A_trps",
                  "A_trevac", "A_bu", "A_bufree", "A_h", "L_pro", "L_cast", "TV",
                  "TS", "G_tld", "V_red", "V_q", "G_twr", "G_st", "B_x", "B_lnb",
                  "B_base", "B_yps", "B_yfree", "B_sq2", "B_var2", "B_rstd2",
                  "B_nb2", "B_ln2tok", "B_p1", "B_p3", "B_p2", "B_h1d", "B_h3d",
                  "B_sil", "B_g", "B_outsb", "B_w2u", "G_out", "V_zw2", "L_ts", "S_red",
                  "V_thr0", "V_thr1", "V_thr2"]:
            sems[n] = mksem(n)
        g = lambda n: sems[n]
        V_thr = [g("V_thr0"), g("V_thr1"), g("V_thr2")]

        ptiles = {"nusb": nusb, "thsb": thsb, "cbc": cbc, "pt1": pt1, "pt2": pt2,
                  "pt3": pt3, "pt4": pt4, "pr_mag": pr_mag, "pr_gam": pr_gam,
                  "pr_cos": pr_cos, "pr_sin": pr_sin}
        PSTEPS = _prologue_steps(ptiles)
        NPS_S = sum(1 for e, _ in PSTEPS if e == "s")
        NPS_V = sum(1 for e, _ in PSTEPS if e == "v")

        block = ctx.enter_context(nc.Block())

        @block.sync
        def _(sync):
            E = Eng(sync, "sync", log)
            E.put(sync.dma_start(out=cbc[:], in_=consts.broadcast_to((128, 8))),
                  g("L_pro"), 16)
            E.put(sync.dma_start(out=nusb[:],
                                 in_=nu_in.rearrange("o (h p) -> p (o h)", p=128),
                                 allow_slow_non_contiguous=True),
                  g("L_pro"), 16)
            E.put(sync.dma_start(out=thsb[:],
                                 in_=th_in.rearrange("o (h p) -> p (o h)", p=128),
                                 allow_slow_non_contiguous=True),
                  g("L_pro"), 16)
            E.put(sync.dma_start(out=ident[:], in_=ident_in), g("L_pro"), 16)
            E.put(sync.dma_start(out=dbc[:], in_=dvec.broadcast_to((128, D))),
                  g("L_pro"), 16)
            s_chunks = []
            for widx in (0, 1, 2):
                for ci in range(NSSC):
                    s_chunks.append((widx, ci))
            flats = [w1flat, w3flat, w2flat]
            si = 0
            for gg in range(NTA * BPT):
                if gg >= 4:
                    E.wait(g("A_ln1tok"), gg - 3)
                E.put(sync.dma_start(out=xar[gg % 4][:],
                                     in_=xa[gg * 128:(gg + 1) * 128, :]),
                      g("A_xa"), 16)
                for _ in range(8):
                    if si < len(s_chunks):
                        widx, ci = s_chunks[si]
                        cw = SSCH if ci < NSSC - 1 else SS_TAIL
                        if si >= 4:
                            E.wait(g("S_red"), si - 3)
                        E.put(sync.dma_start(
                            out=tins[si % 4][:, 0:cw],
                            in_=flats[widx][:, ci * SSCH:ci * SSCH + cw]),
                            g("L_ts"), 16)
                        si += 1
            E.wait(g("G_twr"), 16 * G_TWR_TOTAL)
            E.wait(g("A_h"), NTA)
            for j in range(NTB):
                for hc in range(HCN):
                    gi = HCN * j + hc
                    if gi >= 2:
                        E.wait(g("B_h1d"), gi - 1)
                    E.put(sync.dma_start(out=w1p[hc % 2][:], in_=w1qs[hc, :, :]),
                          g("B_p1"), 16)
                    if gi >= 2:
                        E.wait(g("B_h3d"), gi - 1)
                    E.put(sync.dma_start(out=w3p[hc % 2][:], in_=w3qs[hc, :, :]),
                          g("B_p3"), 16)
                for pair in range(2):
                    for hc in range(HCN):
                        widx = (2 * j + pair) * HCN + hc
                        if widx >= 4:
                            E.wait(g("B_w2u"), widx - 3)
                        E.put(sync.dma_start(
                            out=w2c[widx % 4][:],
                            in_=w2qs[hc, :, pair * 1024:(pair + 1) * 1024]),
                            g("B_p2"), 16)

        @block.scalar
        def _(scalar):
            E = Eng(scalar, "scalar", log)
            E.wait(g("L_pro"), 16 * 3)
            nv = 0
            ns = 0
            for eng, fn in PSTEPS:
                if eng == "s":
                    E.wait(g("TV"), nv)
                    E.put(fn(scalar), g("TS"), 1)
                    ns += 1
                    E.wait(g("TS"), ns)
                else:
                    nv += 1

            s_meta = []
            for widx in (0, 1, 2):
                for ci in range(NSSC):
                    s_meta.append((widx, ci, SSCH if ci < NSSC - 1 else SS_TAIL))
            s_i = [0]

            def abs_unit():
                i = s_i[0]
                if i >= len(s_meta):
                    return
                widx, ci, cw = s_meta[i]
                E.wait(g("L_ts"), 16 * (i + 1))
                E.put(scalar.activation(out=psq[:, 0:cw], in_=tins[i % 4][:, 0:cw],
                                        func=AF.Abs,
                                        accum_out=redarr[:, widx * 128 + ci:
                                                         widx * 128 + ci + 1]),
                      g("S_red"))
                s_i[0] += 1

            for k in range(NTA):
                for b in range(BPT):
                    gg = BPT * k + b
                    E.wait(g("A_xa"), 16 * (gg + 1))
                    E.put(scalar.activation(out=psq[:], in_=xar[gg % 4][:],
                                            func=AF.Square,
                                            accum_out=stat_sqs[:, gg:gg + 1]),
                          g("A_sq") if b == 3 else None)
                    for _ in range(8):
                        abs_unit()
                E.wait(g("A_var"), k + 1)
                E.put(scalar.activation(out=var_t[:], in_=var_t[:], func=AF.Ln,
                                        bias=cbc[:, C_EPS:C_EPS + 1]), g("TS"))
                E.wait(g("TS"), NPS_S + k + 1)
                E.put(scalar.activation(out=stat_rstd[:, 4 * k:4 * k + 4],
                                        in_=var_t[:], func=AF.Exp, scale=-0.5),
                      g("A_rstd"))
                E.wait(g("A_nb"), k + 1)
                for b in range(BPT):
                    gg = BPT * k + b
                    E.put(scalar.activation(out=ln1tok[:, b * D:(b + 1) * D],
                                            in_=xar[gg % 4][:], func=AF.Identity,
                                            scale=stat_rstd[:, gg:gg + 1],
                                            bias=stat_nb[:, gg:gg + 1]),
                          g("A_ln1tok"))


            def xload(j, p):
                qg = 16 * j + p
                b, dq = p % 4, p // 4
                if qg == 0:
                    E.wait(g("A_h"), NTA)
                    E.wait(g("V_q"), 2 * NQ_ALL)
                if qg >= 2:
                    E.wait(g("B_base"), qg - 1)
                r0 = TO + j * TT + b * 128
                E.put(scalar.dma_start(out=xq[qg % 2][:],
                                       in_=xa[r0:r0 + 128, dq * TT:(dq + 1) * TT]),
                      g("B_x"), 16)

            def lnb(j, p):
                qg = 16 * j + p
                b = p % 4
                gg = 32 + 4 * j + b
                E.wait(g("B_x"), 16 * (qg + 1))
                if qg >= 2:
                    E.wait(g("B_base"), qg - 1)
                E.put(scalar.activation(out=lnbq[qg % 2][:], in_=xq[qg % 2][:],
                                        func=AF.Identity,
                                        scale=stat_rstd[:, gg:gg + 1],
                                        bias=stat_nb[:, gg:gg + 1]), g("B_lnb"))

            def sq2_chain(j):
                for b in range(BPT):
                    gg = 4 * j + b
                    E.wait(g("B_yfree"), 16 * j + 4 * b + 4)
                    E.put(scalar.activation(out=psq[:], in_=x1[b][:], func=AF.Square,
                                            accum_out=stat2_sqs[:, gg:gg + 1]),
                          g("B_sq2") if b == 3 else None)
                E.wait(g("B_var2"), j + 1)
                E.put(scalar.activation(out=var_t[:], in_=var_t[:], func=AF.Ln,
                                        bias=cbc[:, C_EPS:C_EPS + 1]), g("TS"))
                E.wait(g("TS"), NPS_S + NTA + j + 1)
                E.put(scalar.activation(out=stat2_rstd[:, 4 * j:4 * j + 4],
                                        in_=var_t[:], func=AF.Exp, scale=-0.5),
                      g("B_rstd2"))
                E.wait(g("B_nb2"), j + 1)
                if j == 0:
                    E.wait(g("A_h"), NTA)
                for b in range(BPT):
                    gg = 4 * j + b
                    prev_groups = 16 * j + 4 * b - 8
                    if prev_groups >= 0:
                        E.wait(g("A_trps"), 256 + prev_groups + 4)
                    E.put(scalar.activation(out=ln2tok[b % 2][:], in_=x1[b][:],
                                            func=AF.Identity,
                                            scale=stat2_rstd[:, gg:gg + 1],
                                            bias=stat2_nb[:, gg:gg + 1]),
                          g("B_ln2tok"))

            def silu(j, hc):
                gi = HCN * j + hc
                E.wait(g("B_h1d"), gi + 1)
                if gi >= 2:
                    E.wait(g("B_g"), gi - 1)
                bank = (hc % 2) * 2
                E.put(scalar.activation(out=silb[hc % 2][:], in_=pbu[bank][:],
                                        func=AF.Silu, scale=sc_s1[:, 0:1]),
                      g("B_sil"))

            for p in range(16):
                xload(0, p)
                lnb(0, p)
            sq2_chain(0)
            for j in range(NTB):
                for hc in range(HCN):
                    silu(j, hc)
                if j + 1 < NTB:
                    for p in range(16):
                        xload(j + 1, p)
                        lnb(j + 1, p)
                    sq2_chain(j + 1)

        @block.vector
        def _(vector):
            E = Eng(vector, "vector", log)
            tvn = [0]

            def vput(inst):
                E.put(inst, g("TV"), 1)
                tvn[0] += 1

            def vbar():
                E.wait(g("TV"), tvn[0])

            E.wait(g("L_pro"), 16 * 5)
            ns = 0
            for eng, fn in PSTEPS:
                if eng == "v":
                    E.wait(g("TS"), ns)
                    vput(fn(vector))
                    vbar()
                else:
                    ns += 1
            E.wait(g("TS"), NPS_S)

            for nh in range(2):
                c0 = nh * TT
                nhs = slice(nh, nh + 1)
                vput(vector.tensor_copy(cs_t[:, c0:c0 + 1], cbc[:, C_ONE:C_ONE + 1]))
                vput(vector.tensor_copy(sn_t[:, c0:c0 + 1], cbc[:, C_ZERO:C_ZERO + 1]))
                vput(vector.tensor_copy(w2k_c[:, nhs], pr_cos[:, nhs]))
                vput(vector.tensor_copy(w2k_s[:, nhs], pr_sin[:, nhs]))
                vbar()
                w = 1
                while w < TT:
                    cc = w2k_c[:, nhs]
                    ss = w2k_s[:, nhs]
                    vput(vector.tensor_scalar(out=scr[0][:, 0:w], in0=cs_t[:, c0:c0 + w],
                                              scalar1=cc, scalar2=None, op0=ALU.mult))
                    vput(vector.tensor_scalar(out=scr[1][:, 0:w], in0=sn_t[:, c0:c0 + w],
                                              scalar1=ss, scalar2=None, op0=ALU.mult))
                    vput(vector.tensor_scalar(out=scr[2][:, 0:w], in0=sn_t[:, c0:c0 + w],
                                              scalar1=cc, scalar2=None, op0=ALU.mult))
                    vput(vector.tensor_scalar(out=scr[3][:, 0:w], in0=cs_t[:, c0:c0 + w],
                                              scalar1=ss, scalar2=None, op0=ALU.mult))
                    vbar()
                    vput(vector.tensor_sub(cs_t[:, c0 + w:c0 + 2 * w],
                                           scr[0][:, 0:w], scr[1][:, 0:w]))
                    vput(vector.tensor_add(sn_t[:, c0 + w:c0 + 2 * w],
                                           scr[2][:, 0:w], scr[3][:, 0:w]))
                    vput(vector.tensor_mul(pt1[:, nhs], cc, cc))
                    vput(vector.tensor_mul(pt2[:, nhs], ss, ss))
                    vput(vector.tensor_mul(pt4[:, nhs], cc, ss))
                    vbar()
                    vput(vector.tensor_sub(w2k_c[:, nhs], pt1[:, nhs], pt2[:, nhs]))
                    vput(vector.tensor_scalar_mul(w2k_s[:, nhs], pt4[:, nhs], 2.0))
                    vbar()
                    w *= 2
                vput(vector.tensor_scalar(out=csp_t[:, c0:c0 + TT],
                                          in0=cs_t[:, c0:c0 + TT],
                                          scalar1=pr_gam[:, nhs], scalar2=None,
                                          op0=ALU.mult))
                vput(vector.tensor_scalar(out=snp_t[:, c0:c0 + TT],
                                          in0=sn_t[:, c0:c0 + TT],
                                          scalar1=pr_gam[:, nhs], scalar2=None,
                                          op0=ALU.mult))
                vbar()

            tern_ops = []
            fins = {}
            thrs = {}
            tld_n = [0]
            qn_n = [0]
            for widx in (0, 1, 2):

                def fin(widx=widx):
                    E.wait(g("S_red"), NSSC * (widx + 1))
                    E.put(vector.reduce_sum(redtot[:],
                                            redarr[:, widx * 128:widx * 128 + NSSC],
                                            axis=AXX), V_thr[widx])
                    E.wait(g("G_st"), 16 * (4 * widx + 2))
                    E.put(vector.reduce_sum(rsum[:], rrow[:], axis=AXX), V_thr[widx])
                    E.wait(V_thr[widx], 2)
                    E.put(vector.tensor_scalar_mul(rsum[:], rsum[:],
                                                   1.0 / (D * HID)), V_thr[widx])
                fins[widx] = fin

                def thr(widx=widx):
                    st = [sc_s1, sc_s3, sc_s2][widx]
                    E.wait(g("G_st"), 16 * (4 * widx + 4))
                    E.put(vector.tensor_scalar_mul(sc_thr[widx][:], st[:], 0.5),
                          V_thr[widx])
                thrs[widx] = thr

            def add_q(widx, pw, valid, zero_first):
                def f():
                    q = qn_n[0]
                    tqs = tq[q % 2]
                    if q >= 2:
                        E.wait(g("G_twr"), 16 * (q - 1))
                    if zero_first:
                        E.put(vector.tensor_copy(
                            tin[q % 4][64:128, 0:768],
                            cbc[64:128, C_ZERO:C_ZERO + 1].broadcast_to((64, 768))),
                            g("V_zw2"))
                    E.wait(g("G_tld"), 16 * (q + 1))
                    E.wait(V_thr[widx], 4)
                    thr_ap = sc_thr[widx][:, 0:1]
                    E.put(vector.tensor_scalar(out=tl[:, 0:valid],
                                               in0=tin[q % 4][:, 0:valid],
                                               scalar1=-1.0, scalar2=thr_ap,
                                               op0=ALU.mult, op1=ALU.is_gt))
                    if pw > valid:
                        E.put(vector.scalar_tensor_tensor(
                            out=tqs[:, 0:valid], in0=tin[q % 4][:, 0:valid],
                            scalar=thr_ap, in1=tl[:, 0:valid],
                            op0=ALU.is_gt, op1=ALU.subtract), g("V_q"))
                        E.put(vector.tensor_copy(
                            tqs[:, valid:pw],
                            cbc[:, C_ZERO:C_ZERO + 1].broadcast_to((128, pw - valid))),
                            g("V_q"))
                    else:
                        E.put(vector.scalar_tensor_tensor(
                            out=tqs[:, 0:valid], in0=tin[q % 4][:, 0:valid],
                            scalar=thr_ap, in1=tl[:, 0:valid],
                            op0=ALU.is_gt, op1=ALU.subtract), g("V_q"), 2)
                    tld_n[0] += 1
                    qn_n[0] += 1
                tern_ops.append(f)

            for widx in (0, 1):
                tern_ops.append(fins[widx])
                tern_ops.append(thrs[widx])
                for dk in range(DK):
                    for (pw, valid) in W13_SUB9:
                        add_q(widx, pw, valid, 0)
            tern_ops.append(fins[2])
            tern_ops.append(thrs[2])
            for hc in range(HCN):
                for (pw, valid) in W2_SUB:
                    add_q(2, pw, valid, hc == HCN - 1)

            n_tern = len(tern_ops)
            tern_sched = {}
            acc = 0
            for k in range(5, 16):
                cnt = (n_tern - acc + (15 - k)) // (16 - k)
                tern_sched[k] = (acc, acc + cnt)
                acc += cnt
            ti = [0]

            for k in range(NTA):
                for b in range(BPT):
                    gg = BPT * k + b
                    E.wait(g("A_xa"), 16 * (gg + 1))
                    E.put(vector.reduce_sum(stat_ssum[:, gg:gg + 1], xar[gg % 4][:],
                                            axis=AXX))
                E.wait(g("A_sq"), k + 1)
                if k > 0:
                    E.wait(g("A_rstd"), k)
                c4 = slice(4 * k, 4 * k + 4)
                vput(vector.tensor_scalar_mul(mean_t[:], stat_ssum[:, c4], 1.0 / D))
                vput(vector.tensor_scalar_mul(meann_t[:], stat_ssum[:, c4], -1.0 / D))
                vbar()
                vput(vector.tensor_mul(msq_t[:], mean_t[:], mean_t[:]))
                vbar()
                E.put(vector.scalar_tensor_tensor(out=var_t[:], in0=stat_sqs[:, c4],
                                                  scalar=1.0 / D, in1=msq_t[:],
                                                  op0=ALU.mult, op1=ALU.subtract),
                      g("A_var"))
                E.wait(g("A_rstd"), k + 1)
                E.put(vector.tensor_mul(stat_nb[:, c4], meann_t[:], stat_rstd[:, c4]),
                      g("A_nb"))
                for dk in range(DK):
                    c = DK * k + dk
                    E.wait(g("A_trps"), c + 1)
                    E.put(vector.tensor_copy(lnF[:, dk * TT:(dk + 1) * TT],
                                             ptr[dk % 2][:]), g("A_trevac"))
                if k in tern_sched:
                    lo, hi = tern_sched[k]
                    while ti[0] < min(hi, n_tern):
                        tern_ops[ti[0]]()
                        ti[0] += 1
                E.wait(g("A_bu"), k + 1)
                for nh in range(2):
                    c0 = nh * TT
                    nhs = slice(nh, nh + 1)
                    psre = pbu[2 * nh + 0]
                    psim = pbu[2 * nh + 1]
                    E.put(vector.tensor_mul(scr[0][:], csp_t[:, c0:c0 + TT], psre[:]))
                    E.put(vector.tensor_mul(scr[1][:], snp_t[:, c0:c0 + TT], psim[:]))
                    E.put(vector.tensor_mul(scr[2][:], snp_t[:, c0:c0 + TT], psre[:]))
                    E.put(vector.tensor_mul(scr[3][:], csp_t[:, c0:c0 + TT], psim[:]),
                          g("A_bufree") if nh == 1 else None)
                    E.put(vector.tensor_add(scr[0][:], scr[0][:], scr[1][:]))
                    E.put(vector.tensor_sub(scr[2][:], scr[2][:], scr[3][:]))
                    if k == 0:
                        init1 = 0.0
                        init2 = 0.0
                    else:
                        vput(vector.tensor_scalar(out=gt1[:, nhs], in0=hreL[:, nhs],
                                                  scalar1=pr_cos[:, nhs], scalar2=None,
                                                  op0=ALU.mult))
                        vput(vector.tensor_scalar(out=gt2[:, nhs], in0=himL[:, nhs],
                                                  scalar1=pr_sin[:, nhs], scalar2=None,
                                                  op0=ALU.mult))
                        vbar()
                        vput(vector.tensor_add(s1i[:, nhs], gt1[:, nhs], gt2[:, nhs]))
                        vput(vector.tensor_scalar(out=gt1[:, nhs], in0=himL[:, nhs],
                                                  scalar1=pr_cos[:, nhs], scalar2=None,
                                                  op0=ALU.mult))
                        vput(vector.tensor_scalar(out=gt2[:, nhs], in0=hreL[:, nhs],
                                                  scalar1=pr_sin[:, nhs], scalar2=None,
                                                  op0=ALU.mult))
                        vbar()
                        vput(vector.tensor_sub(s2i[:, nhs], gt1[:, nhs], gt2[:, nhs]))
                        vbar()
                        init1 = s1i[:, nhs]
                        init2 = s2i[:, nhs]
                    rbc = pr_mag[:, nhs].broadcast_to((128, TT))
                    E.put(vector.tensor_tensor_scan(out=s1t[:], data0=rbc,
                                                    data1=scr[0][:], initial=init1,
                                                    op0=ALU.mult, op1=ALU.add))
                    E.put(vector.tensor_tensor_scan(out=s2t[:], data0=rbc,
                                                    data1=scr[2][:], initial=init2,
                                                    op0=ALU.mult, op1=ALU.add))
                    t0 = (k % 8) * TT
                    dre_s = Hre[nh][:, t0:t0 + TT]
                    dim_s = Him[nh][:, t0:t0 + TT]
                    E.put(vector.tensor_mul(scr[0][:], cs_t[:, c0:c0 + TT], s1t[:]))
                    E.put(vector.tensor_mul(scr[1][:], sn_t[:, c0:c0 + TT], s2t[:]))
                    E.put(vector.tensor_add(dre_s, scr[0][:], scr[1][:]))
                    vput(vector.tensor_add(hreL[:, nhs], scr[0][:, TT - 1:TT],
                                           scr[1][:, TT - 1:TT]))
                    E.put(vector.tensor_mul(scr[2][:], cs_t[:, c0:c0 + TT], s2t[:]))
                    E.put(vector.tensor_mul(scr[3][:], sn_t[:, c0:c0 + TT], s1t[:]))
                    E.put(vector.tensor_sub(dim_s, scr[2][:], scr[3][:]),
                          g("A_h") if nh == 1 else None)
                    vput(vector.tensor_sub(himL[:, nhs], scr[2][:, TT - 1:TT],
                                           scr[3][:, TT - 1:TT]))
                    vbar()
            while ti[0] < n_tern:
                tern_ops[ti[0]]()
                ti[0] += 1

            def base_q(j, p):
                qg = 16 * j + p
                b, dq = p % 4, p // 4
                E.wait(g("B_lnb"), qg + 1)
                sl = slice(dq * TT, (dq + 1) * TT)
                E.put(vector.tensor_mul(x1[b][:, sl], lnbq[qg % 2][:], dbc[:, sl]))
                E.put(vector.tensor_add(x1[b][:, sl], x1[b][:, sl], xq[qg % 2][:]),
                      g("B_base"))

            def yevac(j, qi):
                yc = 16 * j + qi
                b, dq = qi // 4, qi % 4
                E.wait(g("B_yps"), yc + 1)
                sl = slice(dq * TT, (dq + 1) * TT)
                E.put(vector.tensor_add(x1[b][:, sl], x1[b][:, sl], py[qi % 2][:]),
                      g("B_yfree"))
                if dq == 3:
                    gg = 4 * j + b
                    E.put(vector.reduce_sum(stat2_ssum[:, gg:gg + 1], x1[b][:],
                                            axis=AXX))

            def stats2(j):
                E.wait(g("B_sq2"), j + 1)
                if j > 0:
                    E.wait(g("B_rstd2"), j)
                else:
                    E.wait(g("A_rstd"), NTA)
                c4 = slice(4 * j, 4 * j + 4)
                vput(vector.tensor_scalar_mul(mean_t[:], stat2_ssum[:, c4], 1.0 / D))
                vput(vector.tensor_scalar_mul(meann_t[:], stat2_ssum[:, c4], -1.0 / D))
                vbar()
                vput(vector.tensor_mul(msq_t[:], mean_t[:], mean_t[:]))
                vbar()
                E.put(vector.scalar_tensor_tensor(out=var_t[:], in0=stat2_sqs[:, c4],
                                                  scalar=1.0 / D, in1=msq_t[:],
                                                  op0=ALU.mult, op1=ALU.subtract),
                      g("B_var2"))
                E.wait(g("B_rstd2"), j + 1)
                E.put(vector.tensor_mul(stat2_nb[:, c4], meann_t[:],
                                        stat2_rstd[:, c4]), g("B_nb2"))

            def tr2evac(j):
                for b in range(BPT):
                    for dkg in range(4):
                        tc = 256 + 16 * j + 4 * b + dkg
                        E.wait(g("A_trps"), tc + 1)
                        outap = lnF[:].rearrange("p (dk c) -> p dk c", dk=DK)[
                            :, 4 * dkg:4 * dkg + 4, b * 128:(b + 1) * 128]
                        srcap = ptr[(4 * b + dkg) % 2][:].rearrange(
                            "p (i c) -> p i c", i=4)
                        E.put(vector.tensor_copy(outap, srcap), g("A_trevac"))

            def gop(j, hc):
                gi = HCN * j + hc
                E.wait(g("B_h3d"), gi + 1)
                E.wait(g("B_sil"), gi + 1)
                if j > 0 and hc == 0:
                    E.wait(g("B_w2u"), 86 * j)
                bank = (hc % 2) * 2 + 1
                E.put(vector.scalar_tensor_tensor(out=gbuf[:, hc * TT:(hc + 1) * TT],
                                                  in0=pbu[bank][:],
                                                  scalar=sc_s3[:, 0:1],
                                                  in1=silb[hc % 2][:], op0=ALU.mult,
                                                  op1=ALU.mult), g("B_g"))

            psmap_v = [pbu[0], pbu[1], pbu[2], pbu[3], ptr[0], ptr[1], py[0], py[1]]

            def outevac(j, pair):
                E.wait(g("B_w2u"), 86 * j + 43 * (pair + 1))
                for dq2 in range(2):
                    dq = 2 * pair + dq2
                    for b in range(BPT):
                        oi = 16 * j + 4 * dq + b
                        if oi >= 2:
                            E.wait(g("G_out"), 16 * (oi - 1))
                        sl = slice(dq * TT, (dq + 1) * TT)
                        E.put(vector.scalar_tensor_tensor(
                            out=ost[oi % 2][:], in0=psmap_v[4 * dq2 + b][:],
                            scalar=sc_s2[:, 0:1], in1=x1[b][:, sl],
                            op0=ALU.mult, op1=ALU.add), g("B_outsb"))

            for p in range(16):
                base_q(0, p)
            for qi in range(16):
                yevac(0, qi)
            stats2(0)
            tr2evac(0)
            for j in range(NTB):
                for hc in range(HCN):
                    gop(j, hc)
                for pair in range(2):
                    outevac(j, pair)
                    if j + 1 < NTB:
                        for dq2 in range(2):
                            for b in range(BPT):
                                base_q(j + 1, 4 * (2 * pair + dq2) + b)
                if j + 1 < NTB:
                    for qi in range(16):
                        yevac(j + 1, qi)
                    stats2(j + 1)
                    tr2evac(j + 1)

        @block.tensor
        def _(tensor):
            E = Eng(tensor, "tensor", log)
            E.wait(g("L_pro"), 16 * 4)
            E.wait(g("L_cast"), 16 * 4)

            for k in range(NTA):
                E.wait(g("A_ln1tok"), 4 * k + 4)
                for dk in range(DK):
                    c = DK * k + dk
                    E.wait(g("A_trevac"), c - 1)
                    for b in range(BPT):
                        E.put(tensor.matmul(
                            ptr[dk % 2][:, b * 128:(b + 1) * 128],
                            ln1tok[:, b * D + dk * 128:b * D + dk * 128 + 128],
                            ident[:], start=(b == 0), stop=(b == 3),
                            skip_group_check=True),
                            g("A_trps") if b == 3 else None)
                if k > 0:
                    E.wait(g("A_bufree"), k)
                for dk in range(DK):
                    c = DK * k + dk
                    E.wait(g("A_trevac"), c + 1)
                    for nh in range(2):
                        for cm in range(2):
                            last = (dk == DK - 1 and nh == 1 and cm == 1)
                            E.put(tensor.matmul(
                                pbu[2 * nh + cm][:],
                                bT[cm][:, dk * NST + nh * 128:dk * NST + nh * 128 + 128],
                                lnF[:, dk * TT:(dk + 1) * TT],
                                start=(dk == 0), stop=(dk == DK - 1),
                                skip_group_check=True),
                                g("A_bu") if last else None)

            def Y(j):
                for qi in range(16):
                    b, dq = qi // 4, qi % 4
                    yc = 16 * j + qi
                    if qi == 0:
                        E.wait(g("A_h"), min(9 + j, NTA))
                        if j > 0:
                            E.wait(g("B_outsb"), 16 * j)
                    E.wait(g("B_yfree"), yc - 1)
                    t0 = j * TT + b * 128
                    srcs = [(Hre[0], cT[0], 0), (Him[0], cT[1], 0),
                            (Hre[1], cT[0], 1), (Him[1], cT[1], 1)]
                    for i, (hsrc, csrc, nh) in enumerate(srcs):
                        E.put(tensor.matmul(
                            py[qi % 2][:], hsrc[:, t0:t0 + 128],
                            csrc[:, nh * D + dq * TT:nh * D + (dq + 1) * TT],
                            start=(i == 0), stop=(i == 3),
                            skip_group_check=True),
                            g("B_yps") if i == 3 else None)

            def TR2(j):
                if j > 0:
                    E.wait(g("B_outsb"), 16 * j)
                for b in range(BPT):
                    E.wait(g("B_ln2tok"), 4 * j + b + 1)
                    for dkg in range(4):
                        tc = 256 + 16 * j + 4 * b + dkg
                        E.wait(g("A_trevac"), tc - 1)
                        for i in range(4):
                            dk = 4 * dkg + i
                            E.put(tensor.matmul(
                                ptr[(4 * b + dkg) % 2][:, i * 128:(i + 1) * 128],
                                ln2tok[b % 2][:, dk * 128:(dk + 1) * 128],
                                ident[:], start=(i == 0), stop=(i == 3),
                                skip_group_check=True),
                                g("A_trps") if i == 3 else None)

            def H(j):
                if j == 0:
                    E.wait(g("A_bufree"), NTA)
                for hc in range(HCN):
                    gi = HCN * j + hc
                    if hc == 0:
                        E.wait(g("A_trevac"), 256 + 16 * j + 16)
                        E.wait(g("B_outsb"), 16 * j)
                    E.wait(g("B_p1"), 16 * (gi + 1))
                    if gi >= 2:
                        E.wait(g("B_sil"), gi - 1)
                    b1 = (hc % 2) * 2
                    for dk in range(DK):
                        E.put(tensor.matmul(
                            pbu[b1][:], w1p[hc % 2][:, dk * 128:(dk + 1) * 128],
                            lnF[:, dk * TT:(dk + 1) * TT],
                            start=(dk == 0), stop=(dk == DK - 1),
                            skip_group_check=True),
                            g("B_h1d") if dk == DK - 1 else None)
                    E.wait(g("B_p3"), 16 * (gi + 1))
                    if gi >= 2:
                        E.wait(g("B_g"), gi - 1)
                    for dk in range(DK):
                        E.put(tensor.matmul(
                            pbu[b1 + 1][:], w3p[hc % 2][:, dk * 128:(dk + 1) * 128],
                            lnF[:, dk * TT:(dk + 1) * TT],
                            start=(dk == 0), stop=(dk == DK - 1),
                            skip_group_check=True),
                            g("B_h3d") if dk == DK - 1 else None)

            psmap = [pbu[0], pbu[1], pbu[2], pbu[3], ptr[0], ptr[1], py[0], py[1]]

            def OUT(j):
                for pair in range(2):
                    for hc in range(HCN):
                        widx = (2 * j + pair) * HCN + hc
                        E.wait(g("B_p2"), 16 * (widx + 1))
                        E.wait(g("B_g"), HCN * j + hc + 1)
                        if hc == 0:
                            if pair == 0:
                                E.wait(g("B_g"), HCN * (j + 1))
                                E.wait(g("B_yfree"), 16 * (j + 1))
                                E.wait(g("A_trevac"), 256 + 16 * j + 16)
                            else:
                                E.wait(g("B_outsb"), 16 * j + 8)
                        for b in range(BPT):
                            for dq2 in range(2):
                                last = (b == BPT - 1 and dq2 == 1)
                                E.put(tensor.matmul(
                                    psmap[4 * dq2 + b][:],
                                    gbuf[:, hc * TT + b * 128:hc * TT + (b + 1) * 128],
                                    w2c[widx % 4][:, dq2 * TT:(dq2 + 1) * TT],
                                    start=(hc == 0), stop=(hc == HCN - 1),
                                    skip_group_check=True),
                                    g("B_w2u") if last else None)

            Y(0)
            TR2(0)
            for j in range(NTB):
                H(j)
                OUT(j)
                if j + 1 < NTB:
                    Y(j + 1)
                    TR2(j + 1)

        @block.gpsimd
        def _(gpsimd):
            E = Eng(gpsimd, "gpsimd", log)
            E.put(gpsimd.dma_start(out=bT[0][:].rearrange("p (dk n) -> p dk n", dk=DK),
                                   in_=breT.rearrange("(dk p) n -> p dk n", p=128)),
                  g("L_cast"), 16)
            E.put(gpsimd.dma_start(out=bT[1][:].rearrange("p (dk n) -> p dk n", dk=DK),
                                   in_=bimT.rearrange("(dk p) n -> p dk n", p=128)),
                  g("L_cast"), 16)
            E.put(gpsimd.dma_start(out=cT[0][:].rearrange("p (h d) -> p h d", h=2),
                                   in_=creT.rearrange("(h p) d -> p h d", p=128)),
                  g("L_cast"), 16)
            E.put(gpsimd.dma_start(out=cT[1][:].rearrange("p (h d) -> p h d", h=2),
                                   in_=cimT.rearrange("(h p) d -> p h d", p=128)),
                  g("L_cast"), 16)

            tld = [0]
            qn = [0]

            def roundtrip(widx):
                E.wait(V_thr[widx], 1)
                E.put(gpsimd.dma_start(
                    out=tstat[widx:widx + 1, :].rearrange("a p -> p a"),
                    in_=redtot[:]), g("G_st"), 16)
                E.wait(g("G_st"), 16 * (4 * widx + 1))
                E.put(gpsimd.dma_start(out=rrow[:], in_=tstat[widx:widx + 1, :]),
                      g("G_st"), 16)
                E.wait(V_thr[widx], 3)
                E.put(gpsimd.dma_start(out=tstat[4 + widx:5 + widx, 0:1], in_=rsum[:]),
                      g("G_st"), 16)
                E.wait(g("G_st"), 16 * (4 * widx + 3))
                st = [sc_s1, sc_s3, sc_s2][widx]
                E.put(gpsimd.dma_start(
                    out=st[:],
                    in_=tstat[4 + widx:5 + widx, 0:1].broadcast_to((128, 1))),
                    g("G_st"), 16)

            def qload(load_ap):
                q = qn[0]
                if q >= 4:
                    E.wait(g("V_q"), 2 * (q - 3))
                E.put(gpsimd.dma_start(out=load_ap[0], in_=load_ap[1]), g("G_tld"), 16)
                tld[0] += 1
                qn[0] += 1
                E.wait(g("V_q"), 2 * qn[0])

            for widx, (wsrc, wdst) in enumerate(((w1T, w1qs), (w3T, w3qs))):
                roundtrip(widx)
                for dk in range(DK):
                    h0 = 0
                    for (pw, valid) in W13_SUB9:
                        qload((tin[qn[0] % 4][:, 0:valid],
                               wsrc[dk * 128:(dk + 1) * 128, h0:h0 + valid]))
                        npan = pw // 128
                        hc0 = h0 // 128
                        store = wdst[hc0:hc0 + npan, :, dk * 128:(dk + 1) * 128]
                        store = store.rearrange("hc p jj -> p hc jj")
                        srcv = tq[(qn[0] - 1) % 2][:, 0:pw].rearrange(
                            "p (hc jj) -> p hc jj", hc=npan)
                        E.put(gpsimd.dma_start(out=store, in_=srcv), g("G_twr"), 16)
                        h0 += pw
            roundtrip(2)
            for hc in range(HCN):
                rows = 128 if hc < HCN - 1 else HID - 128 * (HCN - 1)
                h0 = 0
                for pi, (pw, valid) in enumerate(W2_SUB):
                    if rows < 128:
                        E.wait(g("V_zw2"), pi + 1)
                    qload((tin[qn[0] % 4][0:rows, 0:valid],
                           w2T[hc * 128:hc * 128 + rows, h0:h0 + valid]))
                    E.put(gpsimd.dma_start(
                        out=w2qs[hc, :, h0:h0 + pw],
                        in_=tq[(qn[0] - 1) % 2][:, 0:pw]), g("G_twr"), 16)
                    h0 += pw

            for j in range(NTB):
                for dq in range(4):
                    for b in range(BPT):
                        oi = 16 * j + 4 * dq + b
                        E.wait(g("B_outsb"), oi + 1)
                        r0 = j * TT + b * 128
                        E.put(gpsimd.dma_start(
                            out=out[r0:r0 + 128, dq * TT:(dq + 1) * TT],
                            in_=ost[oi % 2][:]), g("G_out"), 16)
            if DEBUG:
                E.wait(g("G_out"), 16 * 128)
                for srcp, c0, w in [(pr_mag, 0, 2), (pr_gam, 2, 2), (pr_cos, 4, 2),
                                    (pr_sin, 6, 2), (sc_s1, 8, 1), (sc_s3, 9, 1),
                                    (sc_s2, 10, 1), (sc_thr[0], 11, 1),
                                    (sc_thr[1], 12, 1), (sc_thr[2], 13, 1),
                                    (hreL, 14, 2), (himL, 16, 2), (var_t, 18, 4),
                                    (mean_t, 22, 4), (nusb, 26, 2), (thsb, 28, 2)]:
                    E.put(gpsimd.dma_start(out=dbg_small[:, c0:c0 + w], in_=srcp[:],
                                           allow_slow_non_contiguous=True),
                          g("G_out"), 16)
                E.put(gpsimd.dma_start(out=dbg_stat[:, 0:64], in_=stat_rstd[:],
                                       allow_slow_non_contiguous=True), g("G_out"), 16)
                E.put(gpsimd.dma_start(out=dbg_stat[:, 64:128], in_=stat_nb[:],
                                       allow_slow_non_contiguous=True), g("G_out"), 16)
                E.put(gpsimd.dma_start(out=dbg_h[:], in_=Hre[0][:]), g("G_out"), 16)
                E.wait(g("G_out"), 16 * (128 + 19))
            else:
                E.wait(g("G_out"), 16 * 128)

    _check_totals(log)
    _simulate(log)
    return nc


_NC_CACHE = {}


def _get_nc():
    if "nc" not in _NC_CACHE:
        _NC_CACHE["nc"] = build_nc()
    return _NC_CACHE["nc"]


def kernel(L0_attn_x, L0_attn_nu_log, L0_attn_theta_log, L0_attn_B_re, L0_attn_B_im,
           L0_attn_C_re, L0_attn_C_im, L0_attn_D, L0_ffn_w1, L0_ffn_w3, L0_ffn_w2):
    x = np.ascontiguousarray(np.asarray(L0_attn_x, dtype=np.float32))
    assert x.shape == (4, 8192, 2048)

    shared = {
        "breT": np.ascontiguousarray(np.asarray(L0_attn_B_re, np.float32).T),
        "bimT": np.ascontiguousarray(np.asarray(L0_attn_B_im, np.float32).T),
        "creT": np.ascontiguousarray(np.asarray(L0_attn_C_re, np.float32).T),
        "cimT": np.ascontiguousarray(np.asarray(L0_attn_C_im, np.float32).T),
        "nu_in": np.ascontiguousarray(
            np.asarray(L0_attn_nu_log, np.float32).reshape(1, NST)),
        "th_in": np.ascontiguousarray(
            np.asarray(L0_attn_theta_log, np.float32).reshape(1, NST)),
        "dvec": np.ascontiguousarray(np.asarray(L0_attn_D, np.float32).reshape(1, D)),
        "w1T": np.ascontiguousarray(np.asarray(L0_ffn_w1, np.float32).T),
        "w3T": np.ascontiguousarray(np.asarray(L0_ffn_w3, np.float32).T),
        "w2T": np.ascontiguousarray(np.asarray(L0_ffn_w2, np.float32).T),
        "ident_in": np.eye(128, dtype=ml_dtypes.bfloat16),
        "consts": np.array([[1.0, 0.0, math.pi / 2, 1e-5, 0, 0, 0, 0]], np.float32),
    }
    zeros = np.zeros((TO, D), np.float32)
    in_maps = []
    for core in range(8):
        b, h = core // 2, core % 2
        xa_c = np.concatenate([zeros, x[b, :TO]], axis=0) if h == 0 else x[b]
        m = dict(shared)
        m["xa"] = np.ascontiguousarray(xa_c)
        in_maps.append(m)

    nc = _get_nc()
    res = run_bass_kernel_spmd(nc, in_maps, core_ids=list(range(8)))
    outp = np.empty((4, 8192, 2048), np.float32)
    for core in range(8):
        b, h = core // 2, core % 2
        outp[b, h * TO:(h + 1) * TO] = res.results[core]["out"]
    return outp


if __name__ == "__main__":
    nc = build_nc()
    print("build OK")


# version 08
# speedup vs baseline: 1.3915x; 1.1795x over previous
import math
from contextlib import ExitStack

import numpy as np
import ml_dtypes

import concourse.bass as bass
import concourse.mybir as mybir
from concourse.bass_utils import run_bass_kernel_spmd

F32 = mybir.dt.float32
BF16 = mybir.dt.bfloat16
AF = mybir.ActivationFunctionType
ALU = mybir.AluOpType
AXX = mybir.AxisListType.X

D = 2048
DK = 16
NST = 256
HID = 5461
HCN = 43
TA = 8192
TO = 4096
TT = 512
NTA = 16
NTB = 8
BPT = 4
PI = math.pi
TWO_PI_C1 = 6.28125
TWO_PI_C2 = 2.0 * math.pi - 6.28125
MAGIC = 12582912.0

C_ONE, C_ZERO, C_HALFPI, C_EPS = 0, 1, 2, 3
DEBUG = False

FLATC = (D * HID) // 128
SCH = 2048
NSC = 43
SC_TAIL = FLATC - (NSC - 1) * SCH
SSCH = 768
NSSC = (FLATC + SSCH - 1) // SSCH
SS_TAIL = FLATC - (NSSC - 1) * SSCH
W13_SUB9 = [(640, 640)] * 8 + [(384, 341)]
W2_SUB = [(768, 768), (640, 640), (640, 640)]
W13_SUB = [(1408, 1408), (1408, 1408), (1408, 1408), (1280, 1237)]
NQ_W13 = DK * len(W13_SUB)
NQ_ALL = 2 * NQ_W13 + HCN
NLOAD_S = 3 * NSC
G_TWR_TOTAL = 2 * NQ_W13 + HCN


class Sem:
    def __init__(self, h, name):
        self.h = h
        self.name = name
        self.total = 0


class Eng:
    def __init__(self, e, name, log):
        self.e = e
        self.name = name
        self.log = log

    def wait(self, sem, val):
        if val <= 0:
            return
        self.e.wait_ge(sem.h, val)
        self.log.append((self.name, "wait", sem, val))

    def put(self, inst, sem=None, n=1):
        if sem is not None:
            inst.then_inc(sem.h, n)
            sem.total += n
            self.log.append((self.name, "inc", sem, n))
        return inst


def _simulate(log):
    streams = {}
    for ev in log:
        streams.setdefault(ev[0], []).append(ev)
    pos = {k: 0 for k in streams}
    counts = {}
    progressed = True
    while progressed:
        progressed = False
        for name, evs in streams.items():
            while pos[name] < len(evs):
                _, kind, sem, val = evs[pos[name]]
                if kind == "wait" and counts.get(id(sem), 0) < val:
                    break
                if kind == "inc":
                    counts[id(sem)] = counts.get(id(sem), 0) + val
                pos[name] += 1
                progressed = True
    stuck = {k: pos[k] for k in streams if pos[k] < len(streams[k])}
    if stuck:
        det = []
        for k in stuck:
            ev = streams[k][pos[k]]
            det.append(f"{k}@{pos[k]}/{len(streams[k])} waits {ev[2].name}>={ev[3]} "
                       f"(have {counts.get(id(ev[2]), 0)}, total {ev[2].total})")
        raise RuntimeError("static deadlock: " + "; ".join(det))


def _check_totals(log):
    bad = []
    for ev in log:
        if ev[1] == "wait" and ev[3] > ev[2].total:
            bad.append(f"{ev[0]} waits {ev[2].name}>={ev[3]} total={ev[2].total}")
    if bad:
        raise RuntimeError("impossible waits: " + "; ".join(bad[:10]))


def _prologue_steps(t):
    S = []

    def s(fn):
        S.append(("s", fn))

    def v(fn):
        S.append(("v", fn))

    s(lambda e: e.activation(out=t["pt1"][:], in_=t["nusb"][:], func=AF.Abs))
    s(lambda e: e.activation(out=t["pt2"][:], in_=t["pt1"][:], func=AF.Exp, scale=-1.0))
    s(lambda e: e.activation(out=t["pt2"][:], in_=t["pt2"][:], func=AF.Ln,
                             bias=t["cbc"][:, C_ONE:C_ONE + 1]))
    v(lambda e: e.tensor_scalar_max(t["pt3"][:], t["nusb"][:], 0.0))
    v(lambda e: e.tensor_add(t["pt4"][:], t["pt3"][:], t["pt2"][:]))
    s(lambda e: e.activation(out=t["pt1"][:], in_=t["pt4"][:], func=AF.Exp))
    s(lambda e: e.activation(out=t["pr_mag"][:], in_=t["pt1"][:], func=AF.Exp, scale=-1.0))
    v(lambda e: e.tensor_mul(t["pt3"][:], t["pr_mag"][:], t["pr_mag"][:]))
    v(lambda e: e.tensor_scalar(out=t["pt3"][:], in0=t["pt3"][:], scalar1=-1.0,
                                scalar2=1.0, op0=ALU.mult, op1=ALU.add))
    s(lambda e: e.activation(out=t["pt1"][:], in_=t["pt3"][:], func=AF.Ln))
    s(lambda e: e.activation(out=t["pr_gam"][:], in_=t["pt1"][:], func=AF.Exp, scale=0.5))
    s(lambda e: e.activation(out=t["pt1"][:], in_=t["thsb"][:], func=AF.Exp))
    v(lambda e: e.tensor_scalar_mul(t["pt2"][:], t["pt1"][:], 1.0 / (2 * PI)))
    v(lambda e: e.tensor_scalar(out=t["pt2"][:], in0=t["pt2"][:], scalar1=MAGIC,
                                scalar2=MAGIC, op0=ALU.add, op1=ALU.subtract))
    v(lambda e: e.scalar_tensor_tensor(out=t["pt4"][:], in0=t["pt2"][:],
                                       scalar=-TWO_PI_C1, in1=t["pt1"][:],
                                       op0=ALU.mult, op1=ALU.add))
    v(lambda e: e.scalar_tensor_tensor(out=t["pt3"][:], in0=t["pt2"][:],
                                       scalar=-TWO_PI_C2, in1=t["pt4"][:],
                                       op0=ALU.mult, op1=ALU.add))
    s(lambda e: e.activation(out=t["pr_sin"][:], in_=t["pt3"][:], func=AF.Sin))
    s(lambda e: e.activation(out=t["pt2"][:], in_=t["pt3"][:], func=AF.Abs))
    s(lambda e: e.activation(out=t["pr_cos"][:], in_=t["pt2"][:], func=AF.Sin,
                             scale=-1.0, bias=t["cbc"][:, C_HALFPI:C_HALFPI + 1]))
    return S


def build_nc():
    nc = bass.Bass("TRN2", target_bir_lowering=False, debug=False)

    xa = nc.dram_tensor("xa", [TA, D], F32, kind="ExternalInput").ap()
    breT = nc.dram_tensor("breT", [D, NST], F32, kind="ExternalInput").ap()
    bimT = nc.dram_tensor("bimT", [D, NST], F32, kind="ExternalInput").ap()
    creT = nc.dram_tensor("creT", [NST, D], F32, kind="ExternalInput").ap()
    cimT = nc.dram_tensor("cimT", [NST, D], F32, kind="ExternalInput").ap()
    nu_in = nc.dram_tensor("nu_in", [1, NST], F32, kind="ExternalInput").ap()
    th_in = nc.dram_tensor("th_in", [1, NST], F32, kind="ExternalInput").ap()
    dvec = nc.dram_tensor("dvec", [1, D], F32, kind="ExternalInput").ap()
    w1T = nc.dram_tensor("w1T", [D, HID], F32, kind="ExternalInput").ap()
    w3T = nc.dram_tensor("w3T", [D, HID], F32, kind="ExternalInput").ap()
    w2T = nc.dram_tensor("w2T", [HID, D], F32, kind="ExternalInput").ap()
    ident_in = nc.dram_tensor("ident_in", [128, 128], BF16, kind="ExternalInput").ap()
    consts = nc.dram_tensor("consts", [1, 8], F32, kind="ExternalInput").ap()
    out = nc.dram_tensor("out", [TO, D], F32, kind="ExternalOutput").ap()
    w1qs = nc.dram_tensor("w1qs", [HCN, 128, D], BF16, kind="Internal").ap()
    w3qs = nc.dram_tensor("w3qs", [HCN, 128, D], BF16, kind="Internal").ap()
    w2qs = nc.dram_tensor("w2qs", [HCN, 128, D], BF16, kind="Internal").ap()
    if DEBUG:
        dbg_small = nc.dram_tensor("dbg_small", [128, 32], F32, kind="ExternalOutput").ap()
        dbg_stat = nc.dram_tensor("dbg_stat", [128, 128], F32, kind="ExternalOutput").ap()
        dbg_h = nc.dram_tensor("dbg_h", [128, TO], F32, kind="ExternalOutput").ap()
    tstat = nc.dram_tensor("tstat", [8, 128], F32, kind="Internal").ap()

    w1flat = w1T.rearrange("a b -> (a b)").rearrange("(p c) -> p c", p=128)
    w3flat = w3T.rearrange("a b -> (a b)").rearrange("(p c) -> p c", p=128)
    w2flat = w2T.rearrange("a b -> (a b)").rearrange("(p c) -> p c", p=128)

    log = []
    with ExitStack() as ctx:
        sb = lambda n, s, d=F32: ctx.enter_context(nc.sbuf_tensor(n, s, d))
        ps_ = lambda n: ctx.enter_context(nc.psum_tensor(n, [128, 512], F32))
        mksem = lambda n: Sem(ctx.enter_context(nc.semaphore(n)), n)

        Hre = [sb(f"hre{i}", [128, TO], BF16) for i in range(2)]
        Him = [sb(f"him{i}", [128, TO], BF16) for i in range(2)]
        lnF = sb("lnF", [128, DK * TT], BF16)
        psq = sb("psq", [128, D])
        dbc = sb("dbc", [128, D])
        cT = [sb("creb", [128, 2 * D], BF16), sb("cimb", [128, 2 * D], BF16)]
        ident = sb("ident", [128, 128], BF16)
        cbc = sb("cbc", [128, 8])
        stat_rstd = sb("st_rstd", [128, 64])
        stat_nb = sb("st_nb", [128, 64])
        stat_ssum = sb("st_ssum", [128, 64])
        stat_sqs = sb("st_sqs", [128, 64])
        stat2_rstd = sb("s2_rstd", [128, 32])
        stat2_nb = sb("s2_nb", [128, 32])
        stat2_ssum = sb("s2_ssum", [128, 32])
        stat2_sqs = sb("s2_sqs", [128, 32])
        nusb = sb("nusb", [128, 2])
        thsb = sb("thsb", [128, 2])
        pr_mag = sb("pr_mag", [128, 2])
        pr_gam = sb("pr_gam", [128, 2])
        pr_cos = sb("pr_cos", [128, 2])
        pr_sin = sb("pr_sin", [128, 2])
        pt1 = sb("pt1", [128, 2])
        pt2 = sb("pt2", [128, 2])
        pt3 = sb("pt3", [128, 2])
        pt4 = sb("pt4", [128, 2])
        w2k_c = sb("w2k_c", [128, 2])
        w2k_s = sb("w2k_s", [128, 2])
        sc_s1 = sb("sc_s1", [128, 1])
        sc_s3 = sb("sc_s3", [128, 1])
        sc_s2 = sb("sc_s2", [128, 1])
        sc_thr = [sb(f"sc_thr{i}", [128, 1]) for i in range(3)]
        redarr = sb("redarr", [128, 3 * 128])
        redtot = sb("redtot", [128, 1])
        rrow = sb("rrow", [1, 128])
        rsum = sb("rsum", [1, 1])
        hreL = sb("hreL", [128, 2])
        himL = sb("himL", [128, 2])
        s1i = sb("s1i", [128, 2])
        s2i = sb("s2i", [128, 2])
        gt1 = sb("gt1", [128, 2])
        gt2 = sb("gt2", [128, 2])
        mean_t = sb("mean_t", [128, 4])
        meann_t = sb("meann_t", [128, 4])
        msq_t = sb("msq_t", [128, 4])
        var_t = sb("var_t", [128, 4])

        a_ctx = ExitStack()
        asb = lambda n, s, d=F32: a_ctx.enter_context(nc.sbuf_tensor(n, s, d))
        bT = [asb("breb", [128, DK * NST], BF16), asb("bimb", [128, DK * NST], BF16)]
        xar = [asb(f"xar{i}", [128, D]) for i in range(4)]
        ln1tok = asb("ln1tok", [128, BPT * D], BF16)
        scr = [asb(f"scr{i}", [128, TT]) for i in range(4)]
        s1t = asb("s1t", [128, TT])
        s2t = asb("s2t", [128, TT])
        cs_t = asb("cs_t", [128, 2 * TT])
        sn_t = asb("sn_t", [128, 2 * TT])
        csp_t = asb("csp_t", [128, 2 * TT])
        snp_t = asb("snp_t", [128, 2 * TT])
        tin = [asb(f"tin{i}", [128, 768]) for i in range(4)]
        tins = [asb(f"tins{i}", [128, SSCH]) for i in range(4)]
        tq = [asb("tq0", [128, 768], BF16), asb("tq1", [128, 768], BF16)]
        tl = asb("tl", [128, 768], BF16)
        a_ctx.close()

        x1 = [sb(f"x1_{i}", [128, D]) for i in range(BPT)]
        gbuf = sb("gbuf", [128, HCN * TT], BF16)
        ln2tok = [sb(f"ln2tok{i}", [128, D], BF16) for i in range(2)]
        w1p = [sb(f"w1p{i}", [128, D], BF16) for i in range(2)]
        w3p = [sb(f"w3p{i}", [128, D], BF16) for i in range(2)]
        w2c = [sb(f"w2c{i}", [128, 2 * TT], BF16) for i in range(4)]
        xq = [sb(f"xq{i}", [128, TT]) for i in range(2)]
        lnbq = [sb(f"lnbq{i}", [128, TT]) for i in range(2)]
        silb = [sb(f"silb{i}", [128, TT]) for i in range(2)]
        ost = [sb(f"ost{i}", [128, TT]) for i in range(2)]

        pbu = [ps_(f"pbu{i}") for i in range(4)]
        ptr = [ps_(f"ptr{i}") for i in range(2)]
        py = [ps_(f"py{i}") for i in range(2)]

        sems = {}
        for n in ["A_xa", "A_sq", "A_var", "A_rstd", "A_nb", "A_ln1tok", "A_trps",
                  "A_trevac", "A_bu", "A_bufree", "A_h", "L_pro", "L_cast", "TV",
                  "TS", "G_tld", "V_red", "V_q", "G_twr", "G_st", "B_x", "B_lnb",
                  "B_base", "B_yps", "B_yfree", "B_sq2", "B_var2", "B_rstd2",
                  "B_nb2", "B_ln2tok", "B_p1", "B_p3", "B_p2", "B_h1d", "B_h3d",
                  "B_sil", "B_g", "B_outsb", "B_w2u", "G_out", "V_zw2", "L_ts", "S_red", "V_qm",
                  "V_thr0", "V_thr1", "V_thr2"]:
            sems[n] = mksem(n)
        g = lambda n: sems[n]
        V_thr = [g("V_thr0"), g("V_thr1"), g("V_thr2")]

        ptiles = {"nusb": nusb, "thsb": thsb, "cbc": cbc, "pt1": pt1, "pt2": pt2,
                  "pt3": pt3, "pt4": pt4, "pr_mag": pr_mag, "pr_gam": pr_gam,
                  "pr_cos": pr_cos, "pr_sin": pr_sin}
        PSTEPS = _prologue_steps(ptiles)
        NPS_S = sum(1 for e, _ in PSTEPS if e == "s")
        NPS_V = sum(1 for e, _ in PSTEPS if e == "v")

        block = ctx.enter_context(nc.Block())

        @block.sync
        def _(sync):
            E = Eng(sync, "sync", log)
            E.put(sync.dma_start(out=cbc[:], in_=consts.broadcast_to((128, 8))),
                  g("L_pro"), 16)
            E.put(sync.dma_start(out=nusb[:],
                                 in_=nu_in.rearrange("o (h p) -> p (o h)", p=128),
                                 allow_slow_non_contiguous=True),
                  g("L_pro"), 16)
            E.put(sync.dma_start(out=thsb[:],
                                 in_=th_in.rearrange("o (h p) -> p (o h)", p=128),
                                 allow_slow_non_contiguous=True),
                  g("L_pro"), 16)
            E.put(sync.dma_start(out=ident[:], in_=ident_in), g("L_pro"), 16)
            E.put(sync.dma_start(out=dbc[:], in_=dvec.broadcast_to((128, D))),
                  g("L_pro"), 16)
            for gg in range(NTA * BPT):
                if gg >= 4:
                    E.wait(g("A_ln1tok"), gg - 3)
                E.put(sync.dma_start(out=xar[gg % 4][:],
                                     in_=xa[gg * 128:(gg + 1) * 128, :]),
                      g("A_xa"), 16)
            E.wait(g("G_twr"), 16 * G_TWR_TOTAL)
            E.wait(g("A_h"), NTA)
            for j in range(NTB):
                for hc in range(HCN):
                    gi = HCN * j + hc
                    if gi >= 2:
                        E.wait(g("B_h1d"), gi - 1)
                    E.put(sync.dma_start(out=w1p[hc % 2][:], in_=w1qs[hc, :, :]),
                          g("B_p1"), 16)
                    if gi >= 2:
                        E.wait(g("B_h3d"), gi - 1)
                    E.put(sync.dma_start(out=w3p[hc % 2][:], in_=w3qs[hc, :, :]),
                          g("B_p3"), 16)
                for pair in range(2):
                    for hc in range(HCN):
                        widx = (2 * j + pair) * HCN + hc
                        if widx >= 4:
                            E.wait(g("B_w2u"), widx - 3)
                        E.put(sync.dma_start(
                            out=w2c[widx % 4][:],
                            in_=w2qs[hc, :, pair * 1024:(pair + 1) * 1024]),
                            g("B_p2"), 16)

        @block.scalar
        def _(scalar):
            E = Eng(scalar, "scalar", log)
            E.wait(g("L_pro"), 16 * 3)
            nv = 0
            ns = 0
            for eng, fn in PSTEPS:
                if eng == "s":
                    E.wait(g("TV"), nv)
                    E.put(fn(scalar), g("TS"), 1)
                    ns += 1
                    E.wait(g("TS"), ns)
                else:
                    nv += 1

            s_meta = []
            for widx in (0, 1, 2):
                for ci in range(NSSC):
                    s_meta.append((widx, ci, SSCH if ci < NSSC - 1 else SS_TAIL))
            s_i = [0]
            flats = [w1flat, w3flat, w2flat]

            def s_issue(i):
                if i >= len(s_meta):
                    return
                widx, ci, cw = s_meta[i]
                E.put(scalar.dma_start(out=tins[i % 4][:, 0:cw],
                                       in_=flats[widx][:, ci * SSCH:ci * SSCH + cw]),
                      g("L_ts"), 16)

            for i in range(4):
                s_issue(i)

            def abs_unit():
                i = s_i[0]
                if i >= len(s_meta):
                    return
                widx, ci, cw = s_meta[i]
                E.wait(g("L_ts"), 16 * (i + 1))
                E.put(scalar.activation(out=psq[:, 0:cw], in_=tins[i % 4][:, 0:cw],
                                        func=AF.Abs,
                                        accum_out=redarr[:, widx * 128 + ci:
                                                         widx * 128 + ci + 1]),
                      g("S_red"))
                s_issue(i + 4)
                s_i[0] += 1

            tsn = [NPS_S]
            for k in range(NTA):
                for b in range(BPT):
                    gg = BPT * k + b
                    c1 = slice(gg % 4, gg % 4 + 1)
                    E.wait(g("A_xa"), 16 * (gg + 1))
                    E.put(scalar.activation(out=psq[:], in_=xar[gg % 4][:],
                                            func=AF.Square,
                                            accum_out=stat_sqs[:, gg:gg + 1]),
                          g("A_sq"))
                    for _ in range(8):
                        abs_unit()
                    E.wait(g("A_var"), gg + 1)
                    E.put(scalar.activation(out=var_t[:, c1], in_=var_t[:, c1],
                                            func=AF.Ln,
                                            bias=cbc[:, C_EPS:C_EPS + 1]), g("TS"))
                    tsn[0] += 1
                    E.wait(g("TS"), tsn[0])
                    E.put(scalar.activation(out=stat_rstd[:, gg:gg + 1],
                                            in_=var_t[:, c1], func=AF.Exp,
                                            scale=-0.5), g("A_rstd"))
                    E.wait(g("A_nb"), gg + 1)
                    E.put(scalar.activation(out=ln1tok[:, b * D:(b + 1) * D],
                                            in_=xar[gg % 4][:], func=AF.Identity,
                                            scale=stat_rstd[:, gg:gg + 1],
                                            bias=stat_nb[:, gg:gg + 1]),
                          g("A_ln1tok"))


            def xload(j, p):
                qg = 16 * j + p
                b, dq = p % 4, p // 4
                if qg == 0:
                    E.wait(g("A_h"), NTA)
                    E.wait(g("V_q"), 2 * NQ_ALL)
                if qg >= 2:
                    E.wait(g("B_base"), qg - 1)
                r0 = TO + j * TT + b * 128
                E.put(scalar.dma_start(out=xq[qg % 2][:],
                                       in_=xa[r0:r0 + 128, dq * TT:(dq + 1) * TT]),
                      g("B_x"), 16)

            def lnb(j, p):
                qg = 16 * j + p
                b = p % 4
                gg = 32 + 4 * j + b
                E.wait(g("B_x"), 16 * (qg + 1))
                if qg >= 2:
                    E.wait(g("B_base"), qg - 1)
                E.put(scalar.activation(out=lnbq[qg % 2][:], in_=xq[qg % 2][:],
                                        func=AF.Identity,
                                        scale=stat_rstd[:, gg:gg + 1],
                                        bias=stat_nb[:, gg:gg + 1]), g("B_lnb"))

            def sq2_chain(j):
                for b in range(BPT):
                    gg = 4 * j + b
                    E.wait(g("B_yfree"), 16 * j + 4 * b + 4)
                    E.put(scalar.activation(out=psq[:], in_=x1[b][:], func=AF.Square,
                                            accum_out=stat2_sqs[:, gg:gg + 1]),
                          g("B_sq2") if b == 3 else None)
                E.wait(g("B_var2"), j + 1)
                E.put(scalar.activation(out=var_t[:], in_=var_t[:], func=AF.Ln,
                                        bias=cbc[:, C_EPS:C_EPS + 1]), g("TS"))
                E.wait(g("TS"), NPS_S + NTA * BPT + j + 1)
                E.put(scalar.activation(out=stat2_rstd[:, 4 * j:4 * j + 4],
                                        in_=var_t[:], func=AF.Exp, scale=-0.5),
                      g("B_rstd2"))
                E.wait(g("B_nb2"), j + 1)
                if j == 0:
                    E.wait(g("A_h"), NTA)
                for b in range(BPT):
                    gg = 4 * j + b
                    prev_groups = 16 * j + 4 * b - 8
                    if prev_groups >= 0:
                        E.wait(g("A_trps"), 256 + prev_groups + 4)
                    E.put(scalar.activation(out=ln2tok[b % 2][:], in_=x1[b][:],
                                            func=AF.Identity,
                                            scale=stat2_rstd[:, gg:gg + 1],
                                            bias=stat2_nb[:, gg:gg + 1]),
                          g("B_ln2tok"))

            def silu(j, hc):
                gi = HCN * j + hc
                E.wait(g("B_h1d"), gi + 1)
                if gi >= 2:
                    E.wait(g("B_g"), gi - 1)
                bank = (hc % 2) * 2
                E.put(scalar.activation(out=silb[hc % 2][:], in_=pbu[bank][:],
                                        func=AF.Silu, scale=sc_s1[:, 0:1]),
                      g("B_sil"))

            for p in range(16):
                xload(0, p)
                lnb(0, p)
            sq2_chain(0)
            for j in range(NTB):
                for hc in range(HCN):
                    silu(j, hc)
                if j + 1 < NTB:
                    for p in range(16):
                        xload(j + 1, p)
                        lnb(j + 1, p)
                    sq2_chain(j + 1)

        @block.vector
        def _(vector):
            E = Eng(vector, "vector", log)
            tvn = [0]

            def vput(inst):
                E.put(inst, g("TV"), 1)
                tvn[0] += 1

            def vbar():
                E.wait(g("TV"), tvn[0])

            E.wait(g("L_pro"), 16 * 5)
            ns = 0
            for eng, fn in PSTEPS:
                if eng == "v":
                    E.wait(g("TS"), ns)
                    vput(fn(vector))
                    vbar()
                else:
                    ns += 1
            E.wait(g("TS"), NPS_S)

            for nh in range(2):
                c0 = nh * TT
                nhs = slice(nh, nh + 1)
                vput(vector.tensor_copy(cs_t[:, c0:c0 + 1], cbc[:, C_ONE:C_ONE + 1]))
                vput(vector.tensor_copy(sn_t[:, c0:c0 + 1], cbc[:, C_ZERO:C_ZERO + 1]))
                vput(vector.tensor_copy(w2k_c[:, nhs], pr_cos[:, nhs]))
                vput(vector.tensor_copy(w2k_s[:, nhs], pr_sin[:, nhs]))
                vbar()
                w = 1
                while w < TT:
                    cc = w2k_c[:, nhs]
                    ss = w2k_s[:, nhs]
                    vput(vector.tensor_scalar(out=scr[0][:, 0:w], in0=cs_t[:, c0:c0 + w],
                                              scalar1=cc, scalar2=None, op0=ALU.mult))
                    vput(vector.tensor_scalar(out=scr[1][:, 0:w], in0=sn_t[:, c0:c0 + w],
                                              scalar1=ss, scalar2=None, op0=ALU.mult))
                    vput(vector.tensor_scalar(out=scr[2][:, 0:w], in0=sn_t[:, c0:c0 + w],
                                              scalar1=cc, scalar2=None, op0=ALU.mult))
                    vput(vector.tensor_scalar(out=scr[3][:, 0:w], in0=cs_t[:, c0:c0 + w],
                                              scalar1=ss, scalar2=None, op0=ALU.mult))
                    vbar()
                    vput(vector.tensor_sub(cs_t[:, c0 + w:c0 + 2 * w],
                                           scr[0][:, 0:w], scr[1][:, 0:w]))
                    vput(vector.tensor_add(sn_t[:, c0 + w:c0 + 2 * w],
                                           scr[2][:, 0:w], scr[3][:, 0:w]))
                    vput(vector.tensor_mul(pt1[:, nhs], cc, cc))
                    vput(vector.tensor_mul(pt2[:, nhs], ss, ss))
                    vput(vector.tensor_mul(pt4[:, nhs], cc, ss))
                    vbar()
                    vput(vector.tensor_sub(w2k_c[:, nhs], pt1[:, nhs], pt2[:, nhs]))
                    vput(vector.tensor_scalar_mul(w2k_s[:, nhs], pt4[:, nhs], 2.0))
                    vbar()
                    w *= 2
                vput(vector.tensor_scalar(out=csp_t[:, c0:c0 + TT],
                                          in0=cs_t[:, c0:c0 + TT],
                                          scalar1=pr_gam[:, nhs], scalar2=None,
                                          op0=ALU.mult))
                vput(vector.tensor_scalar(out=snp_t[:, c0:c0 + TT],
                                          in0=sn_t[:, c0:c0 + TT],
                                          scalar1=pr_gam[:, nhs], scalar2=None,
                                          op0=ALU.mult))
                vbar()

            tern_ops = []
            fins = {}
            thrs = {}
            tld_n = [0]
            qn_n = [0]
            for widx in (0, 1, 2):

                def fin(widx=widx):
                    E.wait(g("S_red"), NSSC * (widx + 1))
                    E.put(vector.reduce_sum(redtot[:],
                                            redarr[:, widx * 128:widx * 128 + NSSC],
                                            axis=AXX), V_thr[widx])
                    E.wait(g("G_st"), 16 * (4 * widx + 2))
                    E.put(vector.reduce_sum(rsum[:], rrow[:], axis=AXX), V_thr[widx])
                    E.wait(V_thr[widx], 2)
                    E.put(vector.tensor_scalar_mul(rsum[:], rsum[:],
                                                   1.0 / (D * HID)), V_thr[widx])
                fins[widx] = fin

                def thr(widx=widx):
                    st = [sc_s1, sc_s3, sc_s2][widx]
                    E.wait(g("G_st"), 16 * (4 * widx + 4))
                    E.put(vector.tensor_scalar_mul(sc_thr[widx][:], st[:], 0.5),
                          V_thr[widx])
                thrs[widx] = thr

            def add_q(widx, pw, valid, prezero):
                def f():
                    q = qn_n[0]
                    tqs = tq[q % 2]
                    if q >= 2:
                        E.wait(g("G_twr"), 16 * (q - 1))
                    E.wait(g("G_tld"), 16 * (q + 1))
                    E.wait(V_thr[widx], 4)
                    thr_ap = sc_thr[widx][:, 0:1]
                    E.put(vector.tensor_scalar(out=tl[:, 0:valid],
                                               in0=tin[q % 4][:, 0:valid],
                                               scalar1=-1.0, scalar2=thr_ap,
                                               op0=ALU.mult, op1=ALU.is_gt))
                    if pw > valid:
                        E.put(vector.scalar_tensor_tensor(
                            out=tqs[:, 0:valid], in0=tin[q % 4][:, 0:valid],
                            scalar=thr_ap, in1=tl[:, 0:valid],
                            op0=ALU.is_gt, op1=ALU.subtract), g("V_q"))
                        E.put(vector.tensor_copy(
                            tqs[:, valid:pw],
                            cbc[:, C_ZERO:C_ZERO + 1].broadcast_to((128, pw - valid))),
                            g("V_q"))
                    else:
                        E.put(vector.scalar_tensor_tensor(
                            out=tqs[:, 0:valid], in0=tin[q % 4][:, 0:valid],
                            scalar=thr_ap, in1=tl[:, 0:valid],
                            op0=ALU.is_gt, op1=ALU.subtract), g("V_q"), 2)
                    if prezero:
                        E.put(vector.tensor_copy(
                            tin[q % 4][64:128, 0:768],
                            cbc[64:128, C_ZERO:C_ZERO + 1].broadcast_to((64, 768))),
                            g("V_zw2"))
                    tld_n[0] += 1
                    qn_n[0] += 1
                tern_ops.append(f)

            NQT = 2 * DK * len(W13_SUB9) + HCN * len(W2_SUB)
            qcount = [0]

            def addq2(widx, pw, valid):
                add_q(widx, pw, valid, qcount[0] in (NQT - 7, NQT - 6, NQT - 5))
                qcount[0] += 1

            for widx in (0, 1):
                tern_ops.append(fins[widx])
                tern_ops.append(thrs[widx])
                for dk in range(DK):
                    for (pw, valid) in W13_SUB9:
                        addq2(widx, pw, valid)
            tern_ops.append(fins[2])
            tern_ops.append(thrs[2])
            for hc in range(HCN):
                for (pw, valid) in W2_SUB:
                    addq2(2, pw, valid)

            n_tern = len(tern_ops)
            tern_sched = {}
            acc = 0
            for k in range(4, 16):
                cnt = (n_tern - acc + (15 - k)) // (16 - k)
                tern_sched[k] = (acc, acc + cnt)
                acc += cnt
            ti = [0]

            for k in range(NTA):
                for b in range(BPT):
                    gg = BPT * k + b
                    c1 = slice(gg % 4, gg % 4 + 1)
                    E.wait(g("A_xa"), 16 * (gg + 1))
                    E.put(vector.reduce_sum(stat_ssum[:, gg:gg + 1], xar[gg % 4][:],
                                            axis=AXX))
                    E.wait(g("A_sq"), gg + 1)
                    if gg >= 4:
                        E.wait(g("A_rstd"), gg - 3)
                    vput(vector.tensor_scalar_mul(mean_t[:, c1],
                                                  stat_ssum[:, gg:gg + 1], 1.0 / D))
                    vput(vector.tensor_scalar_mul(meann_t[:, c1],
                                                  stat_ssum[:, gg:gg + 1], -1.0 / D))
                    vbar()
                    vput(vector.tensor_mul(msq_t[:, c1], mean_t[:, c1], mean_t[:, c1]))
                    vbar()
                    E.put(vector.scalar_tensor_tensor(out=var_t[:, c1],
                                                      in0=stat_sqs[:, gg:gg + 1],
                                                      scalar=1.0 / D, in1=msq_t[:, c1],
                                                      op0=ALU.mult, op1=ALU.subtract),
                          g("A_var"))
                    E.wait(g("A_rstd"), gg + 1)
                    E.put(vector.tensor_mul(stat_nb[:, gg:gg + 1], meann_t[:, c1],
                                            stat_rstd[:, gg:gg + 1]), g("A_nb"))
                for dk in range(DK):
                    c = DK * k + dk
                    E.wait(g("A_trps"), c + 1)
                    E.put(vector.tensor_copy(lnF[:, dk * TT:(dk + 1) * TT],
                                             ptr[dk % 2][:]), g("A_trevac"))
                if k in tern_sched:
                    lo, hi = tern_sched[k]
                    while ti[0] < min(hi, n_tern):
                        tern_ops[ti[0]]()
                        ti[0] += 1
                E.wait(g("A_bu"), k + 1)
                for nh in range(2):
                    c0 = nh * TT
                    nhs = slice(nh, nh + 1)
                    psre = pbu[2 * nh + 0]
                    psim = pbu[2 * nh + 1]
                    E.put(vector.tensor_mul(scr[0][:], csp_t[:, c0:c0 + TT], psre[:]))
                    E.put(vector.tensor_mul(scr[1][:], snp_t[:, c0:c0 + TT], psim[:]))
                    E.put(vector.tensor_mul(scr[2][:], snp_t[:, c0:c0 + TT], psre[:]))
                    E.put(vector.tensor_mul(scr[3][:], csp_t[:, c0:c0 + TT], psim[:]),
                          g("A_bufree") if nh == 1 else None)
                    E.put(vector.tensor_add(scr[0][:], scr[0][:], scr[1][:]))
                    E.put(vector.tensor_sub(scr[2][:], scr[2][:], scr[3][:]))
                    if k == 0:
                        init1 = 0.0
                        init2 = 0.0
                    else:
                        vput(vector.tensor_scalar(out=gt1[:, nhs], in0=hreL[:, nhs],
                                                  scalar1=pr_cos[:, nhs], scalar2=None,
                                                  op0=ALU.mult))
                        vput(vector.tensor_scalar(out=gt2[:, nhs], in0=himL[:, nhs],
                                                  scalar1=pr_sin[:, nhs], scalar2=None,
                                                  op0=ALU.mult))
                        vbar()
                        vput(vector.tensor_add(s1i[:, nhs], gt1[:, nhs], gt2[:, nhs]))
                        vput(vector.tensor_scalar(out=gt1[:, nhs], in0=himL[:, nhs],
                                                  scalar1=pr_cos[:, nhs], scalar2=None,
                                                  op0=ALU.mult))
                        vput(vector.tensor_scalar(out=gt2[:, nhs], in0=hreL[:, nhs],
                                                  scalar1=pr_sin[:, nhs], scalar2=None,
                                                  op0=ALU.mult))
                        vbar()
                        vput(vector.tensor_sub(s2i[:, nhs], gt1[:, nhs], gt2[:, nhs]))
                        vbar()
                        init1 = s1i[:, nhs]
                        init2 = s2i[:, nhs]
                    rbc = pr_mag[:, nhs].broadcast_to((128, TT))
                    E.put(vector.tensor_tensor_scan(out=s1t[:], data0=rbc,
                                                    data1=scr[0][:], initial=init1,
                                                    op0=ALU.mult, op1=ALU.add))
                    E.put(vector.tensor_tensor_scan(out=s2t[:], data0=rbc,
                                                    data1=scr[2][:], initial=init2,
                                                    op0=ALU.mult, op1=ALU.add))
                    t0 = (k % 8) * TT
                    dre_s = Hre[nh][:, t0:t0 + TT]
                    dim_s = Him[nh][:, t0:t0 + TT]
                    E.put(vector.tensor_mul(scr[0][:], cs_t[:, c0:c0 + TT], s1t[:]))
                    E.put(vector.tensor_mul(scr[1][:], sn_t[:, c0:c0 + TT], s2t[:]))
                    E.put(vector.tensor_add(dre_s, scr[0][:], scr[1][:]))
                    vput(vector.tensor_add(hreL[:, nhs], scr[0][:, TT - 1:TT],
                                           scr[1][:, TT - 1:TT]))
                    E.put(vector.tensor_mul(scr[2][:], cs_t[:, c0:c0 + TT], s2t[:]))
                    E.put(vector.tensor_mul(scr[3][:], sn_t[:, c0:c0 + TT], s1t[:]))
                    E.put(vector.tensor_sub(dim_s, scr[2][:], scr[3][:]),
                          g("A_h") if nh == 1 else None)
                    vput(vector.tensor_sub(himL[:, nhs], scr[2][:, TT - 1:TT],
                                           scr[3][:, TT - 1:TT]))
                    vbar()
            while ti[0] < n_tern:
                tern_ops[ti[0]]()
                ti[0] += 1

            def base_q(j, p):
                qg = 16 * j + p
                b, dq = p % 4, p // 4
                E.wait(g("B_lnb"), qg + 1)
                sl = slice(dq * TT, (dq + 1) * TT)
                E.put(vector.tensor_mul(x1[b][:, sl], lnbq[qg % 2][:], dbc[:, sl]))
                E.put(vector.tensor_add(x1[b][:, sl], x1[b][:, sl], xq[qg % 2][:]),
                      g("B_base"))

            def yevac(j, qi):
                yc = 16 * j + qi
                b, dq = qi // 4, qi % 4
                E.wait(g("B_yps"), yc + 1)
                sl = slice(dq * TT, (dq + 1) * TT)
                E.put(vector.tensor_add(x1[b][:, sl], x1[b][:, sl], py[qi % 2][:]),
                      g("B_yfree"))
                if dq == 3:
                    gg = 4 * j + b
                    E.put(vector.reduce_sum(stat2_ssum[:, gg:gg + 1], x1[b][:],
                                            axis=AXX))

            def stats2(j):
                E.wait(g("B_sq2"), j + 1)
                if j > 0:
                    E.wait(g("B_rstd2"), j)
                else:
                    E.wait(g("A_rstd"), NTA * BPT)
                c4 = slice(4 * j, 4 * j + 4)
                vput(vector.tensor_scalar_mul(mean_t[:], stat2_ssum[:, c4], 1.0 / D))
                vput(vector.tensor_scalar_mul(meann_t[:], stat2_ssum[:, c4], -1.0 / D))
                vbar()
                vput(vector.tensor_mul(msq_t[:], mean_t[:], mean_t[:]))
                vbar()
                E.put(vector.scalar_tensor_tensor(out=var_t[:], in0=stat2_sqs[:, c4],
                                                  scalar=1.0 / D, in1=msq_t[:],
                                                  op0=ALU.mult, op1=ALU.subtract),
                      g("B_var2"))
                E.wait(g("B_rstd2"), j + 1)
                E.put(vector.tensor_mul(stat2_nb[:, c4], meann_t[:],
                                        stat2_rstd[:, c4]), g("B_nb2"))

            def tr2evac(j):
                for b in range(BPT):
                    for dkg in range(4):
                        tc = 256 + 16 * j + 4 * b + dkg
                        E.wait(g("A_trps"), tc + 1)
                        outap = lnF[:].rearrange("p (dk c) -> p dk c", dk=DK)[
                            :, 4 * dkg:4 * dkg + 4, b * 128:(b + 1) * 128]
                        srcap = ptr[(4 * b + dkg) % 2][:].rearrange(
                            "p (i c) -> p i c", i=4)
                        E.put(vector.tensor_copy(outap, srcap), g("A_trevac"))

            def gop(j, hc):
                gi = HCN * j + hc
                E.wait(g("B_h3d"), gi + 1)
                E.wait(g("B_sil"), gi + 1)
                if j > 0 and hc == 0:
                    E.wait(g("B_w2u"), 86 * j)
                bank = (hc % 2) * 2 + 1
                E.put(vector.scalar_tensor_tensor(out=gbuf[:, hc * TT:(hc + 1) * TT],
                                                  in0=pbu[bank][:],
                                                  scalar=sc_s3[:, 0:1],
                                                  in1=silb[hc % 2][:], op0=ALU.mult,
                                                  op1=ALU.mult), g("B_g"))

            psmap_v = [pbu[0], pbu[1], pbu[2], pbu[3], ptr[0], ptr[1], py[0], py[1]]

            def outevac(j, pair):
                E.wait(g("B_w2u"), 86 * j + 43 * (pair + 1))
                for dq2 in range(2):
                    dq = 2 * pair + dq2
                    for b in range(BPT):
                        oi = 16 * j + 4 * dq + b
                        if oi >= 2:
                            E.wait(g("G_out"), 16 * (oi - 1))
                        sl = slice(dq * TT, (dq + 1) * TT)
                        E.put(vector.scalar_tensor_tensor(
                            out=ost[oi % 2][:], in0=psmap_v[4 * dq2 + b][:],
                            scalar=sc_s2[:, 0:1], in1=x1[b][:, sl],
                            op0=ALU.mult, op1=ALU.add), g("B_outsb"))

            for p in range(16):
                base_q(0, p)
            for qi in range(16):
                yevac(0, qi)
            stats2(0)
            tr2evac(0)
            for j in range(NTB):
                for hc in range(HCN):
                    gop(j, hc)
                for pair in range(2):
                    outevac(j, pair)
                    if j + 1 < NTB:
                        for dq2 in range(2):
                            for b in range(BPT):
                                base_q(j + 1, 4 * (2 * pair + dq2) + b)
                if j + 1 < NTB:
                    for qi in range(16):
                        yevac(j + 1, qi)
                    stats2(j + 1)
                    tr2evac(j + 1)

        @block.tensor
        def _(tensor):
            E = Eng(tensor, "tensor", log)
            E.wait(g("L_pro"), 16 * 4)
            E.wait(g("L_cast"), 16 * 4)

            for k in range(NTA):
                E.wait(g("A_ln1tok"), 4 * k + 4)
                for dk in range(DK):
                    c = DK * k + dk
                    E.wait(g("A_trevac"), c - 1)
                    for b in range(BPT):
                        E.put(tensor.matmul(
                            ptr[dk % 2][:, b * 128:(b + 1) * 128],
                            ln1tok[:, b * D + dk * 128:b * D + dk * 128 + 128],
                            ident[:], start=(b == 0), stop=(b == 3),
                            skip_group_check=True),
                            g("A_trps") if b == 3 else None)
                if k > 0:
                    E.wait(g("A_bufree"), k)
                for dk in range(DK):
                    c = DK * k + dk
                    E.wait(g("A_trevac"), c + 1)
                    for nh in range(2):
                        for cm in range(2):
                            last = (dk == DK - 1 and nh == 1 and cm == 1)
                            E.put(tensor.matmul(
                                pbu[2 * nh + cm][:],
                                bT[cm][:, dk * NST + nh * 128:dk * NST + nh * 128 + 128],
                                lnF[:, dk * TT:(dk + 1) * TT],
                                start=(dk == 0), stop=(dk == DK - 1),
                                skip_group_check=True),
                                g("A_bu") if last else None)

            def Y(j):
                for qi in range(16):
                    b, dq = qi // 4, qi % 4
                    yc = 16 * j + qi
                    if qi == 0:
                        E.wait(g("A_h"), min(9 + j, NTA))
                        if j > 0:
                            E.wait(g("B_outsb"), 16 * j)
                    E.wait(g("B_yfree"), yc - 1)
                    t0 = j * TT + b * 128
                    srcs = [(Hre[0], cT[0], 0), (Him[0], cT[1], 0),
                            (Hre[1], cT[0], 1), (Him[1], cT[1], 1)]
                    for i, (hsrc, csrc, nh) in enumerate(srcs):
                        E.put(tensor.matmul(
                            py[qi % 2][:], hsrc[:, t0:t0 + 128],
                            csrc[:, nh * D + dq * TT:nh * D + (dq + 1) * TT],
                            start=(i == 0), stop=(i == 3),
                            skip_group_check=True),
                            g("B_yps") if i == 3 else None)

            def TR2(j):
                if j > 0:
                    E.wait(g("B_outsb"), 16 * j)
                for b in range(BPT):
                    E.wait(g("B_ln2tok"), 4 * j + b + 1)
                    for dkg in range(4):
                        tc = 256 + 16 * j + 4 * b + dkg
                        E.wait(g("A_trevac"), tc - 1)
                        for i in range(4):
                            dk = 4 * dkg + i
                            E.put(tensor.matmul(
                                ptr[(4 * b + dkg) % 2][:, i * 128:(i + 1) * 128],
                                ln2tok[b % 2][:, dk * 128:(dk + 1) * 128],
                                ident[:], start=(i == 0), stop=(i == 3),
                                skip_group_check=True),
                                g("A_trps") if i == 3 else None)

            def H(j):
                if j == 0:
                    E.wait(g("A_bufree"), NTA)
                for hc in range(HCN):
                    gi = HCN * j + hc
                    if hc == 0:
                        E.wait(g("A_trevac"), 256 + 16 * j + 16)
                        E.wait(g("B_outsb"), 16 * j)
                    E.wait(g("B_p1"), 16 * (gi + 1))
                    if gi >= 2:
                        E.wait(g("B_sil"), gi - 1)
                    b1 = (hc % 2) * 2
                    for dk in range(DK):
                        E.put(tensor.matmul(
                            pbu[b1][:], w1p[hc % 2][:, dk * 128:(dk + 1) * 128],
                            lnF[:, dk * TT:(dk + 1) * TT],
                            start=(dk == 0), stop=(dk == DK - 1),
                            skip_group_check=True),
                            g("B_h1d") if dk == DK - 1 else None)
                    E.wait(g("B_p3"), 16 * (gi + 1))
                    if gi >= 2:
                        E.wait(g("B_g"), gi - 1)
                    for dk in range(DK):
                        E.put(tensor.matmul(
                            pbu[b1 + 1][:], w3p[hc % 2][:, dk * 128:(dk + 1) * 128],
                            lnF[:, dk * TT:(dk + 1) * TT],
                            start=(dk == 0), stop=(dk == DK - 1),
                            skip_group_check=True),
                            g("B_h3d") if dk == DK - 1 else None)

            psmap = [pbu[0], pbu[1], pbu[2], pbu[3], ptr[0], ptr[1], py[0], py[1]]

            def OUT(j):
                for pair in range(2):
                    for hc in range(HCN):
                        widx = (2 * j + pair) * HCN + hc
                        E.wait(g("B_p2"), 16 * (widx + 1))
                        E.wait(g("B_g"), HCN * j + hc + 1)
                        if hc == 0:
                            if pair == 0:
                                E.wait(g("B_g"), HCN * (j + 1))
                                E.wait(g("B_yfree"), 16 * (j + 1))
                                E.wait(g("A_trevac"), 256 + 16 * j + 16)
                            else:
                                E.wait(g("B_outsb"), 16 * j + 8)
                        for b in range(BPT):
                            for dq2 in range(2):
                                last = (b == BPT - 1 and dq2 == 1)
                                E.put(tensor.matmul(
                                    psmap[4 * dq2 + b][:],
                                    gbuf[:, hc * TT + b * 128:hc * TT + (b + 1) * 128],
                                    w2c[widx % 4][:, dq2 * TT:(dq2 + 1) * TT],
                                    start=(hc == 0), stop=(hc == HCN - 1),
                                    skip_group_check=True),
                                    g("B_w2u") if last else None)

            Y(0)
            TR2(0)
            for j in range(NTB):
                H(j)
                OUT(j)
                if j + 1 < NTB:
                    Y(j + 1)
                    TR2(j + 1)

        @block.gpsimd
        def _(gpsimd):
            E = Eng(gpsimd, "gpsimd", log)
            E.put(gpsimd.dma_start(out=bT[0][:].rearrange("p (dk n) -> p dk n", dk=DK),
                                   in_=breT.rearrange("(dk p) n -> p dk n", p=128)),
                  g("L_cast"), 16)
            E.put(gpsimd.dma_start(out=bT[1][:].rearrange("p (dk n) -> p dk n", dk=DK),
                                   in_=bimT.rearrange("(dk p) n -> p dk n", p=128)),
                  g("L_cast"), 16)
            E.put(gpsimd.dma_start(out=cT[0][:].rearrange("p (h d) -> p h d", h=2),
                                   in_=creT.rearrange("(h p) d -> p h d", p=128)),
                  g("L_cast"), 16)
            E.put(gpsimd.dma_start(out=cT[1][:].rearrange("p (h d) -> p h d", h=2),
                                   in_=cimT.rearrange("(h p) d -> p h d", p=128)),
                  g("L_cast"), 16)

            tld = [0]
            qn = [0]

            def roundtrip(widx):
                E.wait(V_thr[widx], 1)
                E.put(gpsimd.dma_start(
                    out=tstat[widx:widx + 1, :].rearrange("a p -> p a"),
                    in_=redtot[:]), g("G_st"), 16)
                E.wait(g("G_st"), 16 * (4 * widx + 1))
                E.put(gpsimd.dma_start(out=rrow[:], in_=tstat[widx:widx + 1, :]),
                      g("G_st"), 16)
                E.wait(V_thr[widx], 3)
                E.put(gpsimd.dma_start(out=tstat[4 + widx:5 + widx, 0:1], in_=rsum[:]),
                      g("G_st"), 16)
                E.wait(g("G_st"), 16 * (4 * widx + 3))
                st = [sc_s1, sc_s3, sc_s2][widx]
                E.put(gpsimd.dma_start(
                    out=st[:],
                    in_=tstat[4 + widx:5 + widx, 0:1].broadcast_to((128, 1))),
                    g("G_st"), 16)

            qjobs = []
            for widx, (wsrc, wdst) in enumerate(((w1T, w1qs), (w3T, w3qs))):
                qjobs.append(("rt", widx))
                for dk in range(DK):
                    h0 = 0
                    for (pw, valid) in W13_SUB9:
                        npan = pw // 128
                        hc0 = h0 // 128
                        store = wdst[hc0:hc0 + npan, :, dk * 128:(dk + 1) * 128]
                        store = store.rearrange("hc p jj -> p hc jj")
                        qjobs.append(("q", valid, pw, True,
                                      wsrc[dk * 128:(dk + 1) * 128, h0:h0 + valid],
                                      store, npan, 0))
                        h0 += pw
            qjobs.append(("rt", 2))
            for hc in range(HCN):
                rows = 128 if hc < HCN - 1 else HID - 128 * (HCN - 1)
                h0 = 0
                for pi, (pw, valid) in enumerate(W2_SUB):
                    qjobs.append(("q", valid, pw, False,
                                  w2T[hc * 128:hc * 128 + rows, h0:h0 + valid],
                                  w2qs[hc, :, h0:h0 + pw], 0,
                                  (pi + 1) if rows < 128 else 0))
                    h0 += pw

            qlist = [j for j in qjobs if j[0] == "q"]

            def q_issue(q):
                if q >= len(qlist):
                    return
                _, valid, pw, panel, src_ap, _, _, zw = qlist[q]
                if zw:
                    E.wait(g("V_zw2"), zw)
                rowsl = src_ap.shape[0]
                E.put(gpsimd.dma_start(out=tin[q % 4][0:rowsl, 0:valid], in_=src_ap),
                      g("G_tld"), 16)

            for q in range(4):
                q_issue(q)
            qd = [0]
            for job in qjobs:
                if job[0] == "rt":
                    roundtrip(job[1])
                    continue
                q = qd[0]
                _, valid, pw, panel, src_ap, store, npan, zw = job
                E.wait(g("V_q"), 2 * (q + 1))
                if panel:
                    srcv = tq[q % 2][:, 0:pw].rearrange("p (hc jj) -> p hc jj",
                                                        hc=npan)
                else:
                    srcv = tq[q % 2][:, 0:pw]
                E.put(gpsimd.dma_start(out=store, in_=srcv), g("G_twr"), 16)
                q_issue(q + 4)
                qd[0] += 1

            for j in range(NTB):
                for dq in range(4):
                    for b in range(BPT):
                        oi = 16 * j + 4 * dq + b
                        E.wait(g("B_outsb"), oi + 1)
                        r0 = j * TT + b * 128
                        E.put(gpsimd.dma_start(
                            out=out[r0:r0 + 128, dq * TT:(dq + 1) * TT],
                            in_=ost[oi % 2][:]), g("G_out"), 16)
            if DEBUG:
                E.wait(g("G_out"), 16 * 128)
                for srcp, c0, w in [(pr_mag, 0, 2), (pr_gam, 2, 2), (pr_cos, 4, 2),
                                    (pr_sin, 6, 2), (sc_s1, 8, 1), (sc_s3, 9, 1),
                                    (sc_s2, 10, 1), (sc_thr[0], 11, 1),
                                    (sc_thr[1], 12, 1), (sc_thr[2], 13, 1),
                                    (hreL, 14, 2), (himL, 16, 2), (var_t, 18, 4),
                                    (mean_t, 22, 4), (nusb, 26, 2), (thsb, 28, 2)]:
                    E.put(gpsimd.dma_start(out=dbg_small[:, c0:c0 + w], in_=srcp[:],
                                           allow_slow_non_contiguous=True),
                          g("G_out"), 16)
                E.put(gpsimd.dma_start(out=dbg_stat[:, 0:64], in_=stat_rstd[:],
                                       allow_slow_non_contiguous=True), g("G_out"), 16)
                E.put(gpsimd.dma_start(out=dbg_stat[:, 64:128], in_=stat_nb[:],
                                       allow_slow_non_contiguous=True), g("G_out"), 16)
                E.put(gpsimd.dma_start(out=dbg_h[:], in_=Hre[0][:]), g("G_out"), 16)
                E.wait(g("G_out"), 16 * (128 + 19))
            else:
                E.wait(g("G_out"), 16 * 128)

    _check_totals(log)
    _simulate(log)
    return nc


_NC_CACHE = {}


def _get_nc():
    if "nc" not in _NC_CACHE:
        _NC_CACHE["nc"] = build_nc()
    return _NC_CACHE["nc"]


def kernel(L0_attn_x, L0_attn_nu_log, L0_attn_theta_log, L0_attn_B_re, L0_attn_B_im,
           L0_attn_C_re, L0_attn_C_im, L0_attn_D, L0_ffn_w1, L0_ffn_w3, L0_ffn_w2):
    x = np.ascontiguousarray(np.asarray(L0_attn_x, dtype=np.float32))
    assert x.shape == (4, 8192, 2048)

    shared = {
        "breT": np.ascontiguousarray(np.asarray(L0_attn_B_re, np.float32).T),
        "bimT": np.ascontiguousarray(np.asarray(L0_attn_B_im, np.float32).T),
        "creT": np.ascontiguousarray(np.asarray(L0_attn_C_re, np.float32).T),
        "cimT": np.ascontiguousarray(np.asarray(L0_attn_C_im, np.float32).T),
        "nu_in": np.ascontiguousarray(
            np.asarray(L0_attn_nu_log, np.float32).reshape(1, NST)),
        "th_in": np.ascontiguousarray(
            np.asarray(L0_attn_theta_log, np.float32).reshape(1, NST)),
        "dvec": np.ascontiguousarray(np.asarray(L0_attn_D, np.float32).reshape(1, D)),
        "w1T": np.ascontiguousarray(np.asarray(L0_ffn_w1, np.float32).T),
        "w3T": np.ascontiguousarray(np.asarray(L0_ffn_w3, np.float32).T),
        "w2T": np.ascontiguousarray(np.asarray(L0_ffn_w2, np.float32).T),
        "ident_in": np.eye(128, dtype=ml_dtypes.bfloat16),
        "consts": np.array([[1.0, 0.0, math.pi / 2, 1e-5, 0, 0, 0, 0]], np.float32),
    }
    zeros = np.zeros((TO, D), np.float32)
    in_maps = []
    for core in range(8):
        b, h = core // 2, core % 2
        xa_c = np.concatenate([zeros, x[b, :TO]], axis=0) if h == 0 else x[b]
        m = dict(shared)
        m["xa"] = np.ascontiguousarray(xa_c)
        in_maps.append(m)

    nc = _get_nc()
    res = run_bass_kernel_spmd(nc, in_maps, core_ids=list(range(8)))
    outp = np.empty((4, 8192, 2048), np.float32)
    for core in range(8):
        b, h = core // 2, core % 2
        outp[b, h * TO:(h + 1) * TO] = res.results[core]["out"]
    return outp


if __name__ == "__main__":
    nc = build_nc()
    print("build OK")


# version 11 (final)
# speedup vs baseline: 1.4735x; 1.0589x over previous
import math
from contextlib import ExitStack

import numpy as np
import ml_dtypes

import concourse.bass as bass
import concourse.mybir as mybir
from concourse.bass_utils import run_bass_kernel_spmd

F32 = mybir.dt.float32
BF16 = mybir.dt.bfloat16
AF = mybir.ActivationFunctionType
ALU = mybir.AluOpType
AXX = mybir.AxisListType.X

D = 2048
DK = 16
NST = 256
HID = 5461
HCN = 43
TA = 8192
TO = 4096
TT = 512
NTA = 16
NTB = 8
BPT = 4
PI = math.pi
TWO_PI_C1 = 6.28125
TWO_PI_C2 = 2.0 * math.pi - 6.28125
MAGIC = 12582912.0

C_ONE, C_ZERO, C_HALFPI, C_EPS = 0, 1, 2, 3
DEBUG = False

FLATC = (D * HID) // 128
SCH = 2048
NSC = 43
SC_TAIL = FLATC - (NSC - 1) * SCH
SSCH = 768
NSSC = (FLATC + SSCH - 1) // SSCH
SS_TAIL = FLATC - (NSSC - 1) * SSCH
W13_SUB9 = [(640, 640)] * 8 + [(384, 341)]
W2_SUB = [(768, 768), (640, 640), (640, 640)]
W13_SUB = [(1408, 1408), (1408, 1408), (1408, 1408), (1280, 1237)]
NQ_W13 = DK * len(W13_SUB)
NQ_ALL = 2 * NQ_W13 + HCN
NLOAD_S = 3 * NSC
G_TWR_TOTAL = 2 * NQ_W13 + HCN


class Sem:
    def __init__(self, h, name):
        self.h = h
        self.name = name
        self.total = 0


class Eng:
    def __init__(self, e, name, log):
        self.e = e
        self.name = name
        self.log = log

    def wait(self, sem, val):
        if val <= 0:
            return
        self.e.wait_ge(sem.h, val)
        self.log.append((self.name, "wait", sem, val))

    def put(self, inst, sem=None, n=1):
        if sem is not None:
            inst.then_inc(sem.h, n)
            sem.total += n
            self.log.append((self.name, "inc", sem, n))
        return inst


def _simulate(log):
    streams = {}
    for ev in log:
        streams.setdefault(ev[0], []).append(ev)
    pos = {k: 0 for k in streams}
    counts = {}
    progressed = True
    while progressed:
        progressed = False
        for name, evs in streams.items():
            while pos[name] < len(evs):
                _, kind, sem, val = evs[pos[name]]
                if kind == "wait" and counts.get(id(sem), 0) < val:
                    break
                if kind == "inc":
                    counts[id(sem)] = counts.get(id(sem), 0) + val
                pos[name] += 1
                progressed = True
    stuck = {k: pos[k] for k in streams if pos[k] < len(streams[k])}
    if stuck:
        det = []
        for k in stuck:
            ev = streams[k][pos[k]]
            det.append(f"{k}@{pos[k]}/{len(streams[k])} waits {ev[2].name}>={ev[3]} "
                       f"(have {counts.get(id(ev[2]), 0)}, total {ev[2].total})")
        raise RuntimeError("static deadlock: " + "; ".join(det))


def _check_totals(log):
    bad = []
    for ev in log:
        if ev[1] == "wait" and ev[3] > ev[2].total:
            bad.append(f"{ev[0]} waits {ev[2].name}>={ev[3]} total={ev[2].total}")
    if bad:
        raise RuntimeError("impossible waits: " + "; ".join(bad[:10]))


def _prologue_steps(t):
    S = []

    def s(fn):
        S.append(("s", fn))

    def v(fn):
        S.append(("v", fn))

    s(lambda e: e.activation(out=t["pt1"][:], in_=t["nusb"][:], func=AF.Abs))
    s(lambda e: e.activation(out=t["pt2"][:], in_=t["pt1"][:], func=AF.Exp, scale=-1.0))
    s(lambda e: e.activation(out=t["pt2"][:], in_=t["pt2"][:], func=AF.Ln,
                             bias=t["cbc"][:, C_ONE:C_ONE + 1]))
    v(lambda e: e.tensor_scalar_max(t["pt3"][:], t["nusb"][:], 0.0))
    v(lambda e: e.tensor_add(t["pt4"][:], t["pt3"][:], t["pt2"][:]))
    s(lambda e: e.activation(out=t["pt1"][:], in_=t["pt4"][:], func=AF.Exp))
    s(lambda e: e.activation(out=t["pr_mag"][:], in_=t["pt1"][:], func=AF.Exp, scale=-1.0))
    v(lambda e: e.tensor_mul(t["pt3"][:], t["pr_mag"][:], t["pr_mag"][:]))
    v(lambda e: e.tensor_scalar(out=t["pt3"][:], in0=t["pt3"][:], scalar1=-1.0,
                                scalar2=1.0, op0=ALU.mult, op1=ALU.add))
    s(lambda e: e.activation(out=t["pt1"][:], in_=t["pt3"][:], func=AF.Ln))
    s(lambda e: e.activation(out=t["pr_gam"][:], in_=t["pt1"][:], func=AF.Exp, scale=0.5))
    s(lambda e: e.activation(out=t["pt1"][:], in_=t["thsb"][:], func=AF.Exp))
    v(lambda e: e.tensor_scalar_mul(t["pt2"][:], t["pt1"][:], 1.0 / (2 * PI)))
    v(lambda e: e.tensor_scalar(out=t["pt2"][:], in0=t["pt2"][:], scalar1=MAGIC,
                                scalar2=MAGIC, op0=ALU.add, op1=ALU.subtract))
    v(lambda e: e.scalar_tensor_tensor(out=t["pt4"][:], in0=t["pt2"][:],
                                       scalar=-TWO_PI_C1, in1=t["pt1"][:],
                                       op0=ALU.mult, op1=ALU.add))
    v(lambda e: e.scalar_tensor_tensor(out=t["pt3"][:], in0=t["pt2"][:],
                                       scalar=-TWO_PI_C2, in1=t["pt4"][:],
                                       op0=ALU.mult, op1=ALU.add))
    s(lambda e: e.activation(out=t["pr_sin"][:], in_=t["pt3"][:], func=AF.Sin))
    s(lambda e: e.activation(out=t["pt2"][:], in_=t["pt3"][:], func=AF.Abs))
    s(lambda e: e.activation(out=t["pr_cos"][:], in_=t["pt2"][:], func=AF.Sin,
                             scale=-1.0, bias=t["cbc"][:, C_HALFPI:C_HALFPI + 1]))
    return S


def build_nc():
    nc = bass.Bass("TRN2", target_bir_lowering=False, debug=False)

    xa = nc.dram_tensor("xa", [TA, D], F32, kind="ExternalInput").ap()
    breT = nc.dram_tensor("breT", [D, NST], F32, kind="ExternalInput").ap()
    bimT = nc.dram_tensor("bimT", [D, NST], F32, kind="ExternalInput").ap()
    creT = nc.dram_tensor("creT", [NST, D], F32, kind="ExternalInput").ap()
    cimT = nc.dram_tensor("cimT", [NST, D], F32, kind="ExternalInput").ap()
    nu_in = nc.dram_tensor("nu_in", [1, NST], F32, kind="ExternalInput").ap()
    th_in = nc.dram_tensor("th_in", [1, NST], F32, kind="ExternalInput").ap()
    dvec = nc.dram_tensor("dvec", [1, D], F32, kind="ExternalInput").ap()
    w1T = nc.dram_tensor("w1T", [D, HID], F32, kind="ExternalInput").ap()
    w3T = nc.dram_tensor("w3T", [D, HID], F32, kind="ExternalInput").ap()
    w2T = nc.dram_tensor("w2T", [HID, D], F32, kind="ExternalInput").ap()
    ident_in = nc.dram_tensor("ident_in", [128, 128], BF16, kind="ExternalInput").ap()
    consts = nc.dram_tensor("consts", [1, 8], F32, kind="ExternalInput").ap()
    out = nc.dram_tensor("out", [TO, D], F32, kind="ExternalOutput").ap()
    w1qs = nc.dram_tensor("w1qs", [HCN, 128, D], BF16, kind="Internal").ap()
    w3qs = nc.dram_tensor("w3qs", [HCN, 128, D], BF16, kind="Internal").ap()
    w2qs = nc.dram_tensor("w2qs", [HCN, 128, D], BF16, kind="Internal").ap()
    if DEBUG:
        dbg_small = nc.dram_tensor("dbg_small", [128, 32], F32, kind="ExternalOutput").ap()
        dbg_stat = nc.dram_tensor("dbg_stat", [128, 256], F32, kind="ExternalOutput").ap()
        dbg_h = nc.dram_tensor("dbg_h", [128, TO], F32, kind="ExternalOutput").ap()
    tstat = nc.dram_tensor("tstat", [8, 128], F32, kind="Internal").ap()

    w1flat = w1T.rearrange("a b -> (a b)").rearrange("(p c) -> p c", p=128)
    w3flat = w3T.rearrange("a b -> (a b)").rearrange("(p c) -> p c", p=128)
    w2flat = w2T.rearrange("a b -> (a b)").rearrange("(p c) -> p c", p=128)

    log = []
    with ExitStack() as ctx:
        sb = lambda n, s, d=F32: ctx.enter_context(nc.sbuf_tensor(n, s, d))
        ps_ = lambda n: ctx.enter_context(nc.psum_tensor(n, [128, 512], F32))
        mksem = lambda n: Sem(ctx.enter_context(nc.semaphore(n)), n)

        Hre = [sb(f"hre{i}", [128, TO], BF16) for i in range(2)]
        Him = [sb(f"him{i}", [128, TO], BF16) for i in range(2)]
        lnF = sb("lnF", [128, DK * TT], BF16)
        psq = sb("psq", [128, D])
        dbc = sb("dbc", [128, D])
        cT = [sb("creb", [128, 2 * D], BF16), sb("cimb", [128, 2 * D], BF16)]
        ident = sb("ident", [128, 128], BF16)
        cbc = sb("cbc", [128, 8])
        stat_rstd = sb("st_rstd", [128, 64])
        stat_nb = sb("st_nb", [128, 64])
        stat_ssum = sb("st_ssum", [128, 64])
        stat_sqs = sb("st_sqs", [128, 64])
        stat2_rstd = sb("s2_rstd", [128, 32])
        stat2_nb = sb("s2_nb", [128, 32])
        stat2_ssum = sb("s2_ssum", [128, 32])
        stat2_sqs = sb("s2_sqs", [128, 32])
        nusb = sb("nusb", [128, 2])
        thsb = sb("thsb", [128, 2])
        pr_mag = sb("pr_mag", [128, 2])
        pr_gam = sb("pr_gam", [128, 2])
        pr_cos = sb("pr_cos", [128, 2])
        pr_sin = sb("pr_sin", [128, 2])
        pt1 = sb("pt1", [128, 2])
        pt2 = sb("pt2", [128, 2])
        pt3 = sb("pt3", [128, 2])
        pt4 = sb("pt4", [128, 2])
        w2k_c = sb("w2k_c", [128, 2])
        w2k_s = sb("w2k_s", [128, 2])
        sc_s1 = sb("sc_s1", [128, 1])
        sc_s3 = sb("sc_s3", [128, 1])
        sc_s2 = sb("sc_s2", [128, 1])
        sc_thr = [sb(f"sc_thr{i}", [128, 1]) for i in range(3)]
        redarr = sb("redarr", [128, 3 * 128])
        redtot = sb("redtot", [128, 1])
        rrow = sb("rrow", [1, 128])
        rsum = sb("rsum", [1, 1])
        hreL = sb("hreL", [128, 2])
        himL = sb("himL", [128, 2])
        s1i = sb("s1i", [128, 2])
        s2i = sb("s2i", [128, 2])
        gt1 = sb("gt1", [128, 2])
        gt2 = sb("gt2", [128, 2])
        mean_t = sb("mean_t", [128, 4])
        meann_t = sb("meann_t", [128, 4])
        msq_t = sb("msq_t", [128, 4])
        var_t = sb("var_t", [128, 4])

        a_ctx = ExitStack()
        asb = lambda n, s, d=F32: a_ctx.enter_context(nc.sbuf_tensor(n, s, d))
        bT = [asb("breb", [128, DK * NST], BF16), asb("bimb", [128, DK * NST], BF16)]
        xar = [asb(f"xar{i}", [128, D]) for i in range(4)]
        ln1tok = asb("ln1tok", [128, BPT * D], BF16)
        scr = [asb(f"scr{i}", [128, TT]) for i in range(4)]
        s1t = asb("s1t", [128, TT])
        s2t = asb("s2t", [128, TT])
        cs_t = asb("cs_t", [128, 2 * TT])
        sn_t = asb("sn_t", [128, 2 * TT])
        csp_t = asb("csp_t", [128, 2 * TT])
        snp_t = asb("snp_t", [128, 2 * TT])
        tin = [asb(f"tin{i}", [128, 768]) for i in range(4)]
        tins = [asb(f"tins{i}", [128, SSCH]) for i in range(4)]
        tq = [asb("tq0", [128, 768], BF16), asb("tq1", [128, 768], BF16)]
        tl = asb("tl", [128, 768], BF16)
        a_ctx.close()

        x1 = [sb(f"x1_{i}", [128, D]) for i in range(BPT)]
        gbuf = sb("gbuf", [128, HCN * TT], BF16)
        ln2tok = [sb(f"ln2tok{i}", [128, D], BF16) for i in range(2)]
        w1p = [sb(f"w1p{i}", [128, D], BF16) for i in range(2)]
        w3p = [sb(f"w3p{i}", [128, D], BF16) for i in range(2)]
        w2c = [sb(f"w2c{i}", [128, 2 * TT], BF16) for i in range(4)]
        xq = [sb(f"xq{i}", [128, TT]) for i in range(2)]
        lnbq = [sb(f"lnbq{i}", [128, TT]) for i in range(2)]
        silb = [sb(f"silb{i}", [128, TT]) for i in range(2)]
        ost = [sb(f"ost{i}", [128, TT]) for i in range(2)]

        pbu = [ps_(f"pbu{i}") for i in range(4)]
        ptr = [ps_(f"ptr{i}") for i in range(2)]
        py = [ps_(f"py{i}") for i in range(2)]

        sems = {}
        for n in ["A_xa", "A_sq", "A_var", "A_rstd", "A_nb", "A_ln1tok", "A_trps",
                  "A_trevac", "A_bu", "A_bufree", "A_h", "L_pro", "L_cast", "TV",
                  "TS", "G_tld", "V_red", "V_q", "G_twr", "G_st", "B_x", "B_lnb",
                  "B_base", "B_yps", "B_yfree", "B_sq2", "B_var2", "B_rstd2",
                  "B_nb2", "B_ln2tok", "B_p1", "B_p3", "B_p2", "B_h1d", "B_h3d",
                  "B_sil", "B_g", "B_outsb", "B_w2u", "G_out", "V_zw2", "L_ts", "S_red", "S_fl",
                  "V_thr0", "V_thr1", "V_thr2"]:
            sems[n] = mksem(n)
        g = lambda n: sems[n]
        V_thr = [g("V_thr0"), g("V_thr1"), g("V_thr2")]

        ptiles = {"nusb": nusb, "thsb": thsb, "cbc": cbc, "pt1": pt1, "pt2": pt2,
                  "pt3": pt3, "pt4": pt4, "pr_mag": pr_mag, "pr_gam": pr_gam,
                  "pr_cos": pr_cos, "pr_sin": pr_sin}
        PSTEPS = _prologue_steps(ptiles)
        NPS_S = sum(1 for e, _ in PSTEPS if e == "s")
        NPS_V = sum(1 for e, _ in PSTEPS if e == "v")

        block = ctx.enter_context(nc.Block())

        @block.sync
        def _(sync):
            E = Eng(sync, "sync", log)
            E.put(sync.dma_start(out=cbc[:], in_=consts.broadcast_to((128, 8))),
                  g("L_pro"), 16)
            E.put(sync.dma_start(out=nusb[:],
                                 in_=nu_in.rearrange("o (h p) -> p (o h)", p=128),
                                 allow_slow_non_contiguous=True),
                  g("L_pro"), 16)
            E.put(sync.dma_start(out=thsb[:],
                                 in_=th_in.rearrange("o (h p) -> p (o h)", p=128),
                                 allow_slow_non_contiguous=True),
                  g("L_pro"), 16)
            E.put(sync.dma_start(out=ident[:], in_=ident_in), g("L_pro"), 16)
            E.put(sync.dma_start(out=dbc[:], in_=dvec.broadcast_to((128, D))),
                  g("L_pro"), 16)
            for gg in range(NTA * BPT):
                if gg >= 4:
                    E.wait(g("A_ln1tok"), gg - 3)
                E.put(sync.dma_start(out=xar[gg % 4][:],
                                     in_=xa[gg * 128:(gg + 1) * 128, :]),
                      g("A_xa"), 16)
            E.wait(g("G_twr"), 16 * G_TWR_TOTAL)
            E.wait(g("A_h"), NTA)
            for j in range(NTB):
                for hc in range(HCN):
                    gi = HCN * j + hc
                    if gi >= 2:
                        E.wait(g("B_h1d"), gi - 1)
                    E.put(sync.dma_start(out=w1p[hc % 2][:], in_=w1qs[hc, :, :]),
                          g("B_p1"), 16)
                    if gi >= 2:
                        E.wait(g("B_h3d"), gi - 1)
                    E.put(sync.dma_start(out=w3p[hc % 2][:], in_=w3qs[hc, :, :]),
                          g("B_p3"), 16)
                for pair in range(2):
                    for hc in range(HCN):
                        widx = (2 * j + pair) * HCN + hc
                        if widx >= 4:
                            E.wait(g("B_w2u"), widx - 3)
                        E.put(sync.dma_start(
                            out=w2c[widx % 4][:],
                            in_=w2qs[hc, :, pair * 1024:(pair + 1) * 1024]),
                            g("B_p2"), 16)

        @block.scalar
        def _(scalar):
            E = Eng(scalar, "scalar", log)
            E.wait(g("L_pro"), 16 * 3)
            nv = 0
            ns = 0
            for eng, fn in PSTEPS:
                if eng == "s":
                    E.wait(g("TV"), nv)
                    E.put(fn(scalar), g("TS"), 1)
                    ns += 1
                    E.wait(g("TS"), ns)
                else:
                    nv += 1

            s_meta = []
            for widx in (0, 1, 2):
                for ci in range(NSSC):
                    s_meta.append((widx, ci, SSCH if ci < NSSC - 1 else SS_TAIL))
            s_i = [0]
            flats = [w1flat, w3flat, w2flat]

            def s_issue(i):
                if i >= len(s_meta):
                    return
                widx, ci, cw = s_meta[i]
                E.put(scalar.dma_start(out=tins[i % 4][:, 0:cw],
                                       in_=flats[widx][:, ci * SSCH:ci * SSCH + cw]),
                      g("L_ts"), 16)

            for i in range(4):
                s_issue(i)

            def abs_unit():
                i = s_i[0]
                if i >= len(s_meta):
                    return
                widx, ci, cw = s_meta[i]
                E.wait(g("L_ts"), 16 * (i + 1))
                E.put(scalar.activation(out=psq[:, 0:cw], in_=tins[i % 4][:, 0:cw],
                                        func=AF.Abs,
                                        accum_out=redarr[:, widx * 128 + ci:
                                                         widx * 128 + ci + 1]),
                      g("S_red"))
                if ci == NSSC - 1:
                    E.put(scalar.activation(out=psq[:, 0:NSSC],
                                            in_=redarr[:, widx * 128:
                                                       widx * 128 + NSSC],
                                            func=AF.Identity), g("S_fl"))
                s_issue(i + 4)
                s_i[0] += 1

            for k in range(NTA):
                for b in range(BPT):
                    gg = BPT * k + b
                    E.wait(g("A_xa"), 16 * (gg + 1))
                    E.put(scalar.activation(out=psq[:], in_=xar[gg % 4][:],
                                            func=AF.Square,
                                            accum_out=stat_sqs[:, gg:gg + 1]))
                    for _ in range(8):
                        abs_unit()
                E.put(scalar.activation(out=psq[:, 0:4],
                                        in_=stat_sqs[:, 4 * k:4 * k + 4],
                                        func=AF.Identity), g("A_sq"))
                E.wait(g("A_var"), k + 1)
                E.put(scalar.activation(out=var_t[:], in_=var_t[:], func=AF.Ln,
                                        bias=cbc[:, C_EPS:C_EPS + 1]), g("TS"))
                E.wait(g("TS"), NPS_S + k + 1)
                E.put(scalar.activation(out=stat_rstd[:, 4 * k:4 * k + 4],
                                        in_=var_t[:], func=AF.Exp, scale=-0.5),
                      g("A_rstd"))
                E.wait(g("A_nb"), k + 1)
                for b in range(BPT):
                    gg = BPT * k + b
                    E.put(scalar.activation(out=ln1tok[:, b * D:(b + 1) * D],
                                            in_=xar[gg % 4][:], func=AF.Identity,
                                            scale=stat_rstd[:, gg:gg + 1],
                                            bias=stat_nb[:, gg:gg + 1]),
                          g("A_ln1tok"))


            def xload(j, p):
                qg = 16 * j + p
                b, dq = p % 4, p // 4
                if qg == 0:
                    E.wait(g("A_h"), NTA)
                    E.wait(g("V_q"), 2 * NQ_ALL)
                if qg >= 2:
                    E.wait(g("B_base"), qg - 1)
                r0 = TO + j * TT + b * 128
                E.put(scalar.dma_start(out=xq[qg % 2][:],
                                       in_=xa[r0:r0 + 128, dq * TT:(dq + 1) * TT]),
                      g("B_x"), 16)

            def lnb(j, p):
                qg = 16 * j + p
                b = p % 4
                gg = 32 + 4 * j + b
                E.wait(g("B_x"), 16 * (qg + 1))
                if qg >= 2:
                    E.wait(g("B_base"), qg - 1)
                E.put(scalar.activation(out=lnbq[qg % 2][:], in_=xq[qg % 2][:],
                                        func=AF.Identity,
                                        scale=stat_rstd[:, gg:gg + 1],
                                        bias=stat_nb[:, gg:gg + 1]), g("B_lnb"))

            def sq2_chain(j):
                for b in range(BPT):
                    gg = 4 * j + b
                    E.wait(g("B_yfree"), 16 * j + 4 * b + 4)
                    E.put(scalar.activation(out=psq[:], in_=x1[b][:], func=AF.Square,
                                            accum_out=stat2_sqs[:, gg:gg + 1]))
                E.put(scalar.activation(out=psq[:, 0:4],
                                        in_=stat2_sqs[:, 4 * j:4 * j + 4],
                                        func=AF.Identity), g("B_sq2"))
                E.wait(g("B_var2"), j + 1)
                E.put(scalar.activation(out=var_t[:], in_=var_t[:], func=AF.Ln,
                                        bias=cbc[:, C_EPS:C_EPS + 1]), g("TS"))
                E.wait(g("TS"), NPS_S + NTA + j + 1)
                E.put(scalar.activation(out=stat2_rstd[:, 4 * j:4 * j + 4],
                                        in_=var_t[:], func=AF.Exp, scale=-0.5),
                      g("B_rstd2"))
                E.wait(g("B_nb2"), j + 1)
                if j == 0:
                    E.wait(g("A_h"), NTA)
                for b in range(BPT):
                    gg = 4 * j + b
                    prev_groups = 16 * j + 4 * b - 8
                    if prev_groups >= 0:
                        E.wait(g("A_trps"), 256 + prev_groups + 4)
                    E.put(scalar.activation(out=ln2tok[b % 2][:], in_=x1[b][:],
                                            func=AF.Identity,
                                            scale=stat2_rstd[:, gg:gg + 1],
                                            bias=stat2_nb[:, gg:gg + 1]),
                          g("B_ln2tok"))

            def silu(j, hc):
                gi = HCN * j + hc
                E.wait(g("B_h1d"), gi + 1)
                if gi >= 2:
                    E.wait(g("B_g"), gi - 1)
                bank = (hc % 2) * 2
                E.put(scalar.activation(out=silb[hc % 2][:], in_=pbu[bank][:],
                                        func=AF.Silu, scale=sc_s1[:, 0:1]),
                      g("B_sil"))

            for p in range(16):
                xload(0, p)
                lnb(0, p)
            sq2_chain(0)
            for j in range(NTB):
                for hc in range(HCN):
                    silu(j, hc)
                if j + 1 < NTB:
                    for p in range(16):
                        xload(j + 1, p)
                        lnb(j + 1, p)
                    sq2_chain(j + 1)

        @block.vector
        def _(vector):
            E = Eng(vector, "vector", log)
            tvn = [0]

            def vput(inst):
                E.put(inst, g("TV"), 1)
                tvn[0] += 1

            def vbar():
                E.wait(g("TV"), tvn[0])

            E.wait(g("L_pro"), 16 * 5)
            ns = 0
            for eng, fn in PSTEPS:
                if eng == "v":
                    E.wait(g("TS"), ns)
                    vput(fn(vector))
                    vbar()
                else:
                    ns += 1
            E.wait(g("TS"), NPS_S)

            for nh in range(2):
                c0 = nh * TT
                nhs = slice(nh, nh + 1)
                vput(vector.tensor_copy(cs_t[:, c0:c0 + 1], cbc[:, C_ONE:C_ONE + 1]))
                vput(vector.tensor_copy(sn_t[:, c0:c0 + 1], cbc[:, C_ZERO:C_ZERO + 1]))
                vput(vector.tensor_copy(w2k_c[:, nhs], pr_cos[:, nhs]))
                vput(vector.tensor_copy(w2k_s[:, nhs], pr_sin[:, nhs]))
                vbar()
                w = 1
                while w < TT:
                    cc = w2k_c[:, nhs]
                    ss = w2k_s[:, nhs]
                    vput(vector.tensor_scalar(out=scr[0][:, 0:w], in0=cs_t[:, c0:c0 + w],
                                              scalar1=cc, scalar2=None, op0=ALU.mult))
                    vput(vector.tensor_scalar(out=scr[1][:, 0:w], in0=sn_t[:, c0:c0 + w],
                                              scalar1=ss, scalar2=None, op0=ALU.mult))
                    vput(vector.tensor_scalar(out=scr[2][:, 0:w], in0=sn_t[:, c0:c0 + w],
                                              scalar1=cc, scalar2=None, op0=ALU.mult))
                    vput(vector.tensor_scalar(out=scr[3][:, 0:w], in0=cs_t[:, c0:c0 + w],
                                              scalar1=ss, scalar2=None, op0=ALU.mult))
                    vbar()
                    vput(vector.tensor_sub(cs_t[:, c0 + w:c0 + 2 * w],
                                           scr[0][:, 0:w], scr[1][:, 0:w]))
                    vput(vector.tensor_add(sn_t[:, c0 + w:c0 + 2 * w],
                                           scr[2][:, 0:w], scr[3][:, 0:w]))
                    vput(vector.tensor_mul(pt1[:, nhs], cc, cc))
                    vput(vector.tensor_mul(pt2[:, nhs], ss, ss))
                    vput(vector.tensor_mul(pt4[:, nhs], cc, ss))
                    vbar()
                    vput(vector.tensor_sub(w2k_c[:, nhs], pt1[:, nhs], pt2[:, nhs]))
                    vput(vector.tensor_scalar_mul(w2k_s[:, nhs], pt4[:, nhs], 2.0))
                    vbar()
                    w *= 2
                vput(vector.tensor_scalar(out=csp_t[:, c0:c0 + TT],
                                          in0=cs_t[:, c0:c0 + TT],
                                          scalar1=pr_gam[:, nhs], scalar2=None,
                                          op0=ALU.mult))
                vput(vector.tensor_scalar(out=snp_t[:, c0:c0 + TT],
                                          in0=sn_t[:, c0:c0 + TT],
                                          scalar1=pr_gam[:, nhs], scalar2=None,
                                          op0=ALU.mult))
                vbar()

            tern_ops = []
            fins = {}
            thrs = {}
            tld_n = [0]
            qn_n = [0]
            for widx in (0, 1, 2):

                def fin(widx=widx):
                    E.wait(g("S_red"), NSSC * (widx + 1))
                    E.wait(g("S_fl"), widx + 1)
                    E.put(vector.reduce_sum(redtot[:],
                                            redarr[:, widx * 128:widx * 128 + NSSC],
                                            axis=AXX), V_thr[widx])
                    E.wait(g("G_st"), 16 * (4 * widx + 2))
                    E.put(vector.reduce_sum(rsum[:], rrow[:], axis=AXX), V_thr[widx])
                    E.wait(V_thr[widx], 2)
                    E.put(vector.tensor_scalar_mul(rsum[:], rsum[:],
                                                   1.0 / (D * HID)), V_thr[widx])
                fins[widx] = fin

                def thr(widx=widx):
                    st = [sc_s1, sc_s3, sc_s2][widx]
                    E.wait(g("G_st"), 16 * (4 * widx + 4))
                    E.put(vector.tensor_scalar_mul(sc_thr[widx][:], st[:], 0.5),
                          V_thr[widx])
                thrs[widx] = thr

            def add_q(widx, pw, valid, prezero):
                def f():
                    q = qn_n[0]
                    tqs = tq[q % 2]
                    if q >= 2:
                        E.wait(g("G_twr"), 16 * (q - 1))
                    E.wait(g("G_tld"), 16 * (q + 1))
                    E.wait(V_thr[widx], 4)
                    thr_ap = sc_thr[widx][:, 0:1]
                    E.put(vector.tensor_scalar(out=tl[:, 0:valid],
                                               in0=tin[q % 4][:, 0:valid],
                                               scalar1=-1.0, scalar2=thr_ap,
                                               op0=ALU.mult, op1=ALU.is_gt))
                    if pw > valid:
                        E.put(vector.scalar_tensor_tensor(
                            out=tqs[:, 0:valid], in0=tin[q % 4][:, 0:valid],
                            scalar=thr_ap, in1=tl[:, 0:valid],
                            op0=ALU.is_gt, op1=ALU.subtract), g("V_q"))
                        E.put(vector.tensor_copy(
                            tqs[:, valid:pw],
                            cbc[:, C_ZERO:C_ZERO + 1].broadcast_to((128, pw - valid))),
                            g("V_q"))
                    else:
                        E.put(vector.scalar_tensor_tensor(
                            out=tqs[:, 0:valid], in0=tin[q % 4][:, 0:valid],
                            scalar=thr_ap, in1=tl[:, 0:valid],
                            op0=ALU.is_gt, op1=ALU.subtract), g("V_q"), 2)
                    if prezero:
                        E.put(vector.tensor_copy(
                            tin[q % 4][64:128, 0:768],
                            cbc[64:128, C_ZERO:C_ZERO + 1].broadcast_to((64, 768))),
                            g("V_zw2"))
                    tld_n[0] += 1
                    qn_n[0] += 1
                tern_ops.append(f)

            NQT = 2 * DK * len(W13_SUB9) + HCN * len(W2_SUB)
            qcount = [0]

            def addq2(widx, pw, valid):
                add_q(widx, pw, valid, qcount[0] in (NQT - 7, NQT - 6, NQT - 5))
                qcount[0] += 1

            for widx in (0, 1):
                tern_ops.append(fins[widx])
                tern_ops.append(thrs[widx])
                for dk in range(DK):
                    for (pw, valid) in W13_SUB9:
                        addq2(widx, pw, valid)
            tern_ops.append(fins[2])
            tern_ops.append(thrs[2])
            for hc in range(HCN):
                for (pw, valid) in W2_SUB:
                    addq2(2, pw, valid)

            n_tern = len(tern_ops)
            tern_sched = {}
            acc = 0
            for k in range(4, 16):
                cnt = (n_tern - acc + (15 - k)) // (16 - k)
                tern_sched[k] = (acc, acc + cnt)
                acc += cnt
            ti = [0]

            for k in range(NTA):
                for b in range(BPT):
                    gg = BPT * k + b
                    E.wait(g("A_xa"), 16 * (gg + 1))
                    E.put(vector.reduce_sum(stat_ssum[:, gg:gg + 1], xar[gg % 4][:],
                                            axis=AXX))
                E.wait(g("A_sq"), k + 1)
                if k > 0:
                    E.wait(g("A_rstd"), k)
                c4 = slice(4 * k, 4 * k + 4)
                vput(vector.tensor_scalar_mul(mean_t[:], stat_ssum[:, c4], 1.0 / D))
                vput(vector.tensor_scalar_mul(meann_t[:], stat_ssum[:, c4], -1.0 / D))
                vbar()
                vput(vector.tensor_mul(msq_t[:], mean_t[:], mean_t[:]))
                vbar()
                E.put(vector.scalar_tensor_tensor(out=var_t[:], in0=stat_sqs[:, c4],
                                                  scalar=1.0 / D, in1=msq_t[:],
                                                  op0=ALU.mult, op1=ALU.subtract),
                      g("A_var"))
                E.wait(g("A_rstd"), k + 1)
                E.put(vector.tensor_mul(stat_nb[:, c4], meann_t[:], stat_rstd[:, c4]),
                      g("A_nb"))
                for dk in range(DK):
                    c = DK * k + dk
                    E.wait(g("A_trps"), c + 1)
                    E.put(vector.tensor_copy(lnF[:, dk * TT:(dk + 1) * TT],
                                             ptr[dk % 2][:]), g("A_trevac"))
                if k in tern_sched:
                    lo, hi = tern_sched[k]
                    while ti[0] < min(hi, n_tern):
                        tern_ops[ti[0]]()
                        ti[0] += 1
                E.wait(g("A_bu"), k + 1)
                for nh in range(2):
                    c0 = nh * TT
                    nhs = slice(nh, nh + 1)
                    psre = pbu[2 * nh + 0]
                    psim = pbu[2 * nh + 1]
                    E.put(vector.tensor_mul(scr[0][:], csp_t[:, c0:c0 + TT], psre[:]))
                    E.put(vector.tensor_mul(scr[1][:], snp_t[:, c0:c0 + TT], psim[:]))
                    E.put(vector.tensor_mul(scr[2][:], snp_t[:, c0:c0 + TT], psre[:]))
                    E.put(vector.tensor_mul(scr[3][:], csp_t[:, c0:c0 + TT], psim[:]),
                          g("A_bufree") if nh == 1 else None)
                    E.put(vector.tensor_add(scr[0][:], scr[0][:], scr[1][:]))
                    E.put(vector.tensor_sub(scr[2][:], scr[2][:], scr[3][:]))
                    if k == 0:
                        init1 = 0.0
                        init2 = 0.0
                    else:
                        vput(vector.tensor_scalar(out=gt1[:, nhs], in0=hreL[:, nhs],
                                                  scalar1=pr_cos[:, nhs], scalar2=None,
                                                  op0=ALU.mult))
                        vput(vector.tensor_scalar(out=gt2[:, nhs], in0=himL[:, nhs],
                                                  scalar1=pr_sin[:, nhs], scalar2=None,
                                                  op0=ALU.mult))
                        vbar()
                        vput(vector.tensor_add(s1i[:, nhs], gt1[:, nhs], gt2[:, nhs]))
                        vput(vector.tensor_scalar(out=gt1[:, nhs], in0=himL[:, nhs],
                                                  scalar1=pr_cos[:, nhs], scalar2=None,
                                                  op0=ALU.mult))
                        vput(vector.tensor_scalar(out=gt2[:, nhs], in0=hreL[:, nhs],
                                                  scalar1=pr_sin[:, nhs], scalar2=None,
                                                  op0=ALU.mult))
                        vbar()
                        vput(vector.tensor_sub(s2i[:, nhs], gt1[:, nhs], gt2[:, nhs]))
                        vbar()
                        init1 = s1i[:, nhs]
                        init2 = s2i[:, nhs]
                    rbc = pr_mag[:, nhs].broadcast_to((128, TT))
                    E.put(vector.tensor_tensor_scan(out=s1t[:], data0=rbc,
                                                    data1=scr[0][:], initial=init1,
                                                    op0=ALU.mult, op1=ALU.add))
                    E.put(vector.tensor_tensor_scan(out=s2t[:], data0=rbc,
                                                    data1=scr[2][:], initial=init2,
                                                    op0=ALU.mult, op1=ALU.add))
                    t0 = (k % 8) * TT
                    dre_s = Hre[nh][:, t0:t0 + TT]
                    dim_s = Him[nh][:, t0:t0 + TT]
                    E.put(vector.tensor_mul(scr[0][:], cs_t[:, c0:c0 + TT], s1t[:]))
                    E.put(vector.tensor_mul(scr[1][:], sn_t[:, c0:c0 + TT], s2t[:]))
                    E.put(vector.tensor_add(dre_s, scr[0][:], scr[1][:]))
                    vput(vector.tensor_add(hreL[:, nhs], scr[0][:, TT - 1:TT],
                                           scr[1][:, TT - 1:TT]))
                    E.put(vector.tensor_mul(scr[2][:], cs_t[:, c0:c0 + TT], s2t[:]))
                    E.put(vector.tensor_mul(scr[3][:], sn_t[:, c0:c0 + TT], s1t[:]))
                    E.put(vector.tensor_sub(dim_s, scr[2][:], scr[3][:]),
                          g("A_h") if nh == 1 else None)
                    vput(vector.tensor_sub(himL[:, nhs], scr[2][:, TT - 1:TT],
                                           scr[3][:, TT - 1:TT]))
                    vbar()
            while ti[0] < n_tern:
                tern_ops[ti[0]]()
                ti[0] += 1

            def base_q(j, p):
                qg = 16 * j + p
                b, dq = p % 4, p // 4
                E.wait(g("B_lnb"), qg + 1)
                sl = slice(dq * TT, (dq + 1) * TT)
                E.put(vector.tensor_mul(x1[b][:, sl], lnbq[qg % 2][:], dbc[:, sl]))
                E.put(vector.tensor_add(x1[b][:, sl], x1[b][:, sl], xq[qg % 2][:]),
                      g("B_base"))

            def yevac(j, qi):
                yc = 16 * j + qi
                b, dq = qi // 4, qi % 4
                E.wait(g("B_yps"), yc + 1)
                sl = slice(dq * TT, (dq + 1) * TT)
                E.put(vector.tensor_add(x1[b][:, sl], x1[b][:, sl], py[qi % 2][:]),
                      g("B_yfree"))
                if dq == 3:
                    gg = 4 * j + b
                    E.put(vector.reduce_sum(stat2_ssum[:, gg:gg + 1], x1[b][:],
                                            axis=AXX))

            def stats2(j):
                E.wait(g("B_sq2"), j + 1)
                if j > 0:
                    E.wait(g("B_rstd2"), j)
                else:
                    E.wait(g("A_rstd"), NTA)
                c4 = slice(4 * j, 4 * j + 4)
                vput(vector.tensor_scalar_mul(mean_t[:], stat2_ssum[:, c4], 1.0 / D))
                vput(vector.tensor_scalar_mul(meann_t[:], stat2_ssum[:, c4], -1.0 / D))
                vbar()
                vput(vector.tensor_mul(msq_t[:], mean_t[:], mean_t[:]))
                vbar()
                E.put(vector.scalar_tensor_tensor(out=var_t[:], in0=stat2_sqs[:, c4],
                                                  scalar=1.0 / D, in1=msq_t[:],
                                                  op0=ALU.mult, op1=ALU.subtract),
                      g("B_var2"))
                E.wait(g("B_rstd2"), j + 1)
                E.put(vector.tensor_mul(stat2_nb[:, c4], meann_t[:],
                                        stat2_rstd[:, c4]), g("B_nb2"))

            def tr2evac(j):
                for b in range(BPT):
                    for dkg in range(4):
                        tc = 256 + 16 * j + 4 * b + dkg
                        E.wait(g("A_trps"), tc + 1)
                        outap = lnF[:].rearrange("p (dk c) -> p dk c", dk=DK)[
                            :, 4 * dkg:4 * dkg + 4, b * 128:(b + 1) * 128]
                        srcap = ptr[(4 * b + dkg) % 2][:].rearrange(
                            "p (i c) -> p i c", i=4)
                        E.put(vector.tensor_copy(outap, srcap), g("A_trevac"))

            def gop(j, hc):
                gi = HCN * j + hc
                E.wait(g("B_h3d"), gi + 1)
                E.wait(g("B_sil"), gi + 1)
                if j > 0 and hc == 0:
                    E.wait(g("B_w2u"), 86 * j)
                bank = (hc % 2) * 2 + 1
                E.put(vector.scalar_tensor_tensor(out=gbuf[:, hc * TT:(hc + 1) * TT],
                                                  in0=pbu[bank][:],
                                                  scalar=sc_s3[:, 0:1],
                                                  in1=silb[hc % 2][:], op0=ALU.mult,
                                                  op1=ALU.mult), g("B_g"))

            psmap_v = [pbu[0], pbu[1], pbu[2], pbu[3], ptr[0], ptr[1], py[0], py[1]]

            def outevac(j, pair):
                E.wait(g("B_w2u"), 86 * j + 43 * (pair + 1))
                for dq2 in range(2):
                    dq = 2 * pair + dq2
                    for b in range(BPT):
                        oi = 16 * j + 4 * dq + b
                        if oi >= 2:
                            E.wait(g("G_out"), 16 * (oi - 1))
                        sl = slice(dq * TT, (dq + 1) * TT)
                        E.put(vector.scalar_tensor_tensor(
                            out=ost[oi % 2][:], in0=psmap_v[4 * dq2 + b][:],
                            scalar=sc_s2[:, 0:1], in1=x1[b][:, sl],
                            op0=ALU.mult, op1=ALU.add), g("B_outsb"))

            for p in range(16):
                base_q(0, p)
            for qi in range(16):
                yevac(0, qi)
            stats2(0)
            tr2evac(0)
            for j in range(NTB):
                for hc in range(HCN):
                    gop(j, hc)
                for pair in range(2):
                    outevac(j, pair)
                    if j + 1 < NTB:
                        for dq2 in range(2):
                            for b in range(BPT):
                                base_q(j + 1, 4 * (2 * pair + dq2) + b)
                if j + 1 < NTB:
                    for qi in range(16):
                        yevac(j + 1, qi)
                    stats2(j + 1)
                    tr2evac(j + 1)

        @block.tensor
        def _(tensor):
            E = Eng(tensor, "tensor", log)
            E.wait(g("L_pro"), 16 * 4)
            E.wait(g("L_cast"), 16 * 4)

            for k in range(NTA):
                E.wait(g("A_ln1tok"), 4 * k + 4)
                for dk in range(DK):
                    c = DK * k + dk
                    E.wait(g("A_trevac"), c - 1)
                    for b in range(BPT):
                        E.put(tensor.matmul(
                            ptr[dk % 2][:, b * 128:(b + 1) * 128],
                            ln1tok[:, b * D + dk * 128:b * D + dk * 128 + 128],
                            ident[:], start=(b == 0), stop=(b == 3),
                            skip_group_check=True),
                            g("A_trps") if b == 3 else None)
                if k > 0:
                    E.wait(g("A_bufree"), k)
                for dk in range(DK):
                    c = DK * k + dk
                    E.wait(g("A_trevac"), c + 1)
                    for nh in range(2):
                        for cm in range(2):
                            last = (dk == DK - 1 and nh == 1 and cm == 1)
                            E.put(tensor.matmul(
                                pbu[2 * nh + cm][:],
                                bT[cm][:, dk * NST + nh * 128:dk * NST + nh * 128 + 128],
                                lnF[:, dk * TT:(dk + 1) * TT],
                                start=(dk == 0), stop=(dk == DK - 1),
                                skip_group_check=True),
                                g("A_bu") if last else None)

            def Y(j):
                for qi in range(16):
                    b, dq = qi // 4, qi % 4
                    yc = 16 * j + qi
                    if qi == 0:
                        E.wait(g("A_h"), min(9 + j, NTA))
                        if j > 0:
                            E.wait(g("B_outsb"), 16 * j)
                    E.wait(g("B_yfree"), yc - 1)
                    t0 = j * TT + b * 128
                    srcs = [(Hre[0], cT[0], 0), (Him[0], cT[1], 0),
                            (Hre[1], cT[0], 1), (Him[1], cT[1], 1)]
                    for i, (hsrc, csrc, nh) in enumerate(srcs):
                        E.put(tensor.matmul(
                            py[qi % 2][:], hsrc[:, t0:t0 + 128],
                            csrc[:, nh * D + dq * TT:nh * D + (dq + 1) * TT],
                            start=(i == 0), stop=(i == 3),
                            skip_group_check=True),
                            g("B_yps") if i == 3 else None)

            def TR2(j):
                if j > 0:
                    E.wait(g("B_outsb"), 16 * j)
                for b in range(BPT):
                    E.wait(g("B_ln2tok"), 4 * j + b + 1)
                    for dkg in range(4):
                        tc = 256 + 16 * j + 4 * b + dkg
                        E.wait(g("A_trevac"), tc - 1)
                        for i in range(4):
                            dk = 4 * dkg + i
                            E.put(tensor.matmul(
                                ptr[(4 * b + dkg) % 2][:, i * 128:(i + 1) * 128],
                                ln2tok[b % 2][:, dk * 128:(dk + 1) * 128],
                                ident[:], start=(i == 0), stop=(i == 3),
                                skip_group_check=True),
                                g("A_trps") if i == 3 else None)

            def H(j):
                if j == 0:
                    E.wait(g("A_bufree"), NTA)
                for hc in range(HCN):
                    gi = HCN * j + hc
                    if hc == 0:
                        E.wait(g("A_trevac"), 256 + 16 * j + 16)
                        E.wait(g("B_outsb"), 16 * j)
                    E.wait(g("B_p1"), 16 * (gi + 1))
                    if gi >= 2:
                        E.wait(g("B_sil"), gi - 1)
                    b1 = (hc % 2) * 2
                    for dk in range(DK):
                        E.put(tensor.matmul(
                            pbu[b1][:], w1p[hc % 2][:, dk * 128:(dk + 1) * 128],
                            lnF[:, dk * TT:(dk + 1) * TT],
                            start=(dk == 0), stop=(dk == DK - 1),
                            skip_group_check=True),
                            g("B_h1d") if dk == DK - 1 else None)
                    E.wait(g("B_p3"), 16 * (gi + 1))
                    if gi >= 2:
                        E.wait(g("B_g"), gi - 1)
                    for dk in range(DK):
                        E.put(tensor.matmul(
                            pbu[b1 + 1][:], w3p[hc % 2][:, dk * 128:(dk + 1) * 128],
                            lnF[:, dk * TT:(dk + 1) * TT],
                            start=(dk == 0), stop=(dk == DK - 1),
                            skip_group_check=True),
                            g("B_h3d") if dk == DK - 1 else None)

            psmap = [pbu[0], pbu[1], pbu[2], pbu[3], ptr[0], ptr[1], py[0], py[1]]

            def OUT(j):
                for pair in range(2):
                    for hc in range(HCN):
                        widx = (2 * j + pair) * HCN + hc
                        E.wait(g("B_p2"), 16 * (widx + 1))
                        E.wait(g("B_g"), HCN * j + hc + 1)
                        if hc == 0:
                            if pair == 0:
                                E.wait(g("B_g"), HCN * (j + 1))
                                E.wait(g("B_yfree"), 16 * (j + 1))
                                E.wait(g("A_trevac"), 256 + 16 * j + 16)
                            else:
                                E.wait(g("B_outsb"), 16 * j + 8)
                        for b in range(BPT):
                            for dq2 in range(2):
                                last = (b == BPT - 1 and dq2 == 1)
                                E.put(tensor.matmul(
                                    psmap[4 * dq2 + b][:],
                                    gbuf[:, hc * TT + b * 128:hc * TT + (b + 1) * 128],
                                    w2c[widx % 4][:, dq2 * TT:(dq2 + 1) * TT],
                                    start=(hc == 0), stop=(hc == HCN - 1),
                                    skip_group_check=True),
                                    g("B_w2u") if last else None)

            Y(0)
            TR2(0)
            for j in range(NTB):
                H(j)
                OUT(j)
                if j + 1 < NTB:
                    Y(j + 1)
                    TR2(j + 1)

        @block.gpsimd
        def _(gpsimd):
            E = Eng(gpsimd, "gpsimd", log)
            E.put(gpsimd.dma_start(out=bT[0][:].rearrange("p (dk n) -> p dk n", dk=DK),
                                   in_=breT.rearrange("(dk p) n -> p dk n", p=128)),
                  g("L_cast"), 16)
            E.put(gpsimd.dma_start(out=bT[1][:].rearrange("p (dk n) -> p dk n", dk=DK),
                                   in_=bimT.rearrange("(dk p) n -> p dk n", p=128)),
                  g("L_cast"), 16)
            E.put(gpsimd.dma_start(out=cT[0][:].rearrange("p (h d) -> p h d", h=2),
                                   in_=creT.rearrange("(h p) d -> p h d", p=128)),
                  g("L_cast"), 16)
            E.put(gpsimd.dma_start(out=cT[1][:].rearrange("p (h d) -> p h d", h=2),
                                   in_=cimT.rearrange("(h p) d -> p h d", p=128)),
                  g("L_cast"), 16)

            tld = [0]
            qn = [0]

            def roundtrip(widx):
                E.wait(V_thr[widx], 1)
                E.put(gpsimd.dma_start(
                    out=tstat[widx:widx + 1, :].rearrange("a p -> p a"),
                    in_=redtot[:]), g("G_st"), 16)
                E.wait(g("G_st"), 16 * (4 * widx + 1))
                E.put(gpsimd.dma_start(out=rrow[:], in_=tstat[widx:widx + 1, :]),
                      g("G_st"), 16)
                E.wait(V_thr[widx], 3)
                E.put(gpsimd.dma_start(out=tstat[4 + widx:5 + widx, 0:1], in_=rsum[:]),
                      g("G_st"), 16)
                E.wait(g("G_st"), 16 * (4 * widx + 3))
                st = [sc_s1, sc_s3, sc_s2][widx]
                E.put(gpsimd.dma_start(
                    out=st[:],
                    in_=tstat[4 + widx:5 + widx, 0:1].broadcast_to((128, 1))),
                    g("G_st"), 16)

            qjobs = []
            for widx, (wsrc, wdst) in enumerate(((w1T, w1qs), (w3T, w3qs))):
                qjobs.append(("rt", widx))
                for dk in range(DK):
                    h0 = 0
                    for (pw, valid) in W13_SUB9:
                        npan = pw // 128
                        hc0 = h0 // 128
                        store = wdst[hc0:hc0 + npan, :, dk * 128:(dk + 1) * 128]
                        store = store.rearrange("hc p jj -> p hc jj")
                        qjobs.append(("q", valid, pw, True,
                                      wsrc[dk * 128:(dk + 1) * 128, h0:h0 + valid],
                                      store, npan, 0))
                        h0 += pw
            qjobs.append(("rt", 2))
            for hc in range(HCN):
                rows = 128 if hc < HCN - 1 else HID - 128 * (HCN - 1)
                h0 = 0
                for pi, (pw, valid) in enumerate(W2_SUB):
                    qjobs.append(("q", valid, pw, False,
                                  w2T[hc * 128:hc * 128 + rows, h0:h0 + valid],
                                  w2qs[hc, :, h0:h0 + pw], 0,
                                  (pi + 1) if rows < 128 else 0))
                    h0 += pw

            qlist = [j for j in qjobs if j[0] == "q"]

            def q_issue(q):
                if q >= len(qlist):
                    return
                _, valid, pw, panel, src_ap, _, _, zw = qlist[q]
                if zw:
                    E.wait(g("V_zw2"), zw)
                rowsl = src_ap.shape[0]
                E.put(gpsimd.dma_start(out=tin[q % 4][0:rowsl, 0:valid], in_=src_ap),
                      g("G_tld"), 16)

            for q in range(4):
                q_issue(q)
            qd = [0]
            for job in qjobs:
                if job[0] == "rt":
                    roundtrip(job[1])
                    continue
                q = qd[0]
                _, valid, pw, panel, src_ap, store, npan, zw = job
                E.wait(g("V_q"), 2 * (q + 1))
                if panel:
                    srcv = tq[q % 2][:, 0:pw].rearrange("p (hc jj) -> p hc jj",
                                                        hc=npan)
                else:
                    srcv = tq[q % 2][:, 0:pw]
                E.put(gpsimd.dma_start(out=store, in_=srcv), g("G_twr"), 16)
                q_issue(q + 4)
                qd[0] += 1

            for j in range(NTB):
                for dq in range(4):
                    for b in range(BPT):
                        oi = 16 * j + 4 * dq + b
                        E.wait(g("B_outsb"), oi + 1)
                        r0 = j * TT + b * 128
                        E.put(gpsimd.dma_start(
                            out=out[r0:r0 + 128, dq * TT:(dq + 1) * TT],
                            in_=ost[oi % 2][:]), g("G_out"), 16)
            if DEBUG:
                E.wait(g("G_out"), 16 * 128)
                for srcp, c0, w in [(pr_mag, 0, 2), (pr_gam, 2, 2), (pr_cos, 4, 2),
                                    (pr_sin, 6, 2), (sc_s1, 8, 1), (sc_s3, 9, 1),
                                    (sc_s2, 10, 1), (sc_thr[0], 11, 1),
                                    (sc_thr[1], 12, 1), (sc_thr[2], 13, 1),
                                    (hreL, 14, 2), (himL, 16, 2), (var_t, 18, 4),
                                    (mean_t, 22, 4), (nusb, 26, 2), (thsb, 28, 2)]:
                    E.put(gpsimd.dma_start(out=dbg_small[:, c0:c0 + w], in_=srcp[:],
                                           allow_slow_non_contiguous=True),
                          g("G_out"), 16)
                E.put(gpsimd.dma_start(out=dbg_stat[:, 0:64], in_=stat_rstd[:],
                                       allow_slow_non_contiguous=True), g("G_out"), 16)
                E.put(gpsimd.dma_start(out=dbg_stat[:, 64:128], in_=stat_nb[:],
                                       allow_slow_non_contiguous=True), g("G_out"), 16)
                E.put(gpsimd.dma_start(out=dbg_stat[:, 128:192], in_=stat_ssum[:],
                                       allow_slow_non_contiguous=True), g("G_out"), 16)
                E.put(gpsimd.dma_start(out=dbg_stat[:, 192:256], in_=stat_sqs[:],
                                       allow_slow_non_contiguous=True), g("G_out"), 16)
                E.put(gpsimd.dma_start(out=dbg_h[:], in_=Hre[0][:]), g("G_out"), 16)
                E.wait(g("G_out"), 16 * (128 + 21))
            else:
                E.wait(g("G_out"), 16 * 128)

    _check_totals(log)
    _simulate(log)
    return nc


_NC_CACHE = {}


def _get_nc():
    if "nc" not in _NC_CACHE:
        _NC_CACHE["nc"] = build_nc()
    return _NC_CACHE["nc"]


def kernel(L0_attn_x, L0_attn_nu_log, L0_attn_theta_log, L0_attn_B_re, L0_attn_B_im,
           L0_attn_C_re, L0_attn_C_im, L0_attn_D, L0_ffn_w1, L0_ffn_w3, L0_ffn_w2):
    x = np.ascontiguousarray(np.asarray(L0_attn_x, dtype=np.float32))
    assert x.shape == (4, 8192, 2048)

    shared = {
        "breT": np.ascontiguousarray(np.asarray(L0_attn_B_re, np.float32).T),
        "bimT": np.ascontiguousarray(np.asarray(L0_attn_B_im, np.float32).T),
        "creT": np.ascontiguousarray(np.asarray(L0_attn_C_re, np.float32).T),
        "cimT": np.ascontiguousarray(np.asarray(L0_attn_C_im, np.float32).T),
        "nu_in": np.ascontiguousarray(
            np.asarray(L0_attn_nu_log, np.float32).reshape(1, NST)),
        "th_in": np.ascontiguousarray(
            np.asarray(L0_attn_theta_log, np.float32).reshape(1, NST)),
        "dvec": np.ascontiguousarray(np.asarray(L0_attn_D, np.float32).reshape(1, D)),
        "w1T": np.ascontiguousarray(np.asarray(L0_ffn_w1, np.float32).T),
        "w3T": np.ascontiguousarray(np.asarray(L0_ffn_w3, np.float32).T),
        "w2T": np.ascontiguousarray(np.asarray(L0_ffn_w2, np.float32).T),
        "ident_in": np.eye(128, dtype=ml_dtypes.bfloat16),
        "consts": np.array([[1.0, 0.0, math.pi / 2, 1e-5, 0, 0, 0, 0]], np.float32),
    }
    zeros = np.zeros((TO, D), np.float32)
    in_maps = []
    for core in range(8):
        b, h = core // 2, core % 2
        xa_c = np.concatenate([zeros, x[b, :TO]], axis=0) if h == 0 else x[b]
        m = dict(shared)
        m["xa"] = np.ascontiguousarray(xa_c)
        in_maps.append(m)

    nc = _get_nc()
    res = run_bass_kernel_spmd(nc, in_maps, core_ids=list(range(8)))
    outp = np.empty((4, 8192, 2048), np.float32)
    for core in range(8):
        b, h = core // 2, core % 2
        outp[b, h * TO:(h + 1) * TO] = res.results[core]["out"]
    return outp


if __name__ == "__main__":
    nc = build_nc()
    print("build OK")
